# version 18
# baseline (speedup 1.0000x reference)
from contextlib import ExitStack

import numpy as np

import concourse.bass as bass
import concourse.mybir as mybir
from concourse.bass_utils import run_bass_kernel_spmd

F32 = mybir.dt.float32
BF16 = mybir.dt.bfloat16
FP16 = mybir.dt.float16
I32 = mybir.dt.int32
AF = mybir.ActivationFunctionType
ALU = mybir.AluOpType
AX = mybir.AxisListType.X

P = 128
TPC = 1024
NT = TPC // P
CH = 32
NJ = 8
NFT = 80
QSCALE = float(np.float32(1.0) / np.float32(np.sqrt(np.float32(128.0))))
EPS = 1e-5
DEBUG_TAPS = False


class S:
    def __init__(self, ctx, nc, name):
        self.h = ctx.enter_context(nc.semaphore(name))
        self.n = 0


def build(taps=DEBUG_TAPS, phases=5):
    nc = bass.Bass("TRN2", target_bir_lowering=False, debug=False)

    x_in = nc.dram_tensor("x_in", [TPC, 4096], F32, kind="ExternalInput").ap()
    wall = nc.dram_tensor("wall", [NFT, 4096, P], F32, kind="ExternalInput").ap()
    bq_d = nc.dram_tensor("bq", [P, 32], F32, kind="ExternalInput").ap()
    bk_d = nc.dram_tensor("bk", [P, 8], F32, kind="ExternalInput").ap()
    bv_d = nc.dram_tensor("bv", [P, 8], F32, kind="ExternalInput").ap()
    fb_d = nc.dram_tensor("fb", [P, 32], F32, kind="ExternalInput").ap()
    ah_d = nc.dram_tensor("ah", [CH, P, 3, 16], F32, kind="ExternalInput").ap()
    bh_d = nc.dram_tensor("bh", [3, 16, 32 * P], F32, kind="ExternalInput").ap()
    lsi_d = nc.dram_tensor("lsi", [1, 4], I32, kind="ExternalInput").ap()
    bsel_d = nc.dram_tensor("bsel", [1, 1], F32, kind="ExternalInput").ap()
    id_d = nc.dram_tensor("ident", [P, P], F32, kind="ExternalInput").ap()
    out_d = nc.dram_tensor("out", [TPC, 4096], F32, kind="ExternalOutput").ap()
    scr_d = nc.dram_tensor("qkscr", [TPC * 16 * P], BF16, kind="Internal").ap()
    tap_d = {}
    if taps:
        tap_d["xnT"] = nc.dram_tensor("tap_xnT", [P, CH * TPC], BF16, kind="ExternalOutput").ap()
        tap_d["qsks"] = nc.dram_tensor("tap_qsks", [P, TPC * 16], BF16, kind="ExternalOutput").ap()
        tap_d["vall"] = nc.dram_tensor("tap_vall", [P, TPC * 9], BF16, kind="ExternalOutput").ap()
        tap_d["xaT"] = nc.dram_tensor("tap_xaT", [16, TPC], BF16, kind="ExternalOutput").ap()
        tap_d["ffn"] = nc.dram_tensor("tap_ffn", [P, TPC * 32], FP16, kind="ExternalOutput").ap()
        tap_d["hbc"] = nc.dram_tensor("tap_hbc", [P, 3], F32, kind="ExternalOutput").ap()
        tap_d["bselt"] = nc.dram_tensor("tap_bselt", [16, 4096], BF16, kind="ExternalOutput").ap()
        tap_d["lnst"] = nc.dram_tensor("tap_lnst", [P, 80], F32, kind="ExternalOutput").ap()

    with ExitStack() as ctx:
        def sbl(n_, s_, d_):
            return ctx.enter_context(nc.sbuf_tensor(n_, s_, d_))

        def sbr(n_, s_, d_):
            return ctx.enter_context(nc.sbuf_tensor(n_, s_, d_, side="right"))

        id_sb = sbl("id_sb", [P, P], BF16)
        bqss = sbl("bqss", [P, NJ], F32)
        bk_sb = sbl("bk_sb", [P, 8], F32)
        bv_sb = sbl("bv_sb", [P, 8], F32)
        fb_sb = sbl("fb_sb", [P, 32], F32)
        bq_sb = sbl("bq_sb", [P, 32], F32)
        hbc = sbl("hbc", [P, 3], F32)
        ones_r = sbl("ones_r", [1, P], BF16)
        h3b = sbl("h3b", [1, 3], BF16)
        lsi_sb = sbl("lsi_sb", [1, 4], I32)
        lsif = sbl("lsif", [1, 4], F32)
        cle = sbl("cle", [1, 4], F32)
        bsel_sb = sbl("bsel_sb", [1, 1], F32)
        tmp1 = sbl("tmp1", [1, 4], F32)
        epsc = sbl("epsc", [P, 1], F32)
        ps_h = ctx.enter_context(nc.psum_tensor("ps_h", [P, 3], F32))
        xasm_e = sbr("xasm_e", [P, P * 32], F32)
        xaT_guard = nc.sbuf_tensor("xaT", [16, TPC], BF16)
        xaT = xaT_guard.__enter__()
        xnT_guard = nc.sbuf_tensor("xnT", [P, CH * TPC], BF16)
        xnT = xnT_guard.__enter__()
        wsl0_guard = nc.sbuf_tensor("wsl0", [P, CH * P], BF16)
        wsl1_guard = nc.sbuf_tensor("wsl1", [P, CH * P], BF16)
        wsl = [wsl0_guard.__enter__(), wsl1_guard.__enter__()]

        GQ = S(ctx, nc, "GQ")
        BS = S(ctx, nc, "BS")
        WQ = [S(ctx, nc, "WQ0"), S(ctx, nc, "WQ1")]
        BO = [S(ctx, nc, "BO0"), S(ctx, nc, "BO1")]
        XD = [S(ctx, nc, f"XD{i}") for i in range(3)]
        QK = [S(ctx, nc, "QK0"), S(ctx, nc, "QK1")]
        OD = [S(ctx, nc, f"OD{i}") for i in range(3)]
        OG = [S(ctx, nc, f"OG{i}") for i in range(3)]
        XS = [S(ctx, nc, f"XS{i}") for i in range(3)]
        SQ = S(ctx, nc, "SQ")
        TEN = S(ctx, nc, "TEN")
        SCA = S(ctx, nc, "SCA")
        VEC = S(ctx, nc, "VEC")
        PL = S(ctx, nc, "PL")

        block = ctx.enter_context(nc.Block())

        p1ctx = ExitStack()
        xt = [p1ctx.enter_context(nc.sbuf_tensor(f"xt{i}", [P, 4096], F32)) for i in range(3)]
        sqd = [p1ctx.enter_context(nc.sbuf_tensor(f"sqd{i}", [P, 4096], BF16))
               for i in range(2)]
        xnt_ = [p1ctx.enter_context(nc.sbuf_tensor(f"xn{i}", [P, 4096], BF16)) for i in range(2)]
        lnst = p1ctx.enter_context(nc.sbuf_tensor("lnst", [P, 80], F32))
        tr_ps = [p1ctx.enter_context(nc.psum_tensor(f"tr_ps{i}", [P, 512], BF16)) for i in range(4)]

        @block.gpsimd
        def _(g):
            g.dma_start(out=id_sb[:], in_=id_d).then_inc(GQ.h, 16)
            g.dma_start(out=bq_sb[:], in_=bq_d).then_inc(GQ.h, 16)
            g.dma_start(out=bk_sb[:], in_=bk_d).then_inc(GQ.h, 16)
            g.dma_start(out=bv_sb[:], in_=bv_d).then_inc(GQ.h, 16)
            g.dma_start(out=fb_sb[:], in_=fb_d).then_inc(GQ.h, 16)
            g.dma_start(out=lsi_sb[:], in_=lsi_d).then_inc(GQ.h, 16)
            g.dma_start(out=bsel_sb[:], in_=bsel_d).then_inc(GQ.h, 16)
            GQ.n += 7 * 16
            for ft in range(2):
                g.dma_start(out=wsl[ft][:].rearrange("p (c f) -> p c f", f=P),
                            in_=wall[ft].rearrange("(c p) f -> p c f", p=P))\
                    .then_inc(WQ[ft].h, 16)
                WQ[ft].n += 16
        p0_gq = GQ.n

        @block.vector
        def _(v):
            v.memset(epsc[:], EPS).then_inc(VEC.h, 1); VEC.n += 1
            v.memset(lnst[:], 0.0).then_inc(VEC.h, 1); VEC.n += 1
        assert VEC.n == 2, VEC.n

        p1_norm_mark = [None] * NT
        p1_trev_mark = [None] * NT
        p1_tr_mark = [[None] * 8 for _ in range(NT)]
        p1_var_mark = [None] * NT
        p1_nmr_mark = [None] * NT

        if True:
            def STT(tt, i):
                return lnst[:, tt * 10 + i: tt * 10 + i + 1]

            p1_sq_mark = [None] * NT
            p1_ln_mark = [None] * NT
            p1_exp_mark = [None] * NT
            p1_hn_mark = [None] * NT
            p1_evac_mark = [[None] * 8 for _ in range(NT)]
            vec, ten = VEC.n, TEN.n
            for tt in range(NT):
                vec += 3
                vec += 2
                p1_var_mark[tt] = vec
                vec += 1
                p1_hn_mark[tt] = vec
                vec += 1
                p1_norm_mark[tt] = vec
                for b4 in range(8):
                    ten += 1
                    p1_tr_mark[tt][b4] = ten
            sca = SCA.n
            for tt in range(NT):
                sca += 1
                p1_sq_mark[tt] = sca
                if tt >= 1:
                    for b4 in range(8):
                        sca += 1
                        p1_evac_mark[tt - 1][b4] = sca
                sca += 1
                p1_ln_mark[tt] = sca
                sca += 1
                p1_exp_mark[tt] = sca
            for b4 in range(8):
                sca += 1
                p1_evac_mark[NT - 1][b4] = sca
            for tt in range(NT):
                p1_trev_mark[tt] = p1_evac_mark[tt][7]

            def xd_mark(tt):
                return 16 * (tt // 3 + 1) + (16 if tt % 3 == 0 else 0)

            @block.sync
            def _(sy):
                for tt in range(NT):
                    if tt >= 3:
                        sy.wait_ge(VEC.h, p1_norm_mark[tt - 3])
                        sy.wait_ge(SCA.h, p1_sq_mark[tt - 3])
                    if tt == 0:
                        sy.dma_start(out=xt[0][:, 0:2048],
                                     in_=x_in[0:P, 0:2048]).then_inc(XD[0].h, 16)
                        XD[0].n += 16
                    else:
                        sy.dma_start(out=xt[tt % 3][:],
                                     in_=x_in[tt * P:(tt + 1) * P, :]).then_inc(XD[tt % 3].h, 16)
                        XD[tt % 3].n += 16

            @block.vector
            def _(v):
                v.wait_ge(VEC.h, VEC.n)
                for tt in range(NT):
                    v.wait_ge(XD[tt % 3].h, xd_mark(tt))
                    v.reduce_sum(STT(tt, 0), xt[tt % 3][:], axis=AX).then_inc(VEC.h, 1)
                    VEC.n += 1
                    v.wait_ge(VEC.h, VEC.n)
                    v.tensor_scalar_mul(STT(tt, 1), STT(tt, 0), -1.0 / 4096.0).then_inc(VEC.h, 1)
                    VEC.n += 1
                    v.wait_ge(VEC.h, VEC.n)
                    v.tensor_tensor(STT(tt, 2), STT(tt, 1), STT(tt, 1), ALU.mult).then_inc(VEC.h, 1)
                    VEC.n += 1
                    v.wait_ge(SCA.h, p1_sq_mark[tt])
                    v.tensor_scalar_mul(STT(tt, 3), STT(tt, 7), 1.0 / 4096.0).then_inc(VEC.h, 1)
                    VEC.n += 1
                    v.wait_ge(VEC.h, VEC.n)
                    v.tensor_tensor(STT(tt, 3), STT(tt, 3), STT(tt, 2), ALU.subtract).then_inc(VEC.h, 1)
                    VEC.n += 1
                    assert VEC.n == p1_var_mark[tt]
                    v.wait_ge(SCA.h, p1_ln_mark[tt])
                    v.tensor_scalar_mul(STT(tt, 8), STT(tt, 4), -0.5).then_inc(VEC.h, 1)
                    VEC.n += 1
                    assert VEC.n == p1_hn_mark[tt]
                    v.wait_ge(SCA.h, p1_exp_mark[tt])
                    if tt >= 2:
                        v.wait_ge(TEN.h, p1_tr_mark[tt - 2][7])
                    v.tensor_scalar(xnt_[tt % 2][:], xt[tt % 3][:], STT(tt, 1), STT(tt, 5),
                                    ALU.add, ALU.mult).then_inc(VEC.h, 1)
                    VEC.n += 1
                    assert VEC.n == p1_norm_mark[tt]

            def p1_emit_evacs(s, te):
                pair = 2 * (te % 2)
                for b4 in range(8):
                    s.wait_ge(TEN.h, p1_tr_mark[te][b4])
                    dst = xnT[:].rearrange("p (c t) -> p c t", t=TPC)[
                        :, b4 * 4:(b4 + 1) * 4, te * P:(te + 1) * P]
                    s.activation(dst, tr_ps[pair + b4 % 2][:].rearrange("p (c t) -> p c t", t=P),
                                 AF.Copy).then_inc(SCA.h, 1)
                    SCA.n += 1
                    assert SCA.n == p1_evac_mark[te][b4]

            @block.scalar
            def _(s):
                s.dma_start(out=xt[0][:, 2048:4096],
                            in_=x_in[0:P, 2048:4096]).then_inc(XD[0].h, 16)
                XD[0].n += 16
                s.wait_ge(VEC.h, 2)
                for tt in range(NT):
                    s.wait_ge(XD[tt % 3].h, xd_mark(tt))
                    s.activation(sqd[tt % 2][:], xt[tt % 3][:], AF.Square,
                                 accum_out=STT(tt, 7)).then_inc(SCA.h, 1)
                    SCA.n += 1
                    assert SCA.n == p1_sq_mark[tt]
                    if tt >= 1:
                        p1_emit_evacs(s, tt - 1)
                    s.wait_ge(VEC.h, p1_var_mark[tt])
                    s.activation(STT(tt, 4), STT(tt, 3), AF.Ln, bias=epsc[:, 0:1]).then_inc(SCA.h, 1)
                    SCA.n += 1
                    assert SCA.n == p1_ln_mark[tt]
                    s.wait_ge(VEC.h, p1_hn_mark[tt])
                    s.activation(STT(tt, 5), STT(tt, 8), AF.Exp).then_inc(SCA.h, 1)
                    SCA.n += 1
                    assert SCA.n == p1_exp_mark[tt]
                p1_emit_evacs(s, NT - 1)

            @block.tensor
            def _(t):
                t.wait_ge(GQ.h, p0_gq)
                for tt in range(NT):
                    pair = 2 * (tt % 2)
                    t.wait_ge(VEC.h, p1_norm_mark[tt])
                    if tt >= 2:
                        t.wait_ge(SCA.h, p1_evac_mark[tt - 2][7])
                    for b4 in range(8):
                        if b4 >= 2:
                            t.wait_ge(SCA.h, p1_evac_mark[tt][b4 - 2])
                        for c in range(4):
                            cc = b4 * 4 + c
                            tr = t.transpose(tr_ps[pair + b4 % 2][:, c * P:(c + 1) * P],
                                             xnt_[tt % 2][:, cc * P:(cc + 1) * P], id_sb[:])
                        tr.then_inc(TEN.h, 1)
                        TEN.n += 1
                        assert TEN.n == p1_tr_mark[tt][b4]
            if taps:
                @block.sync
                def _(sy):
                    sy.wait_ge(SCA.h, p1_trev_mark[NT - 1])
                    sy.dma_start(out=tap_d["lnst"], in_=lnst[:]).then_inc(SQ.h, 16)
                    SQ.n += 16
            tap_lnst_sq = SQ.n
            p1_ten_end, p1_sca_end = TEN.n, SCA.n
        p1ctx.close()
        vall = sbr("vall", [P, TPC * 9], BF16)

        aselt_guard = nc.sbuf_tensor("aselt", [P, CH * 16], BF16)
        aselt = aselt_guard.__enter__()
        bselt_guard = nc.sbuf_tensor("bselt", [16, 32 * P], BF16)
        bselt = bselt_guard.__enter__()
        p0bctx = ExitStack()
        a_all = p0bctx.enter_context(nc.sbuf_tensor("a_all", [P, CH * 48], BF16, side="right"))
        bsl = [p0bctx.enter_context(nc.sbuf_tensor(f"bsl{i}", [16, 32 * P], BF16, side="right"))
               for i in range(3)]
        p0b_vec0 = 2 + NT * 7
        p0b_mask_mark = p0b_vec0 + 12
        p0_vec = p0b_mask_mark + 6
        p0_b_vec_mark = [p0b_mask_mark + 3 + (a + 1) for a in range(3)]
        p0b_gq = p0_gq + 16

        @block.vector
        def _(v):
            assert VEC.n == p0b_vec0, VEC.n
            v.wait_ge(GQ.h, p0_gq)
            v.tensor_copy(lsif[:], lsi_sb[:]).then_inc(VEC.h, 1); VEC.n += 1
            v.wait_ge(VEC.h, VEC.n)
            v.tensor_scalar(cle[:], lsif[:], bsel_sb[0:1, 0:1], None, ALU.is_le)\
                .then_inc(VEC.h, 1); VEC.n += 1
            v.wait_ge(VEC.h, VEC.n)
            v.tensor_tensor(h3b[0:1, 2:3], cle[0:1, 2:3], cle[0:1, 3:4], ALU.mult)\
                .then_inc(VEC.h, 1); VEC.n += 1
            v.wait_ge(VEC.h, VEC.n)
            v.tensor_tensor(h3b[0:1, 1:2], cle[0:1, 2:3], h3b[0:1, 2:3], ALU.subtract)\
                .then_inc(VEC.h, 1); VEC.n += 1
            v.tensor_tensor(tmp1[0:1, 0:1], cle[0:1, 1:2], cle[0:1, 2:3], ALU.mult)\
                .then_inc(VEC.h, 1); VEC.n += 1
            v.wait_ge(VEC.h, VEC.n)
            v.tensor_tensor(h3b[0:1, 0:1], cle[0:1, 1:2], tmp1[0:1, 0:1], ALU.subtract)\
                .then_inc(VEC.h, 1); VEC.n += 1
            v.memset(ones_r[:], 1.0).then_inc(VEC.h, 1); VEC.n += 1
            bqv = bq_sb[:].rearrange("p (j r) -> p j r", r=4)
            v.tensor_tensor(bqss[:], bqv[:, :, 0], bqv[:, :, 1], ALU.add)\
                .then_inc(VEC.h, 1); VEC.n += 1
            v.wait_ge(VEC.h, VEC.n)
            v.tensor_tensor(bqss[:], bqss[:], bqv[:, :, 2], ALU.add)\
                .then_inc(VEC.h, 1); VEC.n += 1
            v.wait_ge(VEC.h, VEC.n)
            v.tensor_tensor(bqss[:], bqss[:], bqv[:, :, 3], ALU.add)\
                .then_inc(VEC.h, 1); VEC.n += 1
            v.wait_ge(VEC.h, VEC.n)
            v.tensor_scalar_mul(bqss[:], bqss[:], QSCALE).then_inc(VEC.h, 1); VEC.n += 1
            v.memset(vall[:], 1.0).then_inc(VEC.h, 1); VEC.n += 1
            assert VEC.n == p0b_mask_mark, VEC.n

        p2_ten0, p2_sca0 = TEN.n, SCA.n
        NQKV = 48

        def wq_mark(ft):
            return 16 * (ft // 2 + 1)

        if phases < 2:
            @block.sync
            def _(sy):
                if taps:
                    sy.wait_ge(SCA.h, p1_trev_mark[NT - 1])
                    sy.dma_start(out=tap_d["xnT"], in_=xnT[:]).then_inc(SQ.h, 16)
                    SQ.n += 16
                sy.wait_ge(SQ.h, SQ.n)
                for sem in (GQ, BS, XD[0], XD[1]):
                    sy.wait_ge(sem.h, sem.n)
            p0bctx.close()
            bselt_guard.__exit__(None, None, None)
            aselt_guard.__exit__(None, None, None)
            wsl1_guard.__exit__(None, None, None)
            wsl0_guard.__exit__(None, None, None)
            xnT_guard.__exit__(None, None, None)
            xaT_guard.__exit__(None, None, None)
            return nc
        def p2tm(ft):
            return p2_ten0 + ft + 1 + (1 if ft >= 3 else 0)

        @block.gpsimd
        def _(g):
            for ft in range(2, NQKV):
                if ft >= 2:
                    g.wait_ge(TEN.h, p2tm(ft - 2))
                g.dma_start(out=wsl[ft % 2][:].rearrange("p (c f) -> p c f", f=P),
                            in_=wall[ft].rearrange("(c p) f -> p c f", p=P))\
                    .then_inc(WQ[ft % 2].h, 16)
                WQ[ft % 2].n += 16
                if ft == 2:
                    g.wait_ge(SCA.h, p1_trev_mark[NT - 1])
                    if taps:
                        g.wait_ge(SQ.h, tap_lnst_sq)
                    g.dma_start(out=a_all[:].rearrange("p (c a r) -> p c a r", c=CH, a=3),
                                in_=ah_d.rearrange("c p a r -> p c a r")).then_inc(GQ.h, 16)
                    GQ.n += 16
                    for a in range(3):
                        g.dma_start(out=bsl[a][:], in_=bh_d[a]).then_inc(BS.h, 16)
                    BS.n += 3 * 16

        with nc.sbuf_tensor("qsks", [P, TPC * 16], BF16, side="right") as qsks:
            with nc.psum_tensor("qk_ps0", [P, 512], F32) as qk_ps0, \
                 nc.psum_tensor("qk_ps1", [P, 512], F32) as qk_ps1, \
                 nc.psum_tensor("qk_ps2", [P, 512], F32) as qk_ps2, \
                 nc.psum_tensor("qk_ps3", [P, 512], F32) as qk_ps3, \
                 nc.psum_tensor("xa_ps0", [16, 512], F32) as xa_ps0, \
                 nc.psum_tensor("xa_ps1", [16, 512], F32) as xa_ps1:
                qk_ps = [qk_ps0, qk_ps1, qk_ps2, qk_ps3]
                xa_ps = [xa_ps0, xa_ps1]

                @block.tensor
                def _(t):
                    t.wait_ge(SCA.h, p1_trev_mark[NT - 1])
                    for ft in range(NQKV):
                        t.wait_ge(WQ[ft % 2].h, wq_mark(ft))
                        if ft < 32:
                            j, r = divmod(ft, 4)
                            first, last = (r == 0), (r == 3)
                            bank = 2 * (j % 2)
                            if first and j >= 2:
                                t.wait_ge(SCA.h, p2_sca0 + 1 + 2 * (j - 1))
                        else:
                            first = last = True
                            bank = 2 * (ft % 2)
                            if ft == 32:
                                t.wait_ge(SCA.h, p2_sca0 + 15)
                            elif ft == 33:
                                t.wait_ge(SCA.h, p2_sca0 + 17)
                            else:
                                t.wait_ge(SCA.h, p2_sca0 + 17 + 2 * (ft - 33))
                        for c in range(CH):
                            for half in range(2):
                                mm = t.matmul(
                                    qk_ps[bank + half][:],
                                    wsl[ft % 2][:, c * P:(c + 1) * P],
                                    xnT[:, c * TPC + half * 512: c * TPC + (half + 1) * 512],
                                    start=(first and c == 0), stop=(last and c == CH - 1))
                        mm.then_inc(TEN.h, 1)
                        TEN.n += 1
                        if ft == 2:
                            t.wait_ge(VEC.h, p0b_mask_mark)
                            t.matmul(ps_h[:], ones_r[:], h3b[:], start=True, stop=True)\
                                .then_inc(TEN.h, 1)
                            TEN.n += 1
                    t.wait_ge(VEC.h, p0_vec)
                    for c in range(CH):
                        for half in range(2):
                            mm = t.matmul(xa_ps[half][:],
                                          aselt[:, c * 16:(c + 1) * 16],
                                          xnT[:, c * TPC + half * 512: c * TPC + (half + 1) * 512],
                                          start=(c == 0), stop=(c == CH - 1))
                    mm.then_inc(TEN.h, 1)
                    TEN.n += 1
                p2_ten_end = TEN.n
                p0_ten = p2_ten0 + 4
                assert p2_ten_end == p2_ten0 + NQKV + 2

                @block.scalar
                def _(s):
                    qv = qsks[:].rearrange("p (t cc) -> p t cc", cc=16)
                    vv = vall[:].rearrange("p (t cc) -> p t cc", cc=9)
                    s.wait_ge(TEN.h, p0_ten)
                    s.activation(hbc[:], ps_h[:], AF.Copy).then_inc(SCA.h, 1)
                    SCA.n += 1
                    for j in range(NJ):
                        s.wait_ge(TEN.h, p2tm(4 * j + 3))
                        if j == 0:
                            s.wait_ge(VEC.h, p0b_mask_mark)
                        bank = 2 * (j % 2)
                        for half in range(2):
                            dst = qv[:, half * 512:(half + 1) * 512, j]
                            s.activation(dst, qk_ps[bank + half][:], AF.Identity,
                                         bias=bqss[:, j:j + 1], scale=QSCALE)\
                                .then_inc(SCA.h, 1)
                            SCA.n += 1
                    for ft in range(32, NQKV):
                        s.wait_ge(TEN.h, p2tm(ft))
                        kv = ft - 32
                        j = kv % 8
                        bank = 2 * (ft % 2)
                        for half in range(2):
                            if kv < 8:
                                dst = qv[:, half * 512:(half + 1) * 512, 8 + j]
                                bias = bk_sb[:, j:j + 1]
                            else:
                                dst = vv[:, half * 512:(half + 1) * 512, j]
                                bias = bv_sb[:, j:j + 1]
                            s.activation(dst, qk_ps[bank + half][:], AF.Identity,
                                         bias=bias).then_inc(SCA.h, 1)
                            SCA.n += 1
                    s.wait_ge(TEN.h, p2_ten_end)
                    for half in range(2):
                        s.activation(xaT[:, half * 512:(half + 1) * 512],
                                     xa_ps[half][:], AF.Copy).then_inc(SCA.h, 1)
                        SCA.n += 1
                p2_sca_end = SCA.n
                p0_sca = p2_sca0 + 1
                assert p2_sca_end == p2_sca0 + 51

                @block.vector
                def _(v):
                    v.wait_ge(SCA.h, p0_sca)
                    v.wait_ge(GQ.h, p0b_gq)
                    av = a_all[:].rearrange("p (c a r) -> p c a r", c=CH, a=3)
                    sv = aselt[:].rearrange("p (c r) -> p c r", c=CH)
                    v.tensor_scalar(sv, av[:, :, 0, :], hbc[:, 0:1], None, ALU.mult)\
                        .then_inc(VEC.h, 1); VEC.n += 1
                    for a in (1, 2):
                        v.wait_ge(VEC.h, VEC.n)
                        v.scalar_tensor_tensor(sv, av[:, :, a, :], hbc[:, a:a + 1], sv,
                                               ALU.mult, ALU.add).then_inc(VEC.h, 1); VEC.n += 1
                    for a in range(3):
                        v.wait_ge(BS.h, 48)
                        if a == 0:
                            v.tensor_scalar(bselt[:], bsl[a][:], hbc[0:16, 0:1], None, ALU.mult)\
                                .then_inc(VEC.h, 1); VEC.n += 1
                        else:
                            v.wait_ge(VEC.h, VEC.n)
                            v.scalar_tensor_tensor(bselt[:], bsl[a][:], hbc[0:16, a:a + 1],
                                                   bselt[:], ALU.mult, ALU.add)\
                                .then_inc(VEC.h, 1); VEC.n += 1
                        assert p0_b_vec_mark[a] == VEC.n, (a, p0_b_vec_mark[a], VEC.n)
                assert VEC.n == p0_vec, (VEC.n, p0_vec)

            p3_ten0, p3_sca0 = TEN.n, SCA.n
            NB = 32
            with nc.sbuf_tensor("tsb0", [P, 1024], BF16) as tsb0, \
                 nc.sbuf_tensor("tsb1", [P, 1024], BF16) as tsb1, \
                 nc.psum_tensor("p3_ps0", [P, 512], BF16) as p3_ps0, \
                 nc.psum_tensor("p3_ps1", [P, 512], BF16) as p3_ps1:
                tsb = [tsb0, tsb1]
                p3_ps = [p3_ps0, p3_ps1]

                @block.tensor
                def _(t):
                    t.wait_ge(SCA.h, p2_sca_end)
                    for b in range(NB):
                        if b >= 2:
                            t.wait_ge(SCA.h, p3_sca0 + b - 1)
                        for g4 in range(4):
                            col = (b * 4 + g4) * P
                            tr = t.transpose(p3_ps[b % 2][:, g4 * P:(g4 + 1) * P],
                                             qsks[:, col:col + P], id_sb[:])
                        tr.then_inc(TEN.h, 1)
                        TEN.n += 1
                p3_ten_end = TEN.n

                @block.scalar
                def _(s):
                    if taps:
                        s.wait_ge(SQ.h, tap_lnst_sq)
                    for b in range(NB):
                        s.wait_ge(TEN.h, p3_ten0 + b + 1)
                        if b >= 4:
                            bbp = b // 2 - 2
                            s.wait_ge(BO[bbp % 2].h, 16 * (bbp // 2 + 1))
                        s.activation(tsb[(b // 2) % 2][:, (b % 2) * 512:(b % 2 + 1) * 512],
                                     p3_ps[b % 2][:], AF.Copy).then_inc(SCA.h, 1)
                        SCA.n += 1
                p3_sca_end = SCA.n

                @block.sync
                def _(sy):
                    scr4 = scr_d.rearrange("(bb gg p d) -> bb p gg d", bb=16, gg=8, p=P)
                    for bb in range(16):
                        sy.wait_ge(SCA.h, p3_sca0 + 2 * bb + 2)
                        sy.dma_start(out=scr4[bb],
                                    in_=tsb[bb % 2][:].rearrange("p (gg d) -> p gg d", gg=8))\
                            .then_inc(BO[bb % 2].h, 16)
                        BO[bb % 2].n += 16

                if taps:
                    @block.sync
                    def _(sy):
                        sy.wait_ge(TEN.h, p3_ten_end)
                        sy.dma_start(out=tap_d["qsks"], in_=qsks[:]).then_inc(SQ.h, 16)
                        SQ.n += 16
                tap_qsks_sq = SQ.n

        p0bctx.close()

        ffno = ctx.enter_context(nc.sbuf_tensor("ffno", [P, TPC * 32], FP16, side="right"))
        p4_ten0, p4_sca0 = TEN.n, SCA.n
        with nc.psum_tensor("f_ps0", [P, 512], F32) as f_ps0, \
             nc.psum_tensor("f_ps1", [P, 512], F32) as f_ps1, \
             nc.psum_tensor("f_ps2", [P, 512], F32) as f_ps2, \
             nc.psum_tensor("f_ps3", [P, 512], F32) as f_ps3:
            f_ps = [f_ps0, f_ps1, f_ps2, f_ps3]

            @block.gpsimd
            def _(g):
                for hh in range(32):
                    ft = NQKV + hh
                    if hh == 0:
                        g.wait_ge(TEN.h, p2tm(46))
                    elif hh == 1:
                        g.wait_ge(TEN.h, p2tm(47))
                    else:
                        g.wait_ge(TEN.h, p4_ten0 + hh - 1)
                    g.dma_start(out=wsl[ft % 2][:].rearrange("p (c f) -> p c f", f=P),
                                in_=wall[ft].rearrange("(c p) f -> p c f", p=P))\
                        .then_inc(WQ[ft % 2].h, 16)
                    WQ[ft % 2].n += 16

            @block.tensor
            def _(t):
                t.wait_ge(SCA.h, p3_sca_end)
                t.wait_ge(VEC.h, p0_vec)
                for hh in range(32):
                    ft = NQKV + hh
                    t.wait_ge(WQ[ft % 2].h, wq_mark(ft))
                    bank = 2 * (hh % 2)
                    if hh >= 2:
                        t.wait_ge(SCA.h, p4_sca0 + 2 * (hh - 1))
                    for c in range(CH):
                        for half in range(2):
                            t.matmul(f_ps[bank + half][:],
                                     wsl[ft % 2][:, c * P:(c + 1) * P],
                                     xnT[:, c * TPC + half * 512: c * TPC + (half + 1) * 512],
                                     start=(c == 0), stop=False)
                    for half in range(2):
                        mm = t.matmul(f_ps[bank + half][:],
                                      bselt[:, hh * P:(hh + 1) * P],
                                      xaT[:, half * 512:(half + 1) * 512],
                                      start=False, stop=True)
                    mm.then_inc(TEN.h, 1)
                    TEN.n += 1
            p4_ten_end = TEN.n

            if taps:
                @block.sync
                def _(sy):
                    sy.wait_ge(TEN.h, p4_ten_end)
                    sy.dma_start(out=tap_d["xnT"], in_=xnT[:]).then_inc(SQ.h, 16)
                    sy.dma_start(out=tap_d["bselt"], in_=bselt[:]).then_inc(SQ.h, 16)
                    sy.dma_start(out=tap_d["xaT"], in_=xaT[:]).then_inc(SQ.h, 16)
                    SQ.n += 3 * 16
            tap_xnT_sq = SQ.n

            @block.scalar
            def _(s):
                if taps:
                    s.wait_ge(SQ.h, tap_qsks_sq)
                fv = ffno[:].rearrange("p (t h) -> p t h", h=32)
                for hh in range(32):
                    s.wait_ge(TEN.h, p4_ten0 + hh + 1)
                    bank = 2 * (hh % 2)
                    for half in range(2):
                        s.activation(fv[:, half * 512:(half + 1) * 512, hh],
                                     f_ps[bank + half][:], AF.Identity,
                                     bias=fb_sb[:, hh:hh + 1]).then_inc(SCA.h, 1)
                        SCA.n += 1
            p4_sca_end = SCA.n

        bselt_guard.__exit__(None, None, None)
        aselt_guard.__exit__(None, None, None)
        wsl1_guard.__exit__(None, None, None)
        wsl0_guard.__exit__(None, None, None)
        xnT_guard.__exit__(None, None, None)
        xaT_guard.__exit__(None, None, None)

        p5_ten0, p5_sca0, p5_vec0 = TEN.n, SCA.n, VEC.n
        ten_sgrp = lambda ck, g: p5_ten0 + ck * 10 + g + 1
        ten_schunk = lambda ck: p5_ten0 + ck * 10 + 8
        ten_osub = lambda ck, sidx: p5_ten0 + ck * 10 + 8 + sidx + 1
        sca_exp = lambda ck, g: p5_sca0 + ck * 8 + g + 1

        vmark_mult = {}
        vmark_q = {}
        vmark_tile = {}
        nv = p5_vec0
        for tt in range(NT):
            for sub in range(8):
                nv += 2
                vmark_mult[tt * 8 + sub] = nv
                if sub % 2 == 1:
                    nv += 2
                    vmark_q[(tt, sub // 2)] = nv
            vmark_tile[tt] = nv
        p5_vec_end = nv

        def qk_mark(cki):
            return 32 * (cki // 2 + 1)

        def xs_mark(tt):
            return 32 * (tt // 3 + 1)

        def od_mark(tt):
            return 32 * (tt // 3 + 1)

        with ExitStack() as p5ctx:
            def p5sb(n_, s_):
                return p5ctx.enter_context(nc.sbuf_tensor(n_, s_, BF16, side="right"))
            qst = [p5sb(f"qst{i}", [NJ, 32 * P]) for i in range(2)]
            kst = [p5sb(f"kst{i}", [NJ, 32 * P]) for i in range(2)]
            ebf = [p5sb(f"ebf{i}", [P, 32 * P]) for i in range(3)]
            xasm = [xasm_e] + [
                p5ctx.enter_context(nc.sbuf_tensor(f"xasm{i}", [P, P * 32], F32, side="right"))
                for i in range(1, 3)]
            tmpa_full = p5ctx.enter_context(
                nc.sbuf_tensor("tmpa_full", [P, 4096], F32, side="right"))
            rdb = p5ctx.enter_context(nc.sbuf_tensor("rdb", [P, 32], F32, side="right"))
            s_ps = [p5ctx.enter_context(nc.psum_tensor(f"s_ps{i}", [P, 512], F32)) for i in range(3)]
            a_ps = [p5ctx.enter_context(nc.psum_tensor(f"a_ps{i}", [P, 512], F32)) for i in range(2)]
            b_ps = [p5ctx.enter_context(nc.psum_tensor(f"b_ps{i}", [P, 16], F32)) for i in range(2)]
            scr5 = scr_d.rearrange("(t cc d) -> cc t d", t=TPC, cc=16)

            def qk_load(eng, cki):
                t0 = (cki // 4) * P + (cki % 4) * 32
                eng.dma_start(out=qst[cki % 2][:].rearrange("j (t d) -> j t d", t=32),
                              in_=scr5[0:8, t0:t0 + 32, :]).then_inc(QK[cki % 2].h, 16)
                eng.dma_start(out=kst[cki % 2][:].rearrange("j (t d) -> j t d", t=32),
                              in_=scr5[8:16, t0:t0 + 32, :]).then_inc(QK[cki % 2].h, 16)
                QK[cki % 2].n += 32

            @block.sync
            def _(sy):
                sy.wait_ge(SCA.h, p4_sca_end)
                if taps:
                    sy.wait_ge(SQ.h, tap_xnT_sq)

                def emit_out(tt, qq):
                    sy.wait_ge(VEC.h, vmark_q[(tt, qq)])
                    sy.dma_start(
                        out=out_d[tt * P + qq * 32: tt * P + (qq + 1) * 32, :]
                        .rearrange("t (d h) -> d t h", h=32),
                        in_=xasm[tt % 3][:, qq * 1024:(qq + 1) * 1024]
                        .rearrange("d (t h) -> d t h", h=32)).then_inc(OD[tt % 3].h, 16)
                    OD[tt % 3].n += 16

                sy.wait_ge(BO[0].h, 8 * 16)
                sy.wait_ge(BO[1].h, 8 * 16)
                qk_load(sy, 0)
                qk_load(sy, 1)
                for cki in range(4 * NT):
                    if cki + 2 < 4 * NT:
                        sy.wait_ge(TEN.h, ten_schunk(cki))
                        qk_load(sy, cki + 2)
                    if cki % 4 == 3 and cki // 4 >= 2:
                        tt = cki // 4 - 2
                        emit_out(tt, 0)
                        emit_out(tt, 2)
                for tt in (NT - 2, NT - 1):
                    emit_out(tt, 0)
                    emit_out(tt, 2)

            @block.gpsimd
            def _(g):
                def emit_out1(ttE):
                    for qq in (1, 3):
                        g.wait_ge(VEC.h, vmark_q[(ttE, qq)])
                        g.dma_start(
                            out=out_d[ttE * P + qq * 32: ttE * P + (qq + 1) * 32, :]
                            .rearrange("t (d h) -> d t h", h=32),
                            in_=xasm[ttE % 3][:, qq * 1024:(qq + 1) * 1024]
                            .rearrange("d (t h) -> d t h", h=32)).then_inc(OG[ttE % 3].h, 16)
                        OG[ttE % 3].n += 16

                if taps:
                    g.wait_ge(SQ.h, tap_xnT_sq)
                for tt in range(NT):
                    if tt == 1:
                        g.wait_ge(SCA.h, p4_sca_end)
                    if tt >= 3:
                        g.wait_ge(OD[tt % 3].h, od_mark(tt - 3))
                        g.wait_ge(OG[tt % 3].h, od_mark(tt - 3))
                    for hf in range(2):
                        g.dma_start(
                            out=xasm[tt % 3][:, hf * 2048:(hf + 1) * 2048]
                            .rearrange("d (t h) -> d t h", h=32),
                            in_=x_in[tt * P + hf * 64: tt * P + (hf + 1) * 64, :]
                            .rearrange("t (d h) -> d t h", h=32)).then_inc(XS[tt % 3].h, 16)
                        XS[tt % 3].n += 16
                    if tt >= 2:
                        emit_out1(tt - 2)
                emit_out1(NT - 2)
                emit_out1(NT - 1)

            @block.tensor
            def _(t):
                t.wait_ge(SCA.h, p4_sca_end)
                for tt in range(NT):
                    for ck in range(4):
                        cki = tt * 4 + ck
                        t.wait_ge(QK[cki % 2].h, qk_mark(cki))
                        for g in range(8):
                            gidx = cki * 8 + g
                            if gidx >= 3:
                                pc, pg = divmod(gidx - 3, 8)
                                t.wait_ge(SCA.h, sca_exp(pc, pg))
                            for k4 in range(4):
                                ts = g * 4 + k4
                                mm = t.matmul(
                                    s_ps[gidx % 3][:, k4 * P:(k4 + 1) * P],
                                    kst[cki % 2][:, ts * P:(ts + 1) * P],
                                    qst[cki % 2][:, ts * P:(ts + 1) * P],
                                    start=True, stop=True)
                            mm.then_inc(TEN.h, 1)
                            TEN.n += 1
                        for ts in range(32):
                            sidx = ts // 16
                            vs = cki * 2 + sidx
                            if ts % 4 == 0:
                                t.wait_ge(SCA.h, sca_exp(cki, ts // 4))
                            if ts % 16 == 0 and vs >= 2:
                                t.wait_ge(VEC.h, vmark_mult[vs - 2])
                            tok = tt * P + ck * 32 + ts
                            vrep = vall[:, tok * 9:tok * 9 + 8].unsqueeze(2)\
                                .broadcast_to([P, 8, 4])
                            t.matmul(a_ps[vs % 2][:, (ts % 16) * 32:(ts % 16) * 32 + 32],
                                     ebf[cki % 3][:, ts * P:(ts + 1) * P], vrep,
                                     start=True, stop=True)
                            mmb = t.matmul(b_ps[vs % 2][:, ts % 16:ts % 16 + 1],
                                           ebf[cki % 3][:, ts * P:(ts + 1) * P],
                                           vall[:, tok * 9 + 8:tok * 9 + 9],
                                           start=True, stop=True)
                            if ts % 16 == 15:
                                mmb.then_inc(TEN.h, 1)
                                TEN.n += 1
            p5_ten_end = TEN.n

            @block.scalar
            def _(s):
                for tt in range(NT):
                    for ck in range(4):
                        cki = tt * 4 + ck
                        for g in range(8):
                            gidx = cki * 8 + g
                            s.wait_ge(TEN.h, ten_sgrp(cki, g))
                            if g == 0 and cki >= 3:
                                s.wait_ge(TEN.h, ten_osub(cki - 3, 1))
                            s.activation(ebf[cki % 3][:, g * 512:(g + 1) * 512],
                                         s_ps[gidx % 3][:], AF.Exp).then_inc(SCA.h, 1)
                            SCA.n += 1
            p5_sca_end = SCA.n

            @block.vector
            def _(v):
                v.wait_ge(SCA.h, p4_sca_end)
                for tt in range(NT):
                    for sub in range(8):
                        vs = tt * 8 + sub
                        v.wait_ge(TEN.h, ten_osub(vs // 2, vs % 2))
                        v.reciprocal(rdb[:, (sub % 2) * 16:(sub % 2) * 16 + 16],
                                     b_ps[vs % 2][:]).then_inc(VEC.h, 1)
                        v.wait_ge(VEC.h, VEC.n + 1)
                        rdv = rdb[:, (sub % 2) * 16:(sub % 2) * 16 + 16]\
                            .unsqueeze(2).broadcast_to([P, 16, 32])
                        v.tensor_tensor(
                            tmpa_full[:, sub * 512:(sub + 1) * 512]
                            .rearrange("p (t h) -> p t h", h=32),
                            a_ps[vs % 2][:].rearrange("p (t h) -> p t h", h=32),
                            rdv, ALU.mult).then_inc(VEC.h, 1)
                        VEC.n += 2
                        assert VEC.n == vmark_mult[vs]
                        if sub % 2 == 1:
                            qq = sub // 2
                            lo, hi = qq * 1024, (qq + 1) * 1024
                            if qq == 0:
                                v.wait_ge(XS[tt % 3].h, xs_mark(tt))
                            v.wait_ge(VEC.h, VEC.n)
                            v.tensor_tensor(xasm[tt % 3][:, lo:hi],
                                            xasm[tt % 3][:, lo:hi],
                                            tmpa_full[:, lo:hi], ALU.add)\
                                .then_inc(VEC.h, 1)
                            VEC.n += 1
                            v.wait_ge(VEC.h, VEC.n)
                            v.tensor_tensor(xasm[tt % 3][:, lo:hi],
                                            xasm[tt % 3][:, lo:hi],
                                            ffno[:, tt * 4096 + lo: tt * 4096 + hi],
                                            ALU.add).then_inc(VEC.h, 1)
                            VEC.n += 1
                            assert VEC.n == vmark_q[(tt, qq)]

            @block.sync
            def _(sy):
                if taps:
                    sy.wait_ge(SCA.h, p5_sca_end)
                    sy.dma_start(out=tap_d["vall"], in_=vall[:]).then_inc(SQ.h, 16)
                    sy.dma_start(out=tap_d["ffn"], in_=ffno[:]).then_inc(SQ.h, 16)
                    sy.dma_start(out=tap_d["hbc"], in_=hbc[:]).then_inc(SQ.h, 16)
                    SQ.n += 3 * 16
                sy.wait_ge(SQ.h, SQ.n)
                for sem in (GQ, BS, WQ[0], WQ[1], BO[0], BO[1],
                            XD[0], XD[1], XD[2],
                            QK[0], QK[1], XS[0], XS[1], XS[2],
                            OD[0], OD[1], OD[2], OG[0], OG[1], OG[2]):
                    sy.wait_ge(sem.h, sem.n)

    return nc


def _host_layout(inputs):
    Wq = np.asarray(inputs["L0_attn_Wq"])
    Wk = np.asarray(inputs["L0_attn_Wk"])
    Wv = np.asarray(inputs["L0_attn_Wv"])
    Wf = np.asarray(inputs["L0_ffn_weight"])
    wall = np.empty((NFT, 4096, P), np.float32)
    for j in range(8):
        for r in range(4):
            wall[j * 4 + r] = Wq[:, :, 4 * j + r]
    for j in range(8):
        wall[32 + j] = Wk[:, :, j]
        wall[40 + j] = Wv[:, :, j]
    wt = Wf.reshape(P, 32, 4096)
    for h in range(32):
        wall[48 + h] = np.ascontiguousarray(wt[:, h, :].T)
    base = {
        "wall": wall,
        "bq": np.ascontiguousarray(inputs["L0_attn_bq"], np.float32),
        "bk": np.ascontiguousarray(inputs["L0_attn_bk"], np.float32),
        "bv": np.ascontiguousarray(inputs["L0_attn_bv"], np.float32),
        "fb": np.ascontiguousarray(np.asarray(inputs["L0_ffn_bias"], np.float32).reshape(P, 32)),
        "ah": np.ascontiguousarray(np.asarray(inputs["L0_ffn_lora_A"], np.float32)
                                   .transpose(2, 0, 1).reshape(CH, P, 3, 16)),
        "bh": np.ascontiguousarray(np.asarray(inputs["L0_ffn_lora_B"], np.float32)
                                   .reshape(3, P, 32, 16).transpose(0, 3, 2, 1)
                                   .reshape(3, 16, 32 * P)),
        "lsi": np.ascontiguousarray(np.asarray(inputs["L0_ffn_lora_start_indices"], np.int32)
                                    .reshape(1, 4)),
        "ident": np.eye(P, dtype=np.float32),
    }
    return base


def kernel(**inputs):
    x = np.asarray(inputs["L0_attn_x"], np.float32)
    B, Sq, E = x.shape
    xf = np.ascontiguousarray(x.reshape(B * Sq, E))
    base = _host_layout(inputs)
    nc = build(taps=False)
    in_maps = []
    for c in range(8):
        m = dict(base)
        m["x_in"] = xf[c * TPC:(c + 1) * TPC]
        m["bsel"] = np.full((1, 1), float((c * TPC) // Sq), np.float32)
        in_maps.append(m)
    res = run_bass_kernel_spmd(nc, in_maps, core_ids=list(range(8)))
    out = np.concatenate([np.asarray(r["out"]) for r in res.results], axis=0)
    return out.reshape(B, Sq, E).astype(np.float32)


# version 19
# speedup vs baseline: 1.0495x; 1.0495x over previous
from contextlib import ExitStack

import numpy as np

import concourse.bass as bass
import concourse.mybir as mybir
from concourse.bass_utils import run_bass_kernel_spmd

F32 = mybir.dt.float32
BF16 = mybir.dt.bfloat16
FP16 = mybir.dt.float16
I32 = mybir.dt.int32
AF = mybir.ActivationFunctionType
ALU = mybir.AluOpType
AX = mybir.AxisListType.X

P = 128
TPC = 1024
NT = TPC // P
CH = 32
NJ = 8
NFT = 80
QSCALE = float(np.float32(1.0) / np.float32(np.sqrt(np.float32(128.0))))
EPS = 1e-5
DEBUG_TAPS = False


class S:
    def __init__(self, ctx, nc, name):
        self.h = ctx.enter_context(nc.semaphore(name))
        self.n = 0


def build(taps=DEBUG_TAPS, phases=5):
    nc = bass.Bass("TRN2", target_bir_lowering=False, debug=False)

    x_in = nc.dram_tensor("x_in", [TPC, 4096], F32, kind="ExternalInput").ap()
    wall = nc.dram_tensor("wall", [NFT, 4096, P], F32, kind="ExternalInput").ap()
    bq_d = nc.dram_tensor("bq", [P, 32], F32, kind="ExternalInput").ap()
    bk_d = nc.dram_tensor("bk", [P, 8], F32, kind="ExternalInput").ap()
    bv_d = nc.dram_tensor("bv", [P, 8], F32, kind="ExternalInput").ap()
    fb_d = nc.dram_tensor("fb", [P, 32], F32, kind="ExternalInput").ap()
    ah_d = nc.dram_tensor("ah", [CH, P, 3, 16], F32, kind="ExternalInput").ap()
    bh_d = nc.dram_tensor("bh", [3, 16, 32 * P], F32, kind="ExternalInput").ap()
    lsi_d = nc.dram_tensor("lsi", [1, 4], I32, kind="ExternalInput").ap()
    bsel_d = nc.dram_tensor("bsel", [1, 1], F32, kind="ExternalInput").ap()
    id_d = nc.dram_tensor("ident", [P, P], F32, kind="ExternalInput").ap()
    out_d = nc.dram_tensor("out", [TPC, 4096], F32, kind="ExternalOutput").ap()
    scr_d = nc.dram_tensor("qkscr", [TPC * 16 * P], BF16, kind="Internal").ap()
    tap_d = {}
    if taps:
        tap_d["xnT"] = nc.dram_tensor("tap_xnT", [P, CH * TPC], BF16, kind="ExternalOutput").ap()
        tap_d["qsks"] = nc.dram_tensor("tap_qsks", [P, TPC * 16], BF16, kind="ExternalOutput").ap()
        tap_d["vall"] = nc.dram_tensor("tap_vall", [P, TPC * 9], BF16, kind="ExternalOutput").ap()
        tap_d["xaT"] = nc.dram_tensor("tap_xaT", [16, TPC], BF16, kind="ExternalOutput").ap()
        tap_d["ffn"] = nc.dram_tensor("tap_ffn", [P, TPC * 32], FP16, kind="ExternalOutput").ap()
        tap_d["hbc"] = nc.dram_tensor("tap_hbc", [P, 3], F32, kind="ExternalOutput").ap()
        tap_d["bselt"] = nc.dram_tensor("tap_bselt", [16, 4096], BF16, kind="ExternalOutput").ap()
        tap_d["lnst"] = nc.dram_tensor("tap_lnst", [P, 80], F32, kind="ExternalOutput").ap()

    with ExitStack() as ctx:
        def sbl(n_, s_, d_):
            return ctx.enter_context(nc.sbuf_tensor(n_, s_, d_))

        def sbr(n_, s_, d_):
            return ctx.enter_context(nc.sbuf_tensor(n_, s_, d_, side="right"))

        id_sb = sbl("id_sb", [P, P], BF16)
        bqss = sbl("bqss", [P, NJ], F32)
        bk_sb = sbl("bk_sb", [P, 8], F32)
        bv_sb = sbl("bv_sb", [P, 8], F32)
        fb_sb = sbl("fb_sb", [P, 32], F32)
        bq_sb = sbl("bq_sb", [P, 32], F32)
        hbc = sbl("hbc", [P, 3], F32)
        ones_r = sbl("ones_r", [1, P], BF16)
        h3b = sbl("h3b", [1, 3], BF16)
        lsi_sb = sbl("lsi_sb", [1, 4], I32)
        lsif = sbl("lsif", [1, 4], F32)
        cle = sbl("cle", [1, 4], F32)
        bsel_sb = sbl("bsel_sb", [1, 1], F32)
        tmp1 = sbl("tmp1", [1, 4], F32)
        epsc = sbl("epsc", [P, 1], F32)
        ps_h = ctx.enter_context(nc.psum_tensor("ps_h", [P, 3], F32))
        xasm_e = sbr("xasm_e", [P, P * 32], F32)
        xaT_guard = nc.sbuf_tensor("xaT", [16, TPC], BF16)
        xaT = xaT_guard.__enter__()
        xnT_guard = nc.sbuf_tensor("xnT", [P, CH * TPC], BF16)
        xnT = xnT_guard.__enter__()
        wsl0_guard = nc.sbuf_tensor("wsl0", [P, CH * P], BF16)
        wsl1_guard = nc.sbuf_tensor("wsl1", [P, CH * P], BF16)
        wsl = [wsl0_guard.__enter__(), wsl1_guard.__enter__()]

        GQ = S(ctx, nc, "GQ")
        BS = S(ctx, nc, "BS")
        WQ = [S(ctx, nc, "WQ0"), S(ctx, nc, "WQ1")]
        BO = [S(ctx, nc, "BO0"), S(ctx, nc, "BO1")]
        XD = [S(ctx, nc, f"XD{i}") for i in range(3)]
        QK = [S(ctx, nc, "QK0"), S(ctx, nc, "QK1")]
        OD = [S(ctx, nc, f"OD{i}") for i in range(3)]
        OG = [S(ctx, nc, f"OG{i}") for i in range(3)]
        XS = [S(ctx, nc, f"XS{i}") for i in range(3)]
        SQ = S(ctx, nc, "SQ")
        TEN = S(ctx, nc, "TEN")
        SCA = S(ctx, nc, "SCA")
        VEC = S(ctx, nc, "VEC")
        PL = S(ctx, nc, "PL")

        block = ctx.enter_context(nc.Block())

        p1ctx = ExitStack()
        xt = [p1ctx.enter_context(nc.sbuf_tensor(f"xt{i}", [P, 4096], F32)) for i in range(3)]
        sqd = [p1ctx.enter_context(nc.sbuf_tensor(f"sqd{i}", [P, 4096], BF16))
               for i in range(2)]
        xnt_ = [p1ctx.enter_context(nc.sbuf_tensor(f"xn{i}", [P, 4096], BF16)) for i in range(2)]
        lnst = p1ctx.enter_context(nc.sbuf_tensor("lnst", [P, 80], F32))
        tr_ps = [p1ctx.enter_context(nc.psum_tensor(f"tr_ps{i}", [P, 512], BF16)) for i in range(4)]

        @block.gpsimd
        def _(g):
            g.dma_start(out=id_sb[:], in_=id_d).then_inc(GQ.h, 16)
            g.dma_start(out=bq_sb[:], in_=bq_d).then_inc(GQ.h, 16)
            g.dma_start(out=bk_sb[:], in_=bk_d).then_inc(GQ.h, 16)
            g.dma_start(out=bv_sb[:], in_=bv_d).then_inc(GQ.h, 16)
            g.dma_start(out=fb_sb[:], in_=fb_d).then_inc(GQ.h, 16)
            g.dma_start(out=lsi_sb[:], in_=lsi_d).then_inc(GQ.h, 16)
            g.dma_start(out=bsel_sb[:], in_=bsel_d).then_inc(GQ.h, 16)
            GQ.n += 7 * 16
            for ft in range(2):
                g.dma_start(out=wsl[ft][:].rearrange("p (c f) -> p c f", f=P),
                            in_=wall[ft].rearrange("(c p) f -> p c f", p=P))\
                    .then_inc(WQ[ft].h, 16)
                WQ[ft].n += 16
        p0_gq = GQ.n

        @block.vector
        def _(v):
            v.memset(epsc[:], EPS).then_inc(VEC.h, 1); VEC.n += 1
            v.memset(lnst[:], 0.0).then_inc(VEC.h, 1); VEC.n += 1
        assert VEC.n == 2, VEC.n

        p1_norm_mark = [None] * NT
        p1_trev_mark = [None] * NT
        p1_tr_mark = [[None] * 8 for _ in range(NT)]
        p1_var_mark = [None] * NT
        p1_nmr_mark = [None] * NT

        if True:
            def STT(tt, i):
                return lnst[:, tt * 10 + i: tt * 10 + i + 1]

            p1_sq_mark = [None] * NT
            p1_ln_mark = [None] * NT
            p1_exp_mark = [None] * NT
            p1_hn_mark = [None] * NT
            p1_evac_mark = [[None] * 8 for _ in range(NT)]
            vec, ten = VEC.n, TEN.n
            for tt in range(NT):
                vec += 3
                vec += 2
                p1_var_mark[tt] = vec
                vec += 1
                p1_hn_mark[tt] = vec
                vec += 1
                p1_norm_mark[tt] = vec
                for b4 in range(8):
                    ten += 1
                    p1_tr_mark[tt][b4] = ten
            sca = SCA.n
            for tt in range(NT):
                sca += 1
                p1_sq_mark[tt] = sca
                if tt >= 1:
                    for b4 in range(8):
                        sca += 1
                        p1_evac_mark[tt - 1][b4] = sca
                sca += 1
                p1_ln_mark[tt] = sca
                sca += 1
                p1_exp_mark[tt] = sca
            for b4 in range(8):
                sca += 1
                p1_evac_mark[NT - 1][b4] = sca
            for tt in range(NT):
                p1_trev_mark[tt] = p1_evac_mark[tt][7]

            def xd_mark(tt):
                return 16 * (tt // 3 + 1) + (16 if tt % 3 == 0 else 0)

            @block.sync
            def _(sy):
                for tt in range(NT):
                    if tt >= 3:
                        sy.wait_ge(VEC.h, p1_norm_mark[tt - 3])
                        sy.wait_ge(SCA.h, p1_sq_mark[tt - 3])
                    if tt == 0:
                        sy.dma_start(out=xt[0][:, 0:2048],
                                     in_=x_in[0:P, 0:2048]).then_inc(XD[0].h, 16)
                        XD[0].n += 16
                    else:
                        sy.dma_start(out=xt[tt % 3][:],
                                     in_=x_in[tt * P:(tt + 1) * P, :]).then_inc(XD[tt % 3].h, 16)
                        XD[tt % 3].n += 16

            @block.vector
            def _(v):
                v.wait_ge(VEC.h, VEC.n)
                for tt in range(NT):
                    v.wait_ge(XD[tt % 3].h, xd_mark(tt))
                    v.reduce_sum(STT(tt, 0), xt[tt % 3][:], axis=AX).then_inc(VEC.h, 1)
                    VEC.n += 1
                    v.wait_ge(VEC.h, VEC.n)
                    v.tensor_scalar_mul(STT(tt, 1), STT(tt, 0), -1.0 / 4096.0).then_inc(VEC.h, 1)
                    VEC.n += 1
                    v.wait_ge(VEC.h, VEC.n)
                    v.tensor_tensor(STT(tt, 2), STT(tt, 1), STT(tt, 1), ALU.mult).then_inc(VEC.h, 1)
                    VEC.n += 1
                    v.wait_ge(SCA.h, p1_sq_mark[tt])
                    v.tensor_scalar_mul(STT(tt, 3), STT(tt, 7), 1.0 / 4096.0).then_inc(VEC.h, 1)
                    VEC.n += 1
                    v.wait_ge(VEC.h, VEC.n)
                    v.tensor_tensor(STT(tt, 3), STT(tt, 3), STT(tt, 2), ALU.subtract).then_inc(VEC.h, 1)
                    VEC.n += 1
                    assert VEC.n == p1_var_mark[tt]
                    v.wait_ge(SCA.h, p1_ln_mark[tt])
                    v.tensor_scalar_mul(STT(tt, 8), STT(tt, 4), -0.5).then_inc(VEC.h, 1)
                    VEC.n += 1
                    assert VEC.n == p1_hn_mark[tt]
                    v.wait_ge(SCA.h, p1_exp_mark[tt])
                    if tt >= 2:
                        v.wait_ge(TEN.h, p1_tr_mark[tt - 2][7])
                    v.tensor_scalar(xnt_[tt % 2][:], xt[tt % 3][:], STT(tt, 1), STT(tt, 5),
                                    ALU.add, ALU.mult).then_inc(VEC.h, 1)
                    VEC.n += 1
                    assert VEC.n == p1_norm_mark[tt]

            def p1_emit_evacs(s, te):
                pair = 2 * (te % 2)
                for b4 in range(8):
                    s.wait_ge(TEN.h, p1_tr_mark[te][b4])
                    dst = xnT[:].rearrange("p (c t) -> p c t", t=TPC)[
                        :, b4 * 4:(b4 + 1) * 4, te * P:(te + 1) * P]
                    s.activation(dst, tr_ps[pair + b4 % 2][:].rearrange("p (c t) -> p c t", t=P),
                                 AF.Copy).then_inc(SCA.h, 1)
                    SCA.n += 1
                    assert SCA.n == p1_evac_mark[te][b4]

            @block.scalar
            def _(s):
                s.dma_start(out=xt[0][:, 2048:4096],
                            in_=x_in[0:P, 2048:4096]).then_inc(XD[0].h, 16)
                XD[0].n += 16
                s.wait_ge(VEC.h, 2)
                for tt in range(NT):
                    s.wait_ge(XD[tt % 3].h, xd_mark(tt))
                    s.activation(sqd[tt % 2][:], xt[tt % 3][:], AF.Square,
                                 accum_out=STT(tt, 7)).then_inc(SCA.h, 1)
                    SCA.n += 1
                    assert SCA.n == p1_sq_mark[tt]
                    if tt >= 1:
                        p1_emit_evacs(s, tt - 1)
                    s.wait_ge(VEC.h, p1_var_mark[tt])
                    s.activation(STT(tt, 4), STT(tt, 3), AF.Ln, bias=epsc[:, 0:1]).then_inc(SCA.h, 1)
                    SCA.n += 1
                    assert SCA.n == p1_ln_mark[tt]
                    s.wait_ge(VEC.h, p1_hn_mark[tt])
                    s.activation(STT(tt, 5), STT(tt, 8), AF.Exp).then_inc(SCA.h, 1)
                    SCA.n += 1
                    assert SCA.n == p1_exp_mark[tt]
                p1_emit_evacs(s, NT - 1)

            @block.tensor
            def _(t):
                t.wait_ge(GQ.h, p0_gq)
                for tt in range(NT):
                    pair = 2 * (tt % 2)
                    t.wait_ge(VEC.h, p1_norm_mark[tt])
                    if tt >= 2:
                        t.wait_ge(SCA.h, p1_evac_mark[tt - 2][7])
                    for b4 in range(8):
                        if b4 >= 2:
                            t.wait_ge(SCA.h, p1_evac_mark[tt][b4 - 2])
                        for c in range(4):
                            cc = b4 * 4 + c
                            tr = t.transpose(tr_ps[pair + b4 % 2][:, c * P:(c + 1) * P],
                                             xnt_[tt % 2][:, cc * P:(cc + 1) * P], id_sb[:])
                        tr.then_inc(TEN.h, 1)
                        TEN.n += 1
                        assert TEN.n == p1_tr_mark[tt][b4]
            if taps:
                @block.sync
                def _(sy):
                    sy.wait_ge(SCA.h, p1_trev_mark[NT - 1])
                    sy.dma_start(out=tap_d["lnst"], in_=lnst[:]).then_inc(SQ.h, 16)
                    SQ.n += 16
            tap_lnst_sq = SQ.n
            p1_ten_end, p1_sca_end = TEN.n, SCA.n
        p1ctx.close()
        vall = sbr("vall", [P, TPC * 9], BF16)

        aselt_guard = nc.sbuf_tensor("aselt", [P, CH * 16], BF16)
        aselt = aselt_guard.__enter__()
        bselt_guard = nc.sbuf_tensor("bselt", [16, 32 * P], BF16)
        bselt = bselt_guard.__enter__()
        p0bctx = ExitStack()
        a_all = p0bctx.enter_context(nc.sbuf_tensor("a_all", [P, CH * 48], BF16, side="right"))
        bsl = [p0bctx.enter_context(nc.sbuf_tensor(f"bsl{i}", [16, 32 * P], BF16, side="right"))
               for i in range(3)]
        p0b_vec0 = 2 + NT * 7
        p0b_mask_mark = p0b_vec0 + 12
        p0_vec = p0b_mask_mark + 6
        p0_b_vec_mark = [p0b_mask_mark + 3 + (a + 1) for a in range(3)]
        p0b_gq = p0_gq + 16

        @block.vector
        def _(v):
            assert VEC.n == p0b_vec0, VEC.n
            v.wait_ge(GQ.h, p0_gq)
            v.tensor_copy(lsif[:], lsi_sb[:]).then_inc(VEC.h, 1); VEC.n += 1
            v.wait_ge(VEC.h, VEC.n)
            v.tensor_scalar(cle[:], lsif[:], bsel_sb[0:1, 0:1], None, ALU.is_le)\
                .then_inc(VEC.h, 1); VEC.n += 1
            v.wait_ge(VEC.h, VEC.n)
            v.tensor_tensor(h3b[0:1, 2:3], cle[0:1, 2:3], cle[0:1, 3:4], ALU.mult)\
                .then_inc(VEC.h, 1); VEC.n += 1
            v.wait_ge(VEC.h, VEC.n)
            v.tensor_tensor(h3b[0:1, 1:2], cle[0:1, 2:3], h3b[0:1, 2:3], ALU.subtract)\
                .then_inc(VEC.h, 1); VEC.n += 1
            v.tensor_tensor(tmp1[0:1, 0:1], cle[0:1, 1:2], cle[0:1, 2:3], ALU.mult)\
                .then_inc(VEC.h, 1); VEC.n += 1
            v.wait_ge(VEC.h, VEC.n)
            v.tensor_tensor(h3b[0:1, 0:1], cle[0:1, 1:2], tmp1[0:1, 0:1], ALU.subtract)\
                .then_inc(VEC.h, 1); VEC.n += 1
            v.memset(ones_r[:], 1.0).then_inc(VEC.h, 1); VEC.n += 1
            bqv = bq_sb[:].rearrange("p (j r) -> p j r", r=4)
            v.tensor_tensor(bqss[:], bqv[:, :, 0], bqv[:, :, 1], ALU.add)\
                .then_inc(VEC.h, 1); VEC.n += 1
            v.wait_ge(VEC.h, VEC.n)
            v.tensor_tensor(bqss[:], bqss[:], bqv[:, :, 2], ALU.add)\
                .then_inc(VEC.h, 1); VEC.n += 1
            v.wait_ge(VEC.h, VEC.n)
            v.tensor_tensor(bqss[:], bqss[:], bqv[:, :, 3], ALU.add)\
                .then_inc(VEC.h, 1); VEC.n += 1
            v.wait_ge(VEC.h, VEC.n)
            v.tensor_scalar_mul(bqss[:], bqss[:], QSCALE).then_inc(VEC.h, 1); VEC.n += 1
            v.memset(vall[:], 1.0).then_inc(VEC.h, 1); VEC.n += 1
            assert VEC.n == p0b_mask_mark, VEC.n

        p2_ten0, p2_sca0 = TEN.n, SCA.n
        NQKV = 48

        def wq_mark(ft):
            return 16 * (ft // 2 + 1)

        if phases < 2:
            @block.sync
            def _(sy):
                if taps:
                    sy.wait_ge(SCA.h, p1_trev_mark[NT - 1])
                    sy.dma_start(out=tap_d["xnT"], in_=xnT[:]).then_inc(SQ.h, 16)
                    SQ.n += 16
                sy.wait_ge(SQ.h, SQ.n)
                for sem in (GQ, BS, XD[0], XD[1]):
                    sy.wait_ge(sem.h, sem.n)
            p0bctx.close()
            bselt_guard.__exit__(None, None, None)
            aselt_guard.__exit__(None, None, None)
            wsl1_guard.__exit__(None, None, None)
            wsl0_guard.__exit__(None, None, None)
            xnT_guard.__exit__(None, None, None)
            xaT_guard.__exit__(None, None, None)
            return nc
        def p2tm(ft):
            return p2_ten0 + ft + 1 + (1 if ft >= 3 else 0)

        @block.gpsimd
        def _(g):
            for ft in range(2, NQKV):
                if ft >= 2:
                    g.wait_ge(TEN.h, p2tm(ft - 2))
                g.dma_start(out=wsl[ft % 2][:].rearrange("p (c f) -> p c f", f=P),
                            in_=wall[ft].rearrange("(c p) f -> p c f", p=P))\
                    .then_inc(WQ[ft % 2].h, 16)
                WQ[ft % 2].n += 16
                if ft == 2:
                    g.wait_ge(SCA.h, p1_trev_mark[NT - 1])
                    if taps:
                        g.wait_ge(SQ.h, tap_lnst_sq)
                    g.dma_start(out=a_all[:].rearrange("p (c a r) -> p c a r", c=CH, a=3),
                                in_=ah_d.rearrange("c p a r -> p c a r")).then_inc(GQ.h, 16)
                    GQ.n += 16
                    for a in range(3):
                        g.dma_start(out=bsl[a][:], in_=bh_d[a]).then_inc(BS.h, 16)
                    BS.n += 3 * 16

        with nc.sbuf_tensor("qsks", [P, TPC * 16], BF16, side="right") as qsks:
            with nc.psum_tensor("qk_ps0", [P, 512], F32) as qk_ps0, \
                 nc.psum_tensor("qk_ps1", [P, 512], F32) as qk_ps1, \
                 nc.psum_tensor("qk_ps2", [P, 512], F32) as qk_ps2, \
                 nc.psum_tensor("qk_ps3", [P, 512], F32) as qk_ps3, \
                 nc.psum_tensor("xa_ps0", [16, 512], F32) as xa_ps0, \
                 nc.psum_tensor("xa_ps1", [16, 512], F32) as xa_ps1:
                qk_ps = [qk_ps0, qk_ps1, qk_ps2, qk_ps3]
                xa_ps = [xa_ps0, xa_ps1]

                @block.tensor
                def _(t):
                    t.wait_ge(SCA.h, p1_trev_mark[NT - 1])
                    for ft in range(NQKV):
                        t.wait_ge(WQ[ft % 2].h, wq_mark(ft))
                        if ft < 32:
                            j, r = divmod(ft, 4)
                            first, last = (r == 0), (r == 3)
                            bank = 2 * (j % 2)
                            if first and j >= 2:
                                t.wait_ge(SCA.h, p2_sca0 + 1 + 2 * (j - 1))
                        else:
                            first = last = True
                            bank = 2 * (ft % 2)
                            if ft == 32:
                                t.wait_ge(SCA.h, p2_sca0 + 15)
                            elif ft == 33:
                                t.wait_ge(SCA.h, p2_sca0 + 17)
                            else:
                                t.wait_ge(SCA.h, p2_sca0 + 17 + 2 * (ft - 33))
                        for c in range(CH):
                            for half in range(2):
                                mm = t.matmul(
                                    qk_ps[bank + half][:],
                                    wsl[ft % 2][:, c * P:(c + 1) * P],
                                    xnT[:, c * TPC + half * 512: c * TPC + (half + 1) * 512],
                                    start=(first and c == 0), stop=(last and c == CH - 1))
                        mm.then_inc(TEN.h, 1)
                        TEN.n += 1
                        if ft == 2:
                            t.wait_ge(VEC.h, p0b_mask_mark)
                            t.matmul(ps_h[:], ones_r[:], h3b[:], start=True, stop=True)\
                                .then_inc(TEN.h, 1)
                            TEN.n += 1
                    t.wait_ge(VEC.h, p0_vec)
                    for c in range(CH):
                        for half in range(2):
                            mm = t.matmul(xa_ps[half][:],
                                          aselt[:, c * 16:(c + 1) * 16],
                                          xnT[:, c * TPC + half * 512: c * TPC + (half + 1) * 512],
                                          start=(c == 0), stop=(c == CH - 1))
                    mm.then_inc(TEN.h, 1)
                    TEN.n += 1
                p2_ten_end = TEN.n
                p0_ten = p2_ten0 + 4
                assert p2_ten_end == p2_ten0 + NQKV + 2

                @block.scalar
                def _(s):
                    qv = qsks[:].rearrange("p (t cc) -> p t cc", cc=16)
                    vv = vall[:].rearrange("p (t cc) -> p t cc", cc=9)
                    s.wait_ge(TEN.h, p0_ten)
                    s.activation(hbc[:], ps_h[:], AF.Copy).then_inc(SCA.h, 1)
                    SCA.n += 1
                    for j in range(NJ):
                        s.wait_ge(TEN.h, p2tm(4 * j + 3))
                        if j == 0:
                            s.wait_ge(VEC.h, p0b_mask_mark)
                        bank = 2 * (j % 2)
                        for half in range(2):
                            dst = qv[:, half * 512:(half + 1) * 512, j]
                            s.activation(dst, qk_ps[bank + half][:], AF.Identity,
                                         bias=bqss[:, j:j + 1], scale=QSCALE)\
                                .then_inc(SCA.h, 1)
                            SCA.n += 1
                    for ft in range(32, NQKV):
                        s.wait_ge(TEN.h, p2tm(ft))
                        kv = ft - 32
                        j = kv % 8
                        bank = 2 * (ft % 2)
                        for half in range(2):
                            if kv < 8:
                                dst = qv[:, half * 512:(half + 1) * 512, 8 + j]
                                bias = bk_sb[:, j:j + 1]
                            else:
                                dst = vv[:, half * 512:(half + 1) * 512, j]
                                bias = bv_sb[:, j:j + 1]
                            s.activation(dst, qk_ps[bank + half][:], AF.Identity,
                                         bias=bias).then_inc(SCA.h, 1)
                            SCA.n += 1
                    s.wait_ge(TEN.h, p2_ten_end)
                    for half in range(2):
                        s.activation(xaT[:, half * 512:(half + 1) * 512],
                                     xa_ps[half][:], AF.Copy).then_inc(SCA.h, 1)
                        SCA.n += 1
                p2_sca_end = SCA.n
                p0_sca = p2_sca0 + 1
                assert p2_sca_end == p2_sca0 + 51

                @block.vector
                def _(v):
                    v.wait_ge(SCA.h, p0_sca)
                    v.wait_ge(GQ.h, p0b_gq)
                    av = a_all[:].rearrange("p (c a r) -> p c a r", c=CH, a=3)
                    sv = aselt[:].rearrange("p (c r) -> p c r", c=CH)
                    v.tensor_scalar(sv, av[:, :, 0, :], hbc[:, 0:1], None, ALU.mult)\
                        .then_inc(VEC.h, 1); VEC.n += 1
                    for a in (1, 2):
                        v.wait_ge(VEC.h, VEC.n)
                        v.scalar_tensor_tensor(sv, av[:, :, a, :], hbc[:, a:a + 1], sv,
                                               ALU.mult, ALU.add).then_inc(VEC.h, 1); VEC.n += 1
                    for a in range(3):
                        v.wait_ge(BS.h, 48)
                        if a == 0:
                            v.tensor_scalar(bselt[:], bsl[a][:], hbc[0:16, 0:1], None, ALU.mult)\
                                .then_inc(VEC.h, 1); VEC.n += 1
                        else:
                            v.wait_ge(VEC.h, VEC.n)
                            v.scalar_tensor_tensor(bselt[:], bsl[a][:], hbc[0:16, a:a + 1],
                                                   bselt[:], ALU.mult, ALU.add)\
                                .then_inc(VEC.h, 1); VEC.n += 1
                        assert p0_b_vec_mark[a] == VEC.n, (a, p0_b_vec_mark[a], VEC.n)
                assert VEC.n == p0_vec, (VEC.n, p0_vec)

            p3_ten0, p3_sca0 = TEN.n, SCA.n
            NB = 32
            with nc.sbuf_tensor("tsb0", [P, 1024], BF16) as tsb0, \
                 nc.sbuf_tensor("tsb1", [P, 1024], BF16) as tsb1, \
                 nc.psum_tensor("p3_ps0", [P, 512], BF16) as p3_ps0, \
                 nc.psum_tensor("p3_ps1", [P, 512], BF16) as p3_ps1:
                tsb = [tsb0, tsb1]
                p3_ps = [p3_ps0, p3_ps1]

                @block.tensor
                def _(t):
                    t.wait_ge(SCA.h, p2_sca_end)
                    for b in range(NB):
                        if b >= 2:
                            t.wait_ge(SCA.h, p3_sca0 + b - 1)
                        for g4 in range(4):
                            col = (b * 4 + g4) * P
                            tr = t.transpose(p3_ps[b % 2][:, g4 * P:(g4 + 1) * P],
                                             qsks[:, col:col + P], id_sb[:])
                        tr.then_inc(TEN.h, 1)
                        TEN.n += 1
                p3_ten_end = TEN.n

                @block.scalar
                def _(s):
                    if taps:
                        s.wait_ge(SQ.h, tap_lnst_sq)
                    for b in range(NB):
                        s.wait_ge(TEN.h, p3_ten0 + b + 1)
                        if b >= 4:
                            bbp = b // 2 - 2
                            s.wait_ge(BO[bbp % 2].h, 16 * (bbp // 2 + 1))
                        s.activation(tsb[(b // 2) % 2][:, (b % 2) * 512:(b % 2 + 1) * 512],
                                     p3_ps[b % 2][:], AF.Copy).then_inc(SCA.h, 1)
                        SCA.n += 1
                p3_sca_end = SCA.n

                @block.sync
                def _(sy):
                    scr4 = scr_d.rearrange("(bb gg p d) -> bb p gg d", bb=16, gg=8, p=P)
                    for bb in range(16):
                        sy.wait_ge(SCA.h, p3_sca0 + 2 * bb + 2)
                        sy.dma_start(out=scr4[bb],
                                    in_=tsb[bb % 2][:].rearrange("p (gg d) -> p gg d", gg=8))\
                            .then_inc(BO[bb % 2].h, 16)
                        BO[bb % 2].n += 16

                if taps:
                    @block.sync
                    def _(sy):
                        sy.wait_ge(TEN.h, p3_ten_end)
                        sy.dma_start(out=tap_d["qsks"], in_=qsks[:]).then_inc(SQ.h, 16)
                        SQ.n += 16
                tap_qsks_sq = SQ.n

        p0bctx.close()

        ffno = ctx.enter_context(nc.sbuf_tensor("ffno", [P, TPC * 32], FP16, side="right"))
        p4_ten0, p4_sca0 = TEN.n, SCA.n
        with nc.psum_tensor("f_ps0", [P, 512], F32) as f_ps0, \
             nc.psum_tensor("f_ps1", [P, 512], F32) as f_ps1, \
             nc.psum_tensor("f_ps2", [P, 512], F32) as f_ps2, \
             nc.psum_tensor("f_ps3", [P, 512], F32) as f_ps3:
            f_ps = [f_ps0, f_ps1, f_ps2, f_ps3]

            @block.gpsimd
            def _(g):
                for hh in range(32):
                    ft = NQKV + hh
                    if hh == 0:
                        g.wait_ge(TEN.h, p2tm(46))
                    elif hh == 1:
                        g.wait_ge(TEN.h, p2tm(47))
                    else:
                        g.wait_ge(TEN.h, p4_ten0 + hh - 1)
                    g.dma_start(out=wsl[ft % 2][:].rearrange("p (c f) -> p c f", f=P),
                                in_=wall[ft].rearrange("(c p) f -> p c f", p=P))\
                        .then_inc(WQ[ft % 2].h, 16)
                    WQ[ft % 2].n += 16

            @block.tensor
            def _(t):
                t.wait_ge(SCA.h, p3_sca_end)
                t.wait_ge(VEC.h, p0_vec)
                for hh in range(32):
                    ft = NQKV + hh
                    t.wait_ge(WQ[ft % 2].h, wq_mark(ft))
                    bank = 2 * (hh % 2)
                    if hh >= 2:
                        t.wait_ge(SCA.h, p4_sca0 + 2 * (hh - 1))
                    for c in range(CH):
                        for half in range(2):
                            t.matmul(f_ps[bank + half][:],
                                     wsl[ft % 2][:, c * P:(c + 1) * P],
                                     xnT[:, c * TPC + half * 512: c * TPC + (half + 1) * 512],
                                     start=(c == 0), stop=False)
                    for half in range(2):
                        mm = t.matmul(f_ps[bank + half][:],
                                      bselt[:, hh * P:(hh + 1) * P],
                                      xaT[:, half * 512:(half + 1) * 512],
                                      start=False, stop=True)
                    mm.then_inc(TEN.h, 1)
                    TEN.n += 1
            p4_ten_end = TEN.n

            if taps:
                @block.sync
                def _(sy):
                    sy.wait_ge(TEN.h, p4_ten_end)
                    sy.dma_start(out=tap_d["xnT"], in_=xnT[:]).then_inc(SQ.h, 16)
                    sy.dma_start(out=tap_d["bselt"], in_=bselt[:]).then_inc(SQ.h, 16)
                    sy.dma_start(out=tap_d["xaT"], in_=xaT[:]).then_inc(SQ.h, 16)
                    SQ.n += 3 * 16
            tap_xnT_sq = SQ.n

            @block.scalar
            def _(s):
                if taps:
                    s.wait_ge(SQ.h, tap_qsks_sq)
                fv = ffno[:].rearrange("p (t h) -> p t h", h=32)
                for hh in range(32):
                    s.wait_ge(TEN.h, p4_ten0 + hh + 1)
                    bank = 2 * (hh % 2)
                    for half in range(2):
                        s.activation(fv[:, half * 512:(half + 1) * 512, hh],
                                     f_ps[bank + half][:], AF.Identity,
                                     bias=fb_sb[:, hh:hh + 1]).then_inc(SCA.h, 1)
                        SCA.n += 1
            p4_sca_end = SCA.n

        bselt_guard.__exit__(None, None, None)
        aselt_guard.__exit__(None, None, None)
        wsl1_guard.__exit__(None, None, None)
        wsl0_guard.__exit__(None, None, None)
        xnT_guard.__exit__(None, None, None)
        xaT_guard.__exit__(None, None, None)

        p5_ten0, p5_sca0, p5_vec0 = TEN.n, SCA.n, VEC.n
        ten_sgrp = lambda ck, g: p5_ten0 + ck * 10 + g + 1
        ten_schunk = lambda ck: p5_ten0 + ck * 10 + 8
        ten_osub = lambda ck, sidx: p5_ten0 + ck * 10 + 8 + sidx + 1
        sca_exp = lambda ck, g: p5_sca0 + ck * 8 + g + 1

        vmark_mult = {}
        vmark_q = {}
        vmark_tile = {}
        nv = p5_vec0
        for tt in range(NT):
            for sub in range(8):
                nv += 2
                vmark_mult[tt * 8 + sub] = nv
                if sub % 2 == 1:
                    nv += 2
                    vmark_q[(tt, sub // 2)] = nv
            vmark_tile[tt] = nv
        p5_vec_end = nv

        def qk_mark(cki):
            return 32 * (cki // 2 + 1)

        def xs_mark(tt):
            return 32 * (tt // 3 + 1)

        def od_mark(tt):
            return 32 * (tt // 3 + 1)

        with ExitStack() as p5ctx:
            def p5sb(n_, s_):
                return p5ctx.enter_context(nc.sbuf_tensor(n_, s_, BF16, side="right"))
            qst = [p5sb(f"qst{i}", [NJ, 32 * P]) for i in range(2)]
            kst = [p5sb(f"kst{i}", [NJ, 32 * P]) for i in range(2)]
            ebf = [p5sb(f"ebf{i}", [P, 32 * P]) for i in range(3)]
            xasm = [xasm_e] + [
                p5ctx.enter_context(nc.sbuf_tensor(f"xasm{i}", [P, P * 32], F32, side="right"))
                for i in range(1, 3)]
            tmpa_full = p5ctx.enter_context(
                nc.sbuf_tensor("tmpa_full", [P, 4096], F32, side="right"))
            rdb = p5ctx.enter_context(nc.sbuf_tensor("rdb", [P, 32], F32, side="right"))
            s_ps = [p5ctx.enter_context(nc.psum_tensor(f"s_ps{i}", [P, 512], F32)) for i in range(3)]
            a_ps = [p5ctx.enter_context(nc.psum_tensor(f"a_ps{i}", [P, 512], F32)) for i in range(2)]
            b_ps = [p5ctx.enter_context(nc.psum_tensor(f"b_ps{i}", [P, 16], F32)) for i in range(2)]
            scr5 = scr_d.rearrange("(t cc d) -> cc t d", t=TPC, cc=16)

            def qk_load(eng, cki):
                t0 = (cki // 4) * P + (cki % 4) * 32
                eng.dma_start(out=qst[cki % 2][:].rearrange("j (t d) -> j t d", t=32),
                              in_=scr5[0:8, t0:t0 + 32, :]).then_inc(QK[cki % 2].h, 16)
                eng.dma_start(out=kst[cki % 2][:].rearrange("j (t d) -> j t d", t=32),
                              in_=scr5[8:16, t0:t0 + 32, :]).then_inc(QK[cki % 2].h, 16)
                QK[cki % 2].n += 32

            @block.sync
            def _(sy):
                sy.wait_ge(SCA.h, p4_sca_end)
                if taps:
                    sy.wait_ge(SQ.h, tap_xnT_sq)

                def emit_out(tt, qq):
                    sy.wait_ge(VEC.h, vmark_q[(tt, qq)])
                    sy.dma_start(
                        out=out_d[tt * P + qq * 32: tt * P + (qq + 1) * 32, :]
                        .rearrange("t (d h) -> d t h", h=32),
                        in_=xasm[tt % 3][:, qq * 1024:(qq + 1) * 1024]
                        .rearrange("d (t h) -> d t h", h=32)).then_inc(OD[tt % 3].h, 16)
                    OD[tt % 3].n += 16

                for tt in range(NT):
                    emit_out(tt, 0)
                    emit_out(tt, 2)

            @block.gpsimd
            def _(g):
                def emit_out1(ttE):
                    for qq in (1, 3):
                        g.wait_ge(VEC.h, vmark_q[(ttE, qq)])
                        g.dma_start(
                            out=out_d[ttE * P + qq * 32: ttE * P + (qq + 1) * 32, :]
                            .rearrange("t (d h) -> d t h", h=32),
                            in_=xasm[ttE % 3][:, qq * 1024:(qq + 1) * 1024]
                            .rearrange("d (t h) -> d t h", h=32)).then_inc(OG[ttE % 3].h, 16)
                        OG[ttE % 3].n += 16

                if taps:
                    g.wait_ge(SQ.h, tap_xnT_sq)
                for tt in range(NT):
                    if tt == 1:
                        g.wait_ge(SCA.h, p4_sca_end)
                    if tt >= 3:
                        g.wait_ge(OD[tt % 3].h, od_mark(tt - 3))
                        g.wait_ge(OG[tt % 3].h, od_mark(tt - 3))
                    for hf in range(2):
                        g.dma_start(
                            out=xasm[tt % 3][:, hf * 2048:(hf + 1) * 2048]
                            .rearrange("d (t h) -> d t h", h=32),
                            in_=x_in[tt * P + hf * 64: tt * P + (hf + 1) * 64, :]
                            .rearrange("t (d h) -> d t h", h=32)).then_inc(XS[tt % 3].h, 16)
                        XS[tt % 3].n += 16
                    if tt >= 2:
                        emit_out1(tt - 2)
                emit_out1(NT - 2)
                emit_out1(NT - 1)

            @block.tensor
            def _(t):
                t.wait_ge(SCA.h, p4_sca_end)
                for tt in range(NT):
                    for ck in range(4):
                        cki = tt * 4 + ck
                        t.wait_ge(QK[cki % 2].h, qk_mark(cki))
                        for g in range(8):
                            gidx = cki * 8 + g
                            if gidx >= 3:
                                pc, pg = divmod(gidx - 3, 8)
                                t.wait_ge(SCA.h, sca_exp(pc, pg))
                            for k4 in range(4):
                                ts = g * 4 + k4
                                mm = t.matmul(
                                    s_ps[gidx % 3][:, k4 * P:(k4 + 1) * P],
                                    kst[cki % 2][:, ts * P:(ts + 1) * P],
                                    qst[cki % 2][:, ts * P:(ts + 1) * P],
                                    start=True, stop=True)
                            mm.then_inc(TEN.h, 1)
                            TEN.n += 1
                        for ts in range(32):
                            sidx = ts // 16
                            vs = cki * 2 + sidx
                            if ts % 4 == 0:
                                t.wait_ge(SCA.h, sca_exp(cki, ts // 4))
                            if ts % 16 == 0 and vs >= 2:
                                t.wait_ge(VEC.h, vmark_mult[vs - 2])
                            tok = tt * P + ck * 32 + ts
                            vrep = vall[:, tok * 9:tok * 9 + 8].unsqueeze(2)\
                                .broadcast_to([P, 8, 4])
                            t.matmul(a_ps[vs % 2][:, (ts % 16) * 32:(ts % 16) * 32 + 32],
                                     ebf[cki % 3][:, ts * P:(ts + 1) * P], vrep,
                                     start=True, stop=True)
                            mmb = t.matmul(b_ps[vs % 2][:, ts % 16:ts % 16 + 1],
                                           ebf[cki % 3][:, ts * P:(ts + 1) * P],
                                           vall[:, tok * 9 + 8:tok * 9 + 9],
                                           start=True, stop=True)
                            if ts % 16 == 15:
                                mmb.then_inc(TEN.h, 1)
                                TEN.n += 1
            p5_ten_end = TEN.n

            @block.scalar
            def _(s):
                s.wait_ge(BO[0].h, 8 * 16)
                s.wait_ge(BO[1].h, 8 * 16)
                if taps:
                    s.wait_ge(SQ.h, tap_xnT_sq)
                qk_load(s, 0)
                qk_load(s, 1)
                for tt in range(NT):
                    for ck in range(4):
                        cki = tt * 4 + ck
                        for g in range(8):
                            gidx = cki * 8 + g
                            s.wait_ge(TEN.h, ten_sgrp(cki, g))
                            if g == 0 and cki >= 3:
                                s.wait_ge(TEN.h, ten_osub(cki - 3, 1))
                            s.activation(ebf[cki % 3][:, g * 512:(g + 1) * 512],
                                         s_ps[gidx % 3][:], AF.Exp).then_inc(SCA.h, 1)
                            SCA.n += 1
                        if cki + 2 < 4 * NT:
                            qk_load(s, cki + 2)
            p5_sca_end = SCA.n

            @block.vector
            def _(v):
                v.wait_ge(SCA.h, p4_sca_end)
                for tt in range(NT):
                    for sub in range(8):
                        vs = tt * 8 + sub
                        v.wait_ge(TEN.h, ten_osub(vs // 2, vs % 2))
                        v.reciprocal(rdb[:, (sub % 2) * 16:(sub % 2) * 16 + 16],
                                     b_ps[vs % 2][:]).then_inc(VEC.h, 1)
                        v.wait_ge(VEC.h, VEC.n + 1)
                        rdv = rdb[:, (sub % 2) * 16:(sub % 2) * 16 + 16]\
                            .unsqueeze(2).broadcast_to([P, 16, 32])
                        v.tensor_tensor(
                            tmpa_full[:, sub * 512:(sub + 1) * 512]
                            .rearrange("p (t h) -> p t h", h=32),
                            a_ps[vs % 2][:].rearrange("p (t h) -> p t h", h=32),
                            rdv, ALU.mult).then_inc(VEC.h, 1)
                        VEC.n += 2
                        assert VEC.n == vmark_mult[vs]
                        if sub % 2 == 1:
                            qq = sub // 2
                            lo, hi = qq * 1024, (qq + 1) * 1024
                            if qq == 0:
                                v.wait_ge(XS[tt % 3].h, xs_mark(tt))
                            v.wait_ge(VEC.h, VEC.n)
                            v.tensor_tensor(xasm[tt % 3][:, lo:hi],
                                            xasm[tt % 3][:, lo:hi],
                                            tmpa_full[:, lo:hi], ALU.add)\
                                .then_inc(VEC.h, 1)
                            VEC.n += 1
                            v.wait_ge(VEC.h, VEC.n)
                            v.tensor_tensor(xasm[tt % 3][:, lo:hi],
                                            xasm[tt % 3][:, lo:hi],
                                            ffno[:, tt * 4096 + lo: tt * 4096 + hi],
                                            ALU.add).then_inc(VEC.h, 1)
                            VEC.n += 1
                            assert VEC.n == vmark_q[(tt, qq)]

            @block.sync
            def _(sy):
                if taps:
                    sy.wait_ge(SCA.h, p5_sca_end)
                    sy.dma_start(out=tap_d["vall"], in_=vall[:]).then_inc(SQ.h, 16)
                    sy.dma_start(out=tap_d["ffn"], in_=ffno[:]).then_inc(SQ.h, 16)
                    sy.dma_start(out=tap_d["hbc"], in_=hbc[:]).then_inc(SQ.h, 16)
                    SQ.n += 3 * 16
                sy.wait_ge(SQ.h, SQ.n)
                for sem in (GQ, BS, WQ[0], WQ[1], BO[0], BO[1],
                            XD[0], XD[1], XD[2],
                            QK[0], QK[1], XS[0], XS[1], XS[2],
                            OD[0], OD[1], OD[2], OG[0], OG[1], OG[2]):
                    sy.wait_ge(sem.h, sem.n)

    return nc


def _host_layout(inputs):
    Wq = np.asarray(inputs["L0_attn_Wq"])
    Wk = np.asarray(inputs["L0_attn_Wk"])
    Wv = np.asarray(inputs["L0_attn_Wv"])
    Wf = np.asarray(inputs["L0_ffn_weight"])
    wall = np.empty((NFT, 4096, P), np.float32)
    for j in range(8):
        for r in range(4):
            wall[j * 4 + r] = Wq[:, :, 4 * j + r]
    for j in range(8):
        wall[32 + j] = Wk[:, :, j]
        wall[40 + j] = Wv[:, :, j]
    wt = Wf.reshape(P, 32, 4096)
    for h in range(32):
        wall[48 + h] = np.ascontiguousarray(wt[:, h, :].T)
    base = {
        "wall": wall,
        "bq": np.ascontiguousarray(inputs["L0_attn_bq"], np.float32),
        "bk": np.ascontiguousarray(inputs["L0_attn_bk"], np.float32),
        "bv": np.ascontiguousarray(inputs["L0_attn_bv"], np.float32),
        "fb": np.ascontiguousarray(np.asarray(inputs["L0_ffn_bias"], np.float32).reshape(P, 32)),
        "ah": np.ascontiguousarray(np.asarray(inputs["L0_ffn_lora_A"], np.float32)
                                   .transpose(2, 0, 1).reshape(CH, P, 3, 16)),
        "bh": np.ascontiguousarray(np.asarray(inputs["L0_ffn_lora_B"], np.float32)
                                   .reshape(3, P, 32, 16).transpose(0, 3, 2, 1)
                                   .reshape(3, 16, 32 * P)),
        "lsi": np.ascontiguousarray(np.asarray(inputs["L0_ffn_lora_start_indices"], np.int32)
                                    .reshape(1, 4)),
        "ident": np.eye(P, dtype=np.float32),
    }
    return base


def kernel(**inputs):
    x = np.asarray(inputs["L0_attn_x"], np.float32)
    B, Sq, E = x.shape
    xf = np.ascontiguousarray(x.reshape(B * Sq, E))
    base = _host_layout(inputs)
    nc = build(taps=False)
    in_maps = []
    for c in range(8):
        m = dict(base)
        m["x_in"] = xf[c * TPC:(c + 1) * TPC]
        m["bsel"] = np.full((1, 1), float((c * TPC) // Sq), np.float32)
        in_maps.append(m)
    res = run_bass_kernel_spmd(nc, in_maps, core_ids=list(range(8)))
    out = np.concatenate([np.asarray(r["out"]) for r in res.results], axis=0)
    return out.reshape(B, Sq, E).astype(np.float32)


# version 20
# speedup vs baseline: 1.0513x; 1.0018x over previous
from contextlib import ExitStack

import numpy as np

import concourse.bass as bass
import concourse.mybir as mybir
from concourse.bass_utils import run_bass_kernel_spmd

F32 = mybir.dt.float32
BF16 = mybir.dt.bfloat16
FP16 = mybir.dt.float16
I32 = mybir.dt.int32
AF = mybir.ActivationFunctionType
ALU = mybir.AluOpType
AX = mybir.AxisListType.X

P = 128
TPC = 1024
NT = TPC // P
CH = 32
NJ = 8
NFT = 80
QSCALE = float(np.float32(1.0) / np.float32(np.sqrt(np.float32(128.0))))
EPS = 1e-5
DEBUG_TAPS = False


class S:
    def __init__(self, ctx, nc, name):
        self.h = ctx.enter_context(nc.semaphore(name))
        self.n = 0


def build(taps=DEBUG_TAPS, phases=5):
    nc = bass.Bass("TRN2", target_bir_lowering=False, debug=False)

    x_in = nc.dram_tensor("x_in", [TPC, 4096], F32, kind="ExternalInput").ap()
    wall = nc.dram_tensor("wall", [NFT, 4096, P], F32, kind="ExternalInput").ap()
    bq_d = nc.dram_tensor("bq", [P, 32], F32, kind="ExternalInput").ap()
    bk_d = nc.dram_tensor("bk", [P, 8], F32, kind="ExternalInput").ap()
    bv_d = nc.dram_tensor("bv", [P, 8], F32, kind="ExternalInput").ap()
    fb_d = nc.dram_tensor("fb", [P, 32], F32, kind="ExternalInput").ap()
    ah_d = nc.dram_tensor("ah", [CH, P, 3, 16], F32, kind="ExternalInput").ap()
    bh_d = nc.dram_tensor("bh", [3, 16, 32 * P], F32, kind="ExternalInput").ap()
    lsi_d = nc.dram_tensor("lsi", [1, 4], I32, kind="ExternalInput").ap()
    bsel_d = nc.dram_tensor("bsel", [1, 1], F32, kind="ExternalInput").ap()
    id_d = nc.dram_tensor("ident", [P, P], F32, kind="ExternalInput").ap()
    out_d = nc.dram_tensor("out", [TPC, 4096], F32, kind="ExternalOutput").ap()
    scr_d = nc.dram_tensor("qkscr", [TPC * 16 * P], BF16, kind="Internal").ap()
    tap_d = {}
    if taps:
        tap_d["xnT"] = nc.dram_tensor("tap_xnT", [P, CH * TPC], BF16, kind="ExternalOutput").ap()
        tap_d["qsks"] = nc.dram_tensor("tap_qsks", [P, TPC * 16], BF16, kind="ExternalOutput").ap()
        tap_d["vall"] = nc.dram_tensor("tap_vall", [P, TPC * 9], BF16, kind="ExternalOutput").ap()
        tap_d["xaT"] = nc.dram_tensor("tap_xaT", [16, TPC], BF16, kind="ExternalOutput").ap()
        tap_d["ffn"] = nc.dram_tensor("tap_ffn", [P, TPC * 32], FP16, kind="ExternalOutput").ap()
        tap_d["hbc"] = nc.dram_tensor("tap_hbc", [P, 3], F32, kind="ExternalOutput").ap()
        tap_d["bselt"] = nc.dram_tensor("tap_bselt", [16, 4096], BF16, kind="ExternalOutput").ap()
        tap_d["lnst"] = nc.dram_tensor("tap_lnst", [P, 80], F32, kind="ExternalOutput").ap()

    with ExitStack() as ctx:
        def sbl(n_, s_, d_):
            return ctx.enter_context(nc.sbuf_tensor(n_, s_, d_))

        def sbr(n_, s_, d_):
            return ctx.enter_context(nc.sbuf_tensor(n_, s_, d_, side="right"))

        id_sb = sbl("id_sb", [P, P], BF16)
        bqss = sbl("bqss", [P, NJ], F32)
        bk_sb = sbl("bk_sb", [P, 8], F32)
        bv_sb = sbl("bv_sb", [P, 8], F32)
        fb_sb = sbl("fb_sb", [P, 32], F32)
        bq_sb = sbl("bq_sb", [P, 32], F32)
        hbc = sbl("hbc", [P, 3], F32)
        ones_r = sbl("ones_r", [1, P], BF16)
        h3b = sbl("h3b", [1, 3], BF16)
        lsi_sb = sbl("lsi_sb", [1, 4], I32)
        lsif = sbl("lsif", [1, 4], F32)
        cle = sbl("cle", [1, 4], F32)
        bsel_sb = sbl("bsel_sb", [1, 1], F32)
        tmp1 = sbl("tmp1", [1, 4], F32)
        epsc = sbl("epsc", [P, 1], F32)
        ps_h = ctx.enter_context(nc.psum_tensor("ps_h", [P, 3], F32))
        xasm_e = sbr("xasm_e", [P, P * 32], F32)
        xaT_guard = nc.sbuf_tensor("xaT", [16, TPC], BF16)
        xaT = xaT_guard.__enter__()
        xnT_guard = nc.sbuf_tensor("xnT", [P, CH * TPC], BF16)
        xnT = xnT_guard.__enter__()
        wsl0_guard = nc.sbuf_tensor("wsl0", [P, CH * P], BF16)
        wsl1_guard = nc.sbuf_tensor("wsl1", [P, CH * P], BF16)
        wsl = [wsl0_guard.__enter__(), wsl1_guard.__enter__()]

        GQ = S(ctx, nc, "GQ")
        BS = S(ctx, nc, "BS")
        WQ = [S(ctx, nc, "WQ0"), S(ctx, nc, "WQ1")]
        BO = [S(ctx, nc, "BO0"), S(ctx, nc, "BO1")]
        XD = [S(ctx, nc, f"XD{i}") for i in range(3)]
        QK = [S(ctx, nc, "QK0"), S(ctx, nc, "QK1")]
        OD = [S(ctx, nc, f"OD{i}") for i in range(4)]
        OG = [S(ctx, nc, f"OG{i}") for i in range(4)]
        XS = [S(ctx, nc, f"XS{i}") for i in range(4)]
        SQ = S(ctx, nc, "SQ")
        TEN = S(ctx, nc, "TEN")
        SCA = S(ctx, nc, "SCA")
        VEC = S(ctx, nc, "VEC")
        PL = S(ctx, nc, "PL")

        block = ctx.enter_context(nc.Block())

        p1ctx = ExitStack()
        xt = [p1ctx.enter_context(nc.sbuf_tensor(f"xt{i}", [P, 4096], F32)) for i in range(3)]
        sqd = [p1ctx.enter_context(nc.sbuf_tensor(f"sqd{i}", [P, 4096], BF16))
               for i in range(2)]
        xnt_ = [p1ctx.enter_context(nc.sbuf_tensor(f"xn{i}", [P, 4096], BF16)) for i in range(2)]
        lnst = p1ctx.enter_context(nc.sbuf_tensor("lnst", [P, 80], F32))
        tr_ps = [p1ctx.enter_context(nc.psum_tensor(f"tr_ps{i}", [P, 512], BF16)) for i in range(4)]

        @block.gpsimd
        def _(g):
            g.dma_start(out=id_sb[:], in_=id_d).then_inc(GQ.h, 16)
            g.dma_start(out=bq_sb[:], in_=bq_d).then_inc(GQ.h, 16)
            g.dma_start(out=bk_sb[:], in_=bk_d).then_inc(GQ.h, 16)
            g.dma_start(out=bv_sb[:], in_=bv_d).then_inc(GQ.h, 16)
            g.dma_start(out=fb_sb[:], in_=fb_d).then_inc(GQ.h, 16)
            g.dma_start(out=lsi_sb[:], in_=lsi_d).then_inc(GQ.h, 16)
            g.dma_start(out=bsel_sb[:], in_=bsel_d).then_inc(GQ.h, 16)
            GQ.n += 7 * 16
            for ft in range(2):
                g.dma_start(out=wsl[ft][:].rearrange("p (c f) -> p c f", f=P),
                            in_=wall[ft].rearrange("(c p) f -> p c f", p=P))\
                    .then_inc(WQ[ft].h, 16)
                WQ[ft].n += 16
        p0_gq = GQ.n

        @block.vector
        def _(v):
            v.memset(epsc[:], EPS).then_inc(VEC.h, 1); VEC.n += 1
            v.memset(lnst[:], 0.0).then_inc(VEC.h, 1); VEC.n += 1
        assert VEC.n == 2, VEC.n

        p1_norm_mark = [None] * NT
        p1_trev_mark = [None] * NT
        p1_tr_mark = [[None] * 8 for _ in range(NT)]
        p1_var_mark = [None] * NT
        p1_nmr_mark = [None] * NT

        if True:
            def STT(tt, i):
                return lnst[:, tt * 10 + i: tt * 10 + i + 1]

            p1_sq_mark = [None] * NT
            p1_ln_mark = [None] * NT
            p1_exp_mark = [None] * NT
            p1_hn_mark = [None] * NT
            p1_evac_mark = [[None] * 8 for _ in range(NT)]
            vec, ten = VEC.n, TEN.n
            for tt in range(NT):
                vec += 3
                vec += 2
                p1_var_mark[tt] = vec
                vec += 1
                p1_hn_mark[tt] = vec
                vec += 1
                p1_norm_mark[tt] = vec
                for b4 in range(8):
                    ten += 1
                    p1_tr_mark[tt][b4] = ten
            sca = SCA.n
            for tt in range(NT):
                sca += 1
                p1_sq_mark[tt] = sca
                if tt >= 1:
                    for b4 in range(8):
                        sca += 1
                        p1_evac_mark[tt - 1][b4] = sca
                sca += 1
                p1_ln_mark[tt] = sca
                sca += 1
                p1_exp_mark[tt] = sca
            for b4 in range(8):
                sca += 1
                p1_evac_mark[NT - 1][b4] = sca
            for tt in range(NT):
                p1_trev_mark[tt] = p1_evac_mark[tt][7]

            def xd_mark(tt):
                return 16 * (tt // 3 + 1) + (16 if tt % 3 == 0 else 0)

            @block.sync
            def _(sy):
                for tt in range(NT):
                    if tt >= 3:
                        sy.wait_ge(VEC.h, p1_norm_mark[tt - 3])
                        sy.wait_ge(SCA.h, p1_sq_mark[tt - 3])
                    if tt == 0:
                        sy.dma_start(out=xt[0][:, 0:2048],
                                     in_=x_in[0:P, 0:2048]).then_inc(XD[0].h, 16)
                        XD[0].n += 16
                    else:
                        sy.dma_start(out=xt[tt % 3][:],
                                     in_=x_in[tt * P:(tt + 1) * P, :]).then_inc(XD[tt % 3].h, 16)
                        XD[tt % 3].n += 16

            @block.vector
            def _(v):
                v.wait_ge(VEC.h, VEC.n)
                for tt in range(NT):
                    v.wait_ge(XD[tt % 3].h, xd_mark(tt))
                    v.reduce_sum(STT(tt, 0), xt[tt % 3][:], axis=AX).then_inc(VEC.h, 1)
                    VEC.n += 1
                    v.wait_ge(VEC.h, VEC.n)
                    v.tensor_scalar_mul(STT(tt, 1), STT(tt, 0), -1.0 / 4096.0).then_inc(VEC.h, 1)
                    VEC.n += 1
                    v.wait_ge(VEC.h, VEC.n)
                    v.tensor_tensor(STT(tt, 2), STT(tt, 1), STT(tt, 1), ALU.mult).then_inc(VEC.h, 1)
                    VEC.n += 1
                    v.wait_ge(SCA.h, p1_sq_mark[tt])
                    v.tensor_scalar_mul(STT(tt, 3), STT(tt, 7), 1.0 / 4096.0).then_inc(VEC.h, 1)
                    VEC.n += 1
                    v.wait_ge(VEC.h, VEC.n)
                    v.tensor_tensor(STT(tt, 3), STT(tt, 3), STT(tt, 2), ALU.subtract).then_inc(VEC.h, 1)
                    VEC.n += 1
                    assert VEC.n == p1_var_mark[tt]
                    v.wait_ge(SCA.h, p1_ln_mark[tt])
                    v.tensor_scalar_mul(STT(tt, 8), STT(tt, 4), -0.5).then_inc(VEC.h, 1)
                    VEC.n += 1
                    assert VEC.n == p1_hn_mark[tt]
                    v.wait_ge(SCA.h, p1_exp_mark[tt])
                    if tt >= 2:
                        v.wait_ge(TEN.h, p1_tr_mark[tt - 2][7])
                    v.tensor_scalar(xnt_[tt % 2][:], xt[tt % 3][:], STT(tt, 1), STT(tt, 5),
                                    ALU.add, ALU.mult).then_inc(VEC.h, 1)
                    VEC.n += 1
                    assert VEC.n == p1_norm_mark[tt]

            def p1_emit_evacs(s, te):
                pair = 2 * (te % 2)
                for b4 in range(8):
                    s.wait_ge(TEN.h, p1_tr_mark[te][b4])
                    dst = xnT[:].rearrange("p (c t) -> p c t", t=TPC)[
                        :, b4 * 4:(b4 + 1) * 4, te * P:(te + 1) * P]
                    s.activation(dst, tr_ps[pair + b4 % 2][:].rearrange("p (c t) -> p c t", t=P),
                                 AF.Copy).then_inc(SCA.h, 1)
                    SCA.n += 1
                    assert SCA.n == p1_evac_mark[te][b4]

            @block.scalar
            def _(s):
                s.dma_start(out=xt[0][:, 2048:4096],
                            in_=x_in[0:P, 2048:4096]).then_inc(XD[0].h, 16)
                XD[0].n += 16
                s.wait_ge(VEC.h, 2)
                for tt in range(NT):
                    s.wait_ge(XD[tt % 3].h, xd_mark(tt))
                    s.activation(sqd[tt % 2][:], xt[tt % 3][:], AF.Square,
                                 accum_out=STT(tt, 7)).then_inc(SCA.h, 1)
                    SCA.n += 1
                    assert SCA.n == p1_sq_mark[tt]
                    if tt >= 1:
                        p1_emit_evacs(s, tt - 1)
                    s.wait_ge(VEC.h, p1_var_mark[tt])
                    s.activation(STT(tt, 4), STT(tt, 3), AF.Ln, bias=epsc[:, 0:1]).then_inc(SCA.h, 1)
                    SCA.n += 1
                    assert SCA.n == p1_ln_mark[tt]
                    s.wait_ge(VEC.h, p1_hn_mark[tt])
                    s.activation(STT(tt, 5), STT(tt, 8), AF.Exp).then_inc(SCA.h, 1)
                    SCA.n += 1
                    assert SCA.n == p1_exp_mark[tt]
                p1_emit_evacs(s, NT - 1)

            @block.tensor
            def _(t):
                t.wait_ge(GQ.h, p0_gq)
                for tt in range(NT):
                    pair = 2 * (tt % 2)
                    t.wait_ge(VEC.h, p1_norm_mark[tt])
                    if tt >= 2:
                        t.wait_ge(SCA.h, p1_evac_mark[tt - 2][7])
                    for b4 in range(8):
                        if b4 >= 2:
                            t.wait_ge(SCA.h, p1_evac_mark[tt][b4 - 2])
                        for c in range(4):
                            cc = b4 * 4 + c
                            tr = t.transpose(tr_ps[pair + b4 % 2][:, c * P:(c + 1) * P],
                                             xnt_[tt % 2][:, cc * P:(cc + 1) * P], id_sb[:])
                        tr.then_inc(TEN.h, 1)
                        TEN.n += 1
                        assert TEN.n == p1_tr_mark[tt][b4]
            if taps:
                @block.sync
                def _(sy):
                    sy.wait_ge(SCA.h, p1_trev_mark[NT - 1])
                    sy.dma_start(out=tap_d["lnst"], in_=lnst[:]).then_inc(SQ.h, 16)
                    SQ.n += 16
            tap_lnst_sq = SQ.n
            p1_ten_end, p1_sca_end = TEN.n, SCA.n
        p1ctx.close()
        vall = sbr("vall", [P, TPC * 9], BF16)

        aselt_guard = nc.sbuf_tensor("aselt", [P, CH * 16], BF16)
        aselt = aselt_guard.__enter__()
        bselt_guard = nc.sbuf_tensor("bselt", [16, 32 * P], BF16)
        bselt = bselt_guard.__enter__()
        p0bctx = ExitStack()
        a_all = p0bctx.enter_context(nc.sbuf_tensor("a_all", [P, CH * 48], BF16, side="right"))
        bsl = [p0bctx.enter_context(nc.sbuf_tensor(f"bsl{i}", [16, 32 * P], BF16, side="right"))
               for i in range(3)]
        p0b_vec0 = 2 + NT * 7
        p0b_mask_mark = p0b_vec0 + 12
        p0_vec = p0b_mask_mark + 6
        p0_b_vec_mark = [p0b_mask_mark + 3 + (a + 1) for a in range(3)]
        p0b_gq = p0_gq + 16

        @block.vector
        def _(v):
            assert VEC.n == p0b_vec0, VEC.n
            v.wait_ge(GQ.h, p0_gq)
            v.tensor_copy(lsif[:], lsi_sb[:]).then_inc(VEC.h, 1); VEC.n += 1
            v.wait_ge(VEC.h, VEC.n)
            v.tensor_scalar(cle[:], lsif[:], bsel_sb[0:1, 0:1], None, ALU.is_le)\
                .then_inc(VEC.h, 1); VEC.n += 1
            v.wait_ge(VEC.h, VEC.n)
            v.tensor_tensor(h3b[0:1, 2:3], cle[0:1, 2:3], cle[0:1, 3:4], ALU.mult)\
                .then_inc(VEC.h, 1); VEC.n += 1
            v.wait_ge(VEC.h, VEC.n)
            v.tensor_tensor(h3b[0:1, 1:2], cle[0:1, 2:3], h3b[0:1, 2:3], ALU.subtract)\
                .then_inc(VEC.h, 1); VEC.n += 1
            v.tensor_tensor(tmp1[0:1, 0:1], cle[0:1, 1:2], cle[0:1, 2:3], ALU.mult)\
                .then_inc(VEC.h, 1); VEC.n += 1
            v.wait_ge(VEC.h, VEC.n)
            v.tensor_tensor(h3b[0:1, 0:1], cle[0:1, 1:2], tmp1[0:1, 0:1], ALU.subtract)\
                .then_inc(VEC.h, 1); VEC.n += 1
            v.memset(ones_r[:], 1.0).then_inc(VEC.h, 1); VEC.n += 1
            bqv = bq_sb[:].rearrange("p (j r) -> p j r", r=4)
            v.tensor_tensor(bqss[:], bqv[:, :, 0], bqv[:, :, 1], ALU.add)\
                .then_inc(VEC.h, 1); VEC.n += 1
            v.wait_ge(VEC.h, VEC.n)
            v.tensor_tensor(bqss[:], bqss[:], bqv[:, :, 2], ALU.add)\
                .then_inc(VEC.h, 1); VEC.n += 1
            v.wait_ge(VEC.h, VEC.n)
            v.tensor_tensor(bqss[:], bqss[:], bqv[:, :, 3], ALU.add)\
                .then_inc(VEC.h, 1); VEC.n += 1
            v.wait_ge(VEC.h, VEC.n)
            v.tensor_scalar_mul(bqss[:], bqss[:], QSCALE).then_inc(VEC.h, 1); VEC.n += 1
            v.memset(vall[:], 1.0).then_inc(VEC.h, 1); VEC.n += 1
            assert VEC.n == p0b_mask_mark, VEC.n

        p2_ten0, p2_sca0 = TEN.n, SCA.n
        NQKV = 48

        def wq_mark(ft):
            return 16 * (ft // 2 + 1)

        if phases < 2:
            @block.sync
            def _(sy):
                if taps:
                    sy.wait_ge(SCA.h, p1_trev_mark[NT - 1])
                    sy.dma_start(out=tap_d["xnT"], in_=xnT[:]).then_inc(SQ.h, 16)
                    SQ.n += 16
                sy.wait_ge(SQ.h, SQ.n)
                for sem in (GQ, BS, XD[0], XD[1]):
                    sy.wait_ge(sem.h, sem.n)
            p0bctx.close()
            bselt_guard.__exit__(None, None, None)
            aselt_guard.__exit__(None, None, None)
            wsl1_guard.__exit__(None, None, None)
            wsl0_guard.__exit__(None, None, None)
            xnT_guard.__exit__(None, None, None)
            xaT_guard.__exit__(None, None, None)
            return nc
        def p2tm(ft):
            return p2_ten0 + ft + 1 + (1 if ft >= 3 else 0)

        @block.gpsimd
        def _(g):
            for ft in range(2, NQKV):
                if ft >= 2:
                    g.wait_ge(TEN.h, p2tm(ft - 2))
                g.dma_start(out=wsl[ft % 2][:].rearrange("p (c f) -> p c f", f=P),
                            in_=wall[ft].rearrange("(c p) f -> p c f", p=P))\
                    .then_inc(WQ[ft % 2].h, 16)
                WQ[ft % 2].n += 16
                if ft == 2:
                    g.wait_ge(SCA.h, p1_trev_mark[NT - 1])
                    if taps:
                        g.wait_ge(SQ.h, tap_lnst_sq)
                    g.dma_start(out=a_all[:].rearrange("p (c a r) -> p c a r", c=CH, a=3),
                                in_=ah_d.rearrange("c p a r -> p c a r")).then_inc(GQ.h, 16)
                    GQ.n += 16
                    for a in range(3):
                        g.dma_start(out=bsl[a][:], in_=bh_d[a]).then_inc(BS.h, 16)
                    BS.n += 3 * 16

        with nc.sbuf_tensor("qsks", [P, TPC * 16], BF16, side="right") as qsks:
            with nc.psum_tensor("qk_ps0", [P, 512], F32) as qk_ps0, \
                 nc.psum_tensor("qk_ps1", [P, 512], F32) as qk_ps1, \
                 nc.psum_tensor("qk_ps2", [P, 512], F32) as qk_ps2, \
                 nc.psum_tensor("qk_ps3", [P, 512], F32) as qk_ps3, \
                 nc.psum_tensor("xa_ps0", [16, 512], F32) as xa_ps0, \
                 nc.psum_tensor("xa_ps1", [16, 512], F32) as xa_ps1:
                qk_ps = [qk_ps0, qk_ps1, qk_ps2, qk_ps3]
                xa_ps = [xa_ps0, xa_ps1]

                @block.tensor
                def _(t):
                    t.wait_ge(SCA.h, p1_trev_mark[NT - 1])
                    for ft in range(NQKV):
                        t.wait_ge(WQ[ft % 2].h, wq_mark(ft))
                        if ft < 32:
                            j, r = divmod(ft, 4)
                            first, last = (r == 0), (r == 3)
                            bank = 2 * (j % 2)
                            if first and j >= 2:
                                t.wait_ge(SCA.h, p2_sca0 + 1 + 2 * (j - 1))
                        else:
                            first = last = True
                            bank = 2 * (ft % 2)
                            if ft == 32:
                                t.wait_ge(SCA.h, p2_sca0 + 15)
                            elif ft == 33:
                                t.wait_ge(SCA.h, p2_sca0 + 17)
                            else:
                                t.wait_ge(SCA.h, p2_sca0 + 17 + 2 * (ft - 33))
                        for c in range(CH):
                            for half in range(2):
                                mm = t.matmul(
                                    qk_ps[bank + half][:],
                                    wsl[ft % 2][:, c * P:(c + 1) * P],
                                    xnT[:, c * TPC + half * 512: c * TPC + (half + 1) * 512],
                                    start=(first and c == 0), stop=(last and c == CH - 1))
                        mm.then_inc(TEN.h, 1)
                        TEN.n += 1
                        if ft == 2:
                            t.wait_ge(VEC.h, p0b_mask_mark)
                            t.matmul(ps_h[:], ones_r[:], h3b[:], start=True, stop=True)\
                                .then_inc(TEN.h, 1)
                            TEN.n += 1
                    t.wait_ge(VEC.h, p0_vec)
                    for c in range(CH):
                        for half in range(2):
                            mm = t.matmul(xa_ps[half][:],
                                          aselt[:, c * 16:(c + 1) * 16],
                                          xnT[:, c * TPC + half * 512: c * TPC + (half + 1) * 512],
                                          start=(c == 0), stop=(c == CH - 1))
                    mm.then_inc(TEN.h, 1)
                    TEN.n += 1
                p2_ten_end = TEN.n
                p0_ten = p2_ten0 + 4
                assert p2_ten_end == p2_ten0 + NQKV + 2

                @block.scalar
                def _(s):
                    qv = qsks[:].rearrange("p (t cc) -> p t cc", cc=16)
                    vv = vall[:].rearrange("p (t cc) -> p t cc", cc=9)
                    s.wait_ge(TEN.h, p0_ten)
                    s.activation(hbc[:], ps_h[:], AF.Copy).then_inc(SCA.h, 1)
                    SCA.n += 1
                    for j in range(NJ):
                        s.wait_ge(TEN.h, p2tm(4 * j + 3))
                        if j == 0:
                            s.wait_ge(VEC.h, p0b_mask_mark)
                        bank = 2 * (j % 2)
                        for half in range(2):
                            dst = qv[:, half * 512:(half + 1) * 512, j]
                            s.activation(dst, qk_ps[bank + half][:], AF.Identity,
                                         bias=bqss[:, j:j + 1], scale=QSCALE)\
                                .then_inc(SCA.h, 1)
                            SCA.n += 1
                    for ft in range(32, NQKV):
                        s.wait_ge(TEN.h, p2tm(ft))
                        kv = ft - 32
                        j = kv % 8
                        bank = 2 * (ft % 2)
                        for half in range(2):
                            if kv < 8:
                                dst = qv[:, half * 512:(half + 1) * 512, 8 + j]
                                bias = bk_sb[:, j:j + 1]
                            else:
                                dst = vv[:, half * 512:(half + 1) * 512, j]
                                bias = bv_sb[:, j:j + 1]
                            s.activation(dst, qk_ps[bank + half][:], AF.Identity,
                                         bias=bias).then_inc(SCA.h, 1)
                            SCA.n += 1
                    s.wait_ge(TEN.h, p2_ten_end)
                    for half in range(2):
                        s.activation(xaT[:, half * 512:(half + 1) * 512],
                                     xa_ps[half][:], AF.Copy).then_inc(SCA.h, 1)
                        SCA.n += 1
                p2_sca_end = SCA.n
                p0_sca = p2_sca0 + 1
                assert p2_sca_end == p2_sca0 + 51

                @block.vector
                def _(v):
                    v.wait_ge(SCA.h, p0_sca)
                    v.wait_ge(GQ.h, p0b_gq)
                    av = a_all[:].rearrange("p (c a r) -> p c a r", c=CH, a=3)
                    sv = aselt[:].rearrange("p (c r) -> p c r", c=CH)
                    v.tensor_scalar(sv, av[:, :, 0, :], hbc[:, 0:1], None, ALU.mult)\
                        .then_inc(VEC.h, 1); VEC.n += 1
                    for a in (1, 2):
                        v.wait_ge(VEC.h, VEC.n)
                        v.scalar_tensor_tensor(sv, av[:, :, a, :], hbc[:, a:a + 1], sv,
                                               ALU.mult, ALU.add).then_inc(VEC.h, 1); VEC.n += 1
                    for a in range(3):
                        v.wait_ge(BS.h, 48)
                        if a == 0:
                            v.tensor_scalar(bselt[:], bsl[a][:], hbc[0:16, 0:1], None, ALU.mult)\
                                .then_inc(VEC.h, 1); VEC.n += 1
                        else:
                            v.wait_ge(VEC.h, VEC.n)
                            v.scalar_tensor_tensor(bselt[:], bsl[a][:], hbc[0:16, a:a + 1],
                                                   bselt[:], ALU.mult, ALU.add)\
                                .then_inc(VEC.h, 1); VEC.n += 1
                        assert p0_b_vec_mark[a] == VEC.n, (a, p0_b_vec_mark[a], VEC.n)
                assert VEC.n == p0_vec, (VEC.n, p0_vec)

            p3_ten0, p3_sca0 = TEN.n, SCA.n
            NB = 32
            with nc.sbuf_tensor("tsb0", [P, 1024], BF16) as tsb0, \
                 nc.sbuf_tensor("tsb1", [P, 1024], BF16) as tsb1, \
                 nc.psum_tensor("p3_ps0", [P, 512], BF16) as p3_ps0, \
                 nc.psum_tensor("p3_ps1", [P, 512], BF16) as p3_ps1:
                tsb = [tsb0, tsb1]
                p3_ps = [p3_ps0, p3_ps1]

                @block.tensor
                def _(t):
                    t.wait_ge(SCA.h, p2_sca_end)
                    for b in range(NB):
                        if b >= 2:
                            t.wait_ge(SCA.h, p3_sca0 + b - 1)
                        for g4 in range(4):
                            col = (b * 4 + g4) * P
                            tr = t.transpose(p3_ps[b % 2][:, g4 * P:(g4 + 1) * P],
                                             qsks[:, col:col + P], id_sb[:])
                        tr.then_inc(TEN.h, 1)
                        TEN.n += 1
                p3_ten_end = TEN.n

                @block.scalar
                def _(s):
                    if taps:
                        s.wait_ge(SQ.h, tap_lnst_sq)
                    for b in range(NB):
                        s.wait_ge(TEN.h, p3_ten0 + b + 1)
                        if b >= 4:
                            bbp = b // 2 - 2
                            s.wait_ge(BO[bbp % 2].h, 16 * (bbp // 2 + 1))
                        s.activation(tsb[(b // 2) % 2][:, (b % 2) * 512:(b % 2 + 1) * 512],
                                     p3_ps[b % 2][:], AF.Copy).then_inc(SCA.h, 1)
                        SCA.n += 1
                p3_sca_end = SCA.n

                @block.sync
                def _(sy):
                    scr4 = scr_d.rearrange("(bb gg p d) -> bb p gg d", bb=16, gg=8, p=P)
                    for bb in range(16):
                        sy.wait_ge(SCA.h, p3_sca0 + 2 * bb + 2)
                        sy.dma_start(out=scr4[bb],
                                    in_=tsb[bb % 2][:].rearrange("p (gg d) -> p gg d", gg=8))\
                            .then_inc(BO[bb % 2].h, 16)
                        BO[bb % 2].n += 16

                if taps:
                    @block.sync
                    def _(sy):
                        sy.wait_ge(TEN.h, p3_ten_end)
                        sy.dma_start(out=tap_d["qsks"], in_=qsks[:]).then_inc(SQ.h, 16)
                        SQ.n += 16
                tap_qsks_sq = SQ.n

        p0bctx.close()

        ffno = ctx.enter_context(nc.sbuf_tensor("ffno", [P, TPC * 32], FP16, side="right"))
        p4_ten0, p4_sca0 = TEN.n, SCA.n
        with nc.psum_tensor("f_ps0", [P, 512], F32) as f_ps0, \
             nc.psum_tensor("f_ps1", [P, 512], F32) as f_ps1, \
             nc.psum_tensor("f_ps2", [P, 512], F32) as f_ps2, \
             nc.psum_tensor("f_ps3", [P, 512], F32) as f_ps3:
            f_ps = [f_ps0, f_ps1, f_ps2, f_ps3]

            @block.gpsimd
            def _(g):
                for hh in range(32):
                    ft = NQKV + hh
                    if hh == 0:
                        g.wait_ge(TEN.h, p2tm(46))
                    elif hh == 1:
                        g.wait_ge(TEN.h, p2tm(47))
                    else:
                        g.wait_ge(TEN.h, p4_ten0 + hh - 1)
                    g.dma_start(out=wsl[ft % 2][:].rearrange("p (c f) -> p c f", f=P),
                                in_=wall[ft].rearrange("(c p) f -> p c f", p=P))\
                        .then_inc(WQ[ft % 2].h, 16)
                    WQ[ft % 2].n += 16

            @block.tensor
            def _(t):
                t.wait_ge(SCA.h, p3_sca_end)
                t.wait_ge(VEC.h, p0_vec)
                for hh in range(32):
                    ft = NQKV + hh
                    t.wait_ge(WQ[ft % 2].h, wq_mark(ft))
                    bank = 2 * (hh % 2)
                    if hh >= 2:
                        t.wait_ge(SCA.h, p4_sca0 + 2 * (hh - 1))
                    for c in range(CH):
                        for half in range(2):
                            t.matmul(f_ps[bank + half][:],
                                     wsl[ft % 2][:, c * P:(c + 1) * P],
                                     xnT[:, c * TPC + half * 512: c * TPC + (half + 1) * 512],
                                     start=(c == 0), stop=False)
                    for half in range(2):
                        mm = t.matmul(f_ps[bank + half][:],
                                      bselt[:, hh * P:(hh + 1) * P],
                                      xaT[:, half * 512:(half + 1) * 512],
                                      start=False, stop=True)
                    mm.then_inc(TEN.h, 1)
                    TEN.n += 1
            p4_ten_end = TEN.n

            if taps:
                @block.sync
                def _(sy):
                    sy.wait_ge(TEN.h, p4_ten_end)
                    sy.dma_start(out=tap_d["xnT"], in_=xnT[:]).then_inc(SQ.h, 16)
                    sy.dma_start(out=tap_d["bselt"], in_=bselt[:]).then_inc(SQ.h, 16)
                    sy.dma_start(out=tap_d["xaT"], in_=xaT[:]).then_inc(SQ.h, 16)
                    SQ.n += 3 * 16
            tap_xnT_sq = SQ.n

            @block.scalar
            def _(s):
                if taps:
                    s.wait_ge(SQ.h, tap_qsks_sq)
                fv = ffno[:].rearrange("p (t h) -> p t h", h=32)
                for hh in range(32):
                    s.wait_ge(TEN.h, p4_ten0 + hh + 1)
                    bank = 2 * (hh % 2)
                    for half in range(2):
                        s.activation(fv[:, half * 512:(half + 1) * 512, hh],
                                     f_ps[bank + half][:], AF.Identity,
                                     bias=fb_sb[:, hh:hh + 1]).then_inc(SCA.h, 1)
                        SCA.n += 1
            p4_sca_end = SCA.n

        bselt_guard.__exit__(None, None, None)
        aselt_guard.__exit__(None, None, None)
        wsl1_guard.__exit__(None, None, None)
        wsl0_guard.__exit__(None, None, None)
        xnT_guard.__exit__(None, None, None)
        xaT_guard.__exit__(None, None, None)

        p5_ten0, p5_sca0, p5_vec0 = TEN.n, SCA.n, VEC.n
        ten_sgrp = lambda ck, g: p5_ten0 + ck * 10 + g + 1
        ten_schunk = lambda ck: p5_ten0 + ck * 10 + 8
        ten_osub = lambda ck, sidx: p5_ten0 + ck * 10 + 8 + sidx + 1
        sca_exp = lambda ck, g: p5_sca0 + ck * 8 + g + 1

        vmark_mult = {}
        vmark_q = {}
        vmark_tile = {}
        nv = p5_vec0
        for tt in range(NT):
            for sub in range(8):
                nv += 2
                vmark_mult[tt * 8 + sub] = nv
                if sub % 2 == 1:
                    nv += 2
                    vmark_q[(tt, sub // 2)] = nv
            vmark_tile[tt] = nv
        p5_vec_end = nv

        def qk_mark(cki):
            return 32 * (cki // 2 + 1)

        def xs_mark(tt):
            return 32 * (tt // 4 + 1)

        def od_mark(tt):
            return 32 * (tt // 4 + 1)

        with ExitStack() as p5ctx:
            def p5sb(n_, s_):
                return p5ctx.enter_context(nc.sbuf_tensor(n_, s_, BF16, side="right"))
            qst = [p5sb(f"qst{i}", [NJ, 32 * P]) for i in range(2)]
            kst = [p5sb(f"kst{i}", [NJ, 32 * P]) for i in range(2)]
            ebf = [p5sb(f"ebf{i}", [P, 32 * P]) for i in range(2)]
            xasm = [xasm_e] + [
                p5ctx.enter_context(nc.sbuf_tensor(f"xasm{i}", [P, P * 32], F32, side="right"))
                for i in range(1, 4)]
            tmpa_full = p5ctx.enter_context(
                nc.sbuf_tensor("tmpa_full", [P, 2048], F32, side="right"))

            def tmpa_slice(sub, n=512):
                base = ((sub // 2) % 2) * 1024 + (sub % 2) * 512
                return tmpa_full[:, base:base + n]
            rdb = p5ctx.enter_context(nc.sbuf_tensor("rdb", [P, 32], F32, side="right"))
            s_ps = [p5ctx.enter_context(nc.psum_tensor(f"s_ps{i}", [P, 512], F32)) for i in range(3)]
            a_ps = [p5ctx.enter_context(nc.psum_tensor(f"a_ps{i}", [P, 512], F32)) for i in range(2)]
            b_ps = [p5ctx.enter_context(nc.psum_tensor(f"b_ps{i}", [P, 16], F32)) for i in range(2)]
            scr5 = scr_d.rearrange("(t cc d) -> cc t d", t=TPC, cc=16)

            def qk_load(eng, cki):
                t0 = (cki // 4) * P + (cki % 4) * 32
                eng.dma_start(out=qst[cki % 2][:].rearrange("j (t d) -> j t d", t=32),
                              in_=scr5[0:8, t0:t0 + 32, :]).then_inc(QK[cki % 2].h, 16)
                eng.dma_start(out=kst[cki % 2][:].rearrange("j (t d) -> j t d", t=32),
                              in_=scr5[8:16, t0:t0 + 32, :]).then_inc(QK[cki % 2].h, 16)
                QK[cki % 2].n += 32

            @block.sync
            def _(sy):
                sy.wait_ge(SCA.h, p4_sca_end)
                if taps:
                    sy.wait_ge(SQ.h, tap_xnT_sq)

                def emit_out(tt, qq):
                    sy.wait_ge(VEC.h, vmark_q[(tt, qq)])
                    sy.dma_start(
                        out=out_d[tt * P + qq * 32: tt * P + (qq + 1) * 32, :]
                        .rearrange("t (d h) -> d t h", h=32),
                        in_=xasm[tt % 4][:, qq * 1024:(qq + 1) * 1024]
                        .rearrange("d (t h) -> d t h", h=32)).then_inc(OD[tt % 4].h, 16)
                    OD[tt % 4].n += 16

                for tt in range(NT):
                    emit_out(tt, 0)
                    emit_out(tt, 2)

            @block.gpsimd
            def _(g):
                def emit_out1(ttE):
                    for qq in (1, 3):
                        g.wait_ge(VEC.h, vmark_q[(ttE, qq)])
                        g.dma_start(
                            out=out_d[ttE * P + qq * 32: ttE * P + (qq + 1) * 32, :]
                            .rearrange("t (d h) -> d t h", h=32),
                            in_=xasm[ttE % 4][:, qq * 1024:(qq + 1) * 1024]
                            .rearrange("d (t h) -> d t h", h=32)).then_inc(OG[ttE % 4].h, 16)
                        OG[ttE % 4].n += 16

                if taps:
                    g.wait_ge(SQ.h, tap_xnT_sq)
                for tt in range(NT):
                    if tt == 1:
                        g.wait_ge(SCA.h, p4_sca_end)
                    if tt >= 4:
                        g.wait_ge(OD[tt % 4].h, od_mark(tt - 4))
                        g.wait_ge(OG[tt % 4].h, od_mark(tt - 4))
                    for hf in range(2):
                        g.dma_start(
                            out=xasm[tt % 4][:, hf * 2048:(hf + 1) * 2048]
                            .rearrange("d (t h) -> d t h", h=32),
                            in_=x_in[tt * P + hf * 64: tt * P + (hf + 1) * 64, :]
                            .rearrange("t (d h) -> d t h", h=32)).then_inc(XS[tt % 4].h, 16)
                        XS[tt % 4].n += 16
                    if tt >= 2:
                        emit_out1(tt - 2)
                emit_out1(NT - 2)
                emit_out1(NT - 1)

            @block.tensor
            def _(t):
                t.wait_ge(SCA.h, p4_sca_end)
                for tt in range(NT):
                    for ck in range(4):
                        cki = tt * 4 + ck
                        t.wait_ge(QK[cki % 2].h, qk_mark(cki))
                        for g in range(8):
                            gidx = cki * 8 + g
                            if gidx >= 3:
                                pc, pg = divmod(gidx - 3, 8)
                                t.wait_ge(SCA.h, sca_exp(pc, pg))
                            for k4 in range(4):
                                ts = g * 4 + k4
                                mm = t.matmul(
                                    s_ps[gidx % 3][:, k4 * P:(k4 + 1) * P],
                                    kst[cki % 2][:, ts * P:(ts + 1) * P],
                                    qst[cki % 2][:, ts * P:(ts + 1) * P],
                                    start=True, stop=True)
                            mm.then_inc(TEN.h, 1)
                            TEN.n += 1
                        for ts in range(32):
                            sidx = ts // 16
                            vs = cki * 2 + sidx
                            if ts % 4 == 0:
                                t.wait_ge(SCA.h, sca_exp(cki, ts // 4))
                            if ts % 16 == 0 and vs >= 2:
                                t.wait_ge(VEC.h, vmark_mult[vs - 2])
                            tok = tt * P + ck * 32 + ts
                            vrep = vall[:, tok * 9:tok * 9 + 8].unsqueeze(2)\
                                .broadcast_to([P, 8, 4])
                            t.matmul(a_ps[vs % 2][:, (ts % 16) * 32:(ts % 16) * 32 + 32],
                                     ebf[cki % 2][:, ts * P:(ts + 1) * P], vrep,
                                     start=True, stop=True)
                            mmb = t.matmul(b_ps[vs % 2][:, ts % 16:ts % 16 + 1],
                                           ebf[cki % 2][:, ts * P:(ts + 1) * P],
                                           vall[:, tok * 9 + 8:tok * 9 + 9],
                                           start=True, stop=True)
                            if ts % 16 == 15:
                                mmb.then_inc(TEN.h, 1)
                                TEN.n += 1
            p5_ten_end = TEN.n

            @block.scalar
            def _(s):
                s.wait_ge(BO[0].h, 8 * 16)
                s.wait_ge(BO[1].h, 8 * 16)
                if taps:
                    s.wait_ge(SQ.h, tap_xnT_sq)
                qk_load(s, 0)
                qk_load(s, 1)
                for tt in range(NT):
                    for ck in range(4):
                        cki = tt * 4 + ck
                        for g in range(8):
                            gidx = cki * 8 + g
                            s.wait_ge(TEN.h, ten_sgrp(cki, g))
                            if g == 0 and cki >= 2:
                                s.wait_ge(TEN.h, ten_osub(cki - 2, 1))
                            s.activation(ebf[cki % 2][:, g * 512:(g + 1) * 512],
                                         s_ps[gidx % 3][:], AF.Exp).then_inc(SCA.h, 1)
                            SCA.n += 1
                        if cki + 2 < 4 * NT:
                            qk_load(s, cki + 2)
            p5_sca_end = SCA.n

            @block.vector
            def _(v):
                v.wait_ge(SCA.h, p4_sca_end)
                for tt in range(NT):
                    for sub in range(8):
                        vs = tt * 8 + sub
                        v.wait_ge(TEN.h, ten_osub(vs // 2, vs % 2))
                        v.reciprocal(rdb[:, (sub % 2) * 16:(sub % 2) * 16 + 16],
                                     b_ps[vs % 2][:]).then_inc(VEC.h, 1)
                        v.wait_ge(VEC.h, VEC.n + 1)
                        rdv = rdb[:, (sub % 2) * 16:(sub % 2) * 16 + 16]\
                            .unsqueeze(2).broadcast_to([P, 16, 32])
                        v.tensor_tensor(
                            tmpa_slice(sub).rearrange("p (t h) -> p t h", h=32),
                            a_ps[vs % 2][:].rearrange("p (t h) -> p t h", h=32),
                            rdv, ALU.mult).then_inc(VEC.h, 1)
                        VEC.n += 2
                        assert VEC.n == vmark_mult[vs]
                        if sub % 2 == 1:
                            qq = sub // 2
                            lo, hi = qq * 1024, (qq + 1) * 1024
                            if qq == 0:
                                v.wait_ge(XS[tt % 4].h, xs_mark(tt))
                            v.wait_ge(VEC.h, VEC.n)
                            v.tensor_tensor(xasm[tt % 4][:, lo:hi],
                                            xasm[tt % 4][:, lo:hi],
                                            tmpa_slice(2 * qq, 1024), ALU.add)\
                                .then_inc(VEC.h, 1)
                            VEC.n += 1
                            v.wait_ge(VEC.h, VEC.n)
                            v.tensor_tensor(xasm[tt % 4][:, lo:hi],
                                            xasm[tt % 4][:, lo:hi],
                                            ffno[:, tt * 4096 + lo: tt * 4096 + hi],
                                            ALU.add).then_inc(VEC.h, 1)
                            VEC.n += 1
                            assert VEC.n == vmark_q[(tt, qq)]

            @block.sync
            def _(sy):
                if taps:
                    sy.wait_ge(SCA.h, p5_sca_end)
                    sy.dma_start(out=tap_d["vall"], in_=vall[:]).then_inc(SQ.h, 16)
                    sy.dma_start(out=tap_d["ffn"], in_=ffno[:]).then_inc(SQ.h, 16)
                    sy.dma_start(out=tap_d["hbc"], in_=hbc[:]).then_inc(SQ.h, 16)
                    SQ.n += 3 * 16
                sy.wait_ge(SQ.h, SQ.n)
                for sem in (GQ, BS, WQ[0], WQ[1], BO[0], BO[1],
                            XD[0], XD[1], XD[2],
                            QK[0], QK[1], XS[0], XS[1], XS[2],
                            OD[0], OD[1], OD[2], OG[0], OG[1], OG[2]):
                    sy.wait_ge(sem.h, sem.n)

    return nc


def _host_layout(inputs):
    Wq = np.asarray(inputs["L0_attn_Wq"])
    Wk = np.asarray(inputs["L0_attn_Wk"])
    Wv = np.asarray(inputs["L0_attn_Wv"])
    Wf = np.asarray(inputs["L0_ffn_weight"])
    wall = np.empty((NFT, 4096, P), np.float32)
    for j in range(8):
        for r in range(4):
            wall[j * 4 + r] = Wq[:, :, 4 * j + r]
    for j in range(8):
        wall[32 + j] = Wk[:, :, j]
        wall[40 + j] = Wv[:, :, j]
    wt = Wf.reshape(P, 32, 4096)
    for h in range(32):
        wall[48 + h] = np.ascontiguousarray(wt[:, h, :].T)
    base = {
        "wall": wall,
        "bq": np.ascontiguousarray(inputs["L0_attn_bq"], np.float32),
        "bk": np.ascontiguousarray(inputs["L0_attn_bk"], np.float32),
        "bv": np.ascontiguousarray(inputs["L0_attn_bv"], np.float32),
        "fb": np.ascontiguousarray(np.asarray(inputs["L0_ffn_bias"], np.float32).reshape(P, 32)),
        "ah": np.ascontiguousarray(np.asarray(inputs["L0_ffn_lora_A"], np.float32)
                                   .transpose(2, 0, 1).reshape(CH, P, 3, 16)),
        "bh": np.ascontiguousarray(np.asarray(inputs["L0_ffn_lora_B"], np.float32)
                                   .reshape(3, P, 32, 16).transpose(0, 3, 2, 1)
                                   .reshape(3, 16, 32 * P)),
        "lsi": np.ascontiguousarray(np.asarray(inputs["L0_ffn_lora_start_indices"], np.int32)
                                    .reshape(1, 4)),
        "ident": np.eye(P, dtype=np.float32),
    }
    return base


def kernel(**inputs):
    x = np.asarray(inputs["L0_attn_x"], np.float32)
    B, Sq, E = x.shape
    xf = np.ascontiguousarray(x.reshape(B * Sq, E))
    base = _host_layout(inputs)
    nc = build(taps=False)
    in_maps = []
    for c in range(8):
        m = dict(base)
        m["x_in"] = xf[c * TPC:(c + 1) * TPC]
        m["bsel"] = np.full((1, 1), float((c * TPC) // Sq), np.float32)
        in_maps.append(m)
    res = run_bass_kernel_spmd(nc, in_maps, core_ids=list(range(8)))
    out = np.concatenate([np.asarray(r["out"]) for r in res.results], axis=0)
    return out.reshape(B, Sq, E).astype(np.float32)


# version 21 (final)
# speedup vs baseline: 1.0575x; 1.0059x over previous
from contextlib import ExitStack

import numpy as np

import concourse.bass as bass
import concourse.mybir as mybir
from concourse.bass_utils import run_bass_kernel_spmd

F32 = mybir.dt.float32
BF16 = mybir.dt.bfloat16
FP16 = mybir.dt.float16
I32 = mybir.dt.int32
AF = mybir.ActivationFunctionType
ALU = mybir.AluOpType
AX = mybir.AxisListType.X

P = 128
TPC = 1024
NT = TPC // P
CH = 32
NJ = 8
NFT = 80
QSCALE = float(np.float32(1.0) / np.float32(np.sqrt(np.float32(128.0))))
EPS = 1e-5
DEBUG_TAPS = False


class S:
    def __init__(self, ctx, nc, name):
        self.h = ctx.enter_context(nc.semaphore(name))
        self.n = 0


def build(taps=DEBUG_TAPS, phases=5):
    nc = bass.Bass("TRN2", target_bir_lowering=False, debug=False)

    x_in = nc.dram_tensor("x_in", [TPC, 4096], F32, kind="ExternalInput").ap()
    wall = nc.dram_tensor("wall", [NFT, 4096, P], F32, kind="ExternalInput").ap()
    bq_d = nc.dram_tensor("bq", [P, 32], F32, kind="ExternalInput").ap()
    bk_d = nc.dram_tensor("bk", [P, 8], F32, kind="ExternalInput").ap()
    bv_d = nc.dram_tensor("bv", [P, 8], F32, kind="ExternalInput").ap()
    fb_d = nc.dram_tensor("fb", [P, 32], F32, kind="ExternalInput").ap()
    ah_d = nc.dram_tensor("ah", [CH, P, 3, 16], F32, kind="ExternalInput").ap()
    bh_d = nc.dram_tensor("bh", [3, 16, 32 * P], F32, kind="ExternalInput").ap()
    lsi_d = nc.dram_tensor("lsi", [1, 4], I32, kind="ExternalInput").ap()
    bsel_d = nc.dram_tensor("bsel", [1, 1], F32, kind="ExternalInput").ap()
    id_d = nc.dram_tensor("ident", [P, P], F32, kind="ExternalInput").ap()
    out_d = nc.dram_tensor("out", [TPC, 4096], F32, kind="ExternalOutput").ap()
    scr_d = nc.dram_tensor("qkscr", [TPC * 16 * P], BF16, kind="Internal").ap()
    tap_d = {}
    if taps:
        tap_d["xnT"] = nc.dram_tensor("tap_xnT", [P, CH * TPC], BF16, kind="ExternalOutput").ap()
        tap_d["qsks"] = nc.dram_tensor("tap_qsks", [P, TPC * 16], BF16, kind="ExternalOutput").ap()
        tap_d["vall"] = nc.dram_tensor("tap_vall", [P, TPC * 9], BF16, kind="ExternalOutput").ap()
        tap_d["xaT"] = nc.dram_tensor("tap_xaT", [16, TPC], BF16, kind="ExternalOutput").ap()
        tap_d["ffn"] = nc.dram_tensor("tap_ffn", [P, TPC * 32], FP16, kind="ExternalOutput").ap()
        tap_d["hbc"] = nc.dram_tensor("tap_hbc", [P, 3], F32, kind="ExternalOutput").ap()
        tap_d["bselt"] = nc.dram_tensor("tap_bselt", [16, 4096], BF16, kind="ExternalOutput").ap()
        tap_d["lnst"] = nc.dram_tensor("tap_lnst", [P, 80], F32, kind="ExternalOutput").ap()

    with ExitStack() as ctx:
        def sbl(n_, s_, d_):
            return ctx.enter_context(nc.sbuf_tensor(n_, s_, d_))

        def sbr(n_, s_, d_):
            return ctx.enter_context(nc.sbuf_tensor(n_, s_, d_, side="right"))

        id_sb = sbl("id_sb", [P, P], BF16)
        bqss = sbl("bqss", [P, NJ], F32)
        bk_sb = sbl("bk_sb", [P, 8], F32)
        bv_sb = sbl("bv_sb", [P, 8], F32)
        fb_sb = sbl("fb_sb", [P, 32], F32)
        bq_sb = sbl("bq_sb", [P, 32], F32)
        hbc = sbl("hbc", [P, 3], F32)
        ones_r = sbl("ones_r", [1, P], BF16)
        h3b = sbl("h3b", [1, 3], BF16)
        lsi_sb = sbl("lsi_sb", [1, 4], I32)
        lsif = sbl("lsif", [1, 4], F32)
        cle = sbl("cle", [1, 4], F32)
        bsel_sb = sbl("bsel_sb", [1, 1], F32)
        tmp1 = sbl("tmp1", [1, 4], F32)
        epsc = sbl("epsc", [P, 1], F32)
        ps_h_guard = nc.psum_tensor("ps_h", [P, 3], F32)
        ps_h = ps_h_guard.__enter__()
        xasm_e = sbr("xasm_e", [P, P * 32], F32)
        xaT_guard = nc.sbuf_tensor("xaT", [16, TPC], BF16)
        xaT = xaT_guard.__enter__()
        xnT_guard = nc.sbuf_tensor("xnT", [P, CH * TPC], BF16)
        xnT = xnT_guard.__enter__()
        wsl0_guard = nc.sbuf_tensor("wsl0", [P, CH * P], BF16)
        wsl1_guard = nc.sbuf_tensor("wsl1", [P, CH * P], BF16)
        wsl = [wsl0_guard.__enter__(), wsl1_guard.__enter__()]

        GQ = S(ctx, nc, "GQ")
        BS = S(ctx, nc, "BS")
        WQ = [S(ctx, nc, "WQ0"), S(ctx, nc, "WQ1")]
        BO = [S(ctx, nc, "BO0"), S(ctx, nc, "BO1")]
        XD = [S(ctx, nc, f"XD{i}") for i in range(3)]
        QK = [S(ctx, nc, "QK0"), S(ctx, nc, "QK1")]
        OD = [S(ctx, nc, f"OD{i}") for i in range(4)]
        OG = [S(ctx, nc, f"OG{i}") for i in range(4)]
        XS = [S(ctx, nc, f"XS{i}") for i in range(4)]
        SQ = S(ctx, nc, "SQ")
        TEN = S(ctx, nc, "TEN")
        SCA = S(ctx, nc, "SCA")
        VEC = S(ctx, nc, "VEC")
        PL = S(ctx, nc, "PL")

        block = ctx.enter_context(nc.Block())

        p1ctx = ExitStack()
        xt = [p1ctx.enter_context(nc.sbuf_tensor(f"xt{i}", [P, 4096], F32)) for i in range(3)]
        sqd = [p1ctx.enter_context(nc.sbuf_tensor(f"sqd{i}", [P, 4096], BF16))
               for i in range(2)]
        xnt_ = [p1ctx.enter_context(nc.sbuf_tensor(f"xn{i}", [P, 4096], BF16)) for i in range(2)]
        lnst = p1ctx.enter_context(nc.sbuf_tensor("lnst", [P, 80], F32))
        tr_ps = [p1ctx.enter_context(nc.psum_tensor(f"tr_ps{i}", [P, 512], BF16)) for i in range(4)]

        @block.gpsimd
        def _(g):
            g.dma_start(out=id_sb[:], in_=id_d).then_inc(GQ.h, 16)
            g.dma_start(out=bq_sb[:], in_=bq_d).then_inc(GQ.h, 16)
            g.dma_start(out=bk_sb[:], in_=bk_d).then_inc(GQ.h, 16)
            g.dma_start(out=bv_sb[:], in_=bv_d).then_inc(GQ.h, 16)
            g.dma_start(out=fb_sb[:], in_=fb_d).then_inc(GQ.h, 16)
            g.dma_start(out=lsi_sb[:], in_=lsi_d).then_inc(GQ.h, 16)
            g.dma_start(out=bsel_sb[:], in_=bsel_d).then_inc(GQ.h, 16)
            GQ.n += 7 * 16
            for ft in range(2):
                g.dma_start(out=wsl[ft][:].rearrange("p (c f) -> p c f", f=P),
                            in_=wall[ft].rearrange("(c p) f -> p c f", p=P))\
                    .then_inc(WQ[ft].h, 16)
                WQ[ft].n += 16
        p0_gq = GQ.n

        @block.vector
        def _(v):
            v.memset(epsc[:], EPS).then_inc(VEC.h, 1); VEC.n += 1
            v.memset(lnst[:], 0.0).then_inc(VEC.h, 1); VEC.n += 1
        assert VEC.n == 2, VEC.n

        p1_norm_mark = [None] * NT
        p1_trev_mark = [None] * NT
        p1_tr_mark = [[None] * 8 for _ in range(NT)]
        p1_var_mark = [None] * NT
        p1_nmr_mark = [None] * NT

        if True:
            def STT(tt, i):
                return lnst[:, tt * 10 + i: tt * 10 + i + 1]

            p1_sq_mark = [None] * NT
            p1_ln_mark = [None] * NT
            p1_exp_mark = [None] * NT
            p1_hn_mark = [None] * NT
            p1_evac_mark = [[None] * 8 for _ in range(NT)]
            vec, ten = VEC.n, TEN.n
            for tt in range(NT):
                vec += 3
                vec += 2
                p1_var_mark[tt] = vec
                vec += 1
                p1_hn_mark[tt] = vec
                vec += 1
                p1_norm_mark[tt] = vec
                for b4 in range(8):
                    ten += 1
                    p1_tr_mark[tt][b4] = ten
            sca = SCA.n
            for tt in range(NT):
                sca += 1
                p1_sq_mark[tt] = sca
                if tt >= 1:
                    for b4 in range(8):
                        sca += 1
                        p1_evac_mark[tt - 1][b4] = sca
                sca += 1
                p1_ln_mark[tt] = sca
                sca += 1
                p1_exp_mark[tt] = sca
            for b4 in range(8):
                sca += 1
                p1_evac_mark[NT - 1][b4] = sca
            for tt in range(NT):
                p1_trev_mark[tt] = p1_evac_mark[tt][7]

            def xd_mark(tt):
                return 16 * (tt // 3 + 1) + (16 if tt % 3 == 0 else 0)

            @block.sync
            def _(sy):
                for tt in range(NT):
                    if tt >= 3:
                        sy.wait_ge(VEC.h, p1_norm_mark[tt - 3])
                        sy.wait_ge(SCA.h, p1_sq_mark[tt - 3])
                    if tt == 0:
                        sy.dma_start(out=xt[0][:, 0:2048],
                                     in_=x_in[0:P, 0:2048]).then_inc(XD[0].h, 16)
                        XD[0].n += 16
                    else:
                        sy.dma_start(out=xt[tt % 3][:],
                                     in_=x_in[tt * P:(tt + 1) * P, :]).then_inc(XD[tt % 3].h, 16)
                        XD[tt % 3].n += 16

            @block.vector
            def _(v):
                v.wait_ge(VEC.h, VEC.n)
                for tt in range(NT):
                    v.wait_ge(XD[tt % 3].h, xd_mark(tt))
                    v.reduce_sum(STT(tt, 0), xt[tt % 3][:], axis=AX).then_inc(VEC.h, 1)
                    VEC.n += 1
                    v.wait_ge(VEC.h, VEC.n)
                    v.tensor_scalar_mul(STT(tt, 1), STT(tt, 0), -1.0 / 4096.0).then_inc(VEC.h, 1)
                    VEC.n += 1
                    v.wait_ge(VEC.h, VEC.n)
                    v.tensor_tensor(STT(tt, 2), STT(tt, 1), STT(tt, 1), ALU.mult).then_inc(VEC.h, 1)
                    VEC.n += 1
                    v.wait_ge(SCA.h, p1_sq_mark[tt])
                    v.tensor_scalar_mul(STT(tt, 3), STT(tt, 7), 1.0 / 4096.0).then_inc(VEC.h, 1)
                    VEC.n += 1
                    v.wait_ge(VEC.h, VEC.n)
                    v.tensor_tensor(STT(tt, 3), STT(tt, 3), STT(tt, 2), ALU.subtract).then_inc(VEC.h, 1)
                    VEC.n += 1
                    assert VEC.n == p1_var_mark[tt]
                    v.wait_ge(SCA.h, p1_ln_mark[tt])
                    v.tensor_scalar_mul(STT(tt, 8), STT(tt, 4), -0.5).then_inc(VEC.h, 1)
                    VEC.n += 1
                    assert VEC.n == p1_hn_mark[tt]
                    v.wait_ge(SCA.h, p1_exp_mark[tt])
                    if tt >= 2:
                        v.wait_ge(TEN.h, p1_tr_mark[tt - 2][7])
                    v.tensor_scalar(xnt_[tt % 2][:], xt[tt % 3][:], STT(tt, 1), STT(tt, 5),
                                    ALU.add, ALU.mult).then_inc(VEC.h, 1)
                    VEC.n += 1
                    assert VEC.n == p1_norm_mark[tt]

            def p1_emit_evacs(s, te):
                pair = 2 * (te % 2)
                for b4 in range(8):
                    s.wait_ge(TEN.h, p1_tr_mark[te][b4])
                    dst = xnT[:].rearrange("p (c t) -> p c t", t=TPC)[
                        :, b4 * 4:(b4 + 1) * 4, te * P:(te + 1) * P]
                    s.activation(dst, tr_ps[pair + b4 % 2][:].rearrange("p (c t) -> p c t", t=P),
                                 AF.Copy).then_inc(SCA.h, 1)
                    SCA.n += 1
                    assert SCA.n == p1_evac_mark[te][b4]

            @block.scalar
            def _(s):
                s.dma_start(out=xt[0][:, 2048:4096],
                            in_=x_in[0:P, 2048:4096]).then_inc(XD[0].h, 16)
                XD[0].n += 16
                s.wait_ge(VEC.h, 2)
                for tt in range(NT):
                    s.wait_ge(XD[tt % 3].h, xd_mark(tt))
                    s.activation(sqd[tt % 2][:], xt[tt % 3][:], AF.Square,
                                 accum_out=STT(tt, 7)).then_inc(SCA.h, 1)
                    SCA.n += 1
                    assert SCA.n == p1_sq_mark[tt]
                    if tt >= 1:
                        p1_emit_evacs(s, tt - 1)
                    s.wait_ge(VEC.h, p1_var_mark[tt])
                    s.activation(STT(tt, 4), STT(tt, 3), AF.Ln, bias=epsc[:, 0:1]).then_inc(SCA.h, 1)
                    SCA.n += 1
                    assert SCA.n == p1_ln_mark[tt]
                    s.wait_ge(VEC.h, p1_hn_mark[tt])
                    s.activation(STT(tt, 5), STT(tt, 8), AF.Exp).then_inc(SCA.h, 1)
                    SCA.n += 1
                    assert SCA.n == p1_exp_mark[tt]
                p1_emit_evacs(s, NT - 1)

            @block.tensor
            def _(t):
                t.wait_ge(GQ.h, p0_gq)
                for tt in range(NT):
                    pair = 2 * (tt % 2)
                    t.wait_ge(VEC.h, p1_norm_mark[tt])
                    if tt >= 2:
                        t.wait_ge(SCA.h, p1_evac_mark[tt - 2][7])
                    for b4 in range(8):
                        if b4 >= 2:
                            t.wait_ge(SCA.h, p1_evac_mark[tt][b4 - 2])
                        for c in range(4):
                            cc = b4 * 4 + c
                            tr = t.transpose(tr_ps[pair + b4 % 2][:, c * P:(c + 1) * P],
                                             xnt_[tt % 2][:, cc * P:(cc + 1) * P], id_sb[:])
                        tr.then_inc(TEN.h, 1)
                        TEN.n += 1
                        assert TEN.n == p1_tr_mark[tt][b4]
            if taps:
                @block.sync
                def _(sy):
                    sy.wait_ge(SCA.h, p1_trev_mark[NT - 1])
                    sy.dma_start(out=tap_d["lnst"], in_=lnst[:]).then_inc(SQ.h, 16)
                    SQ.n += 16
            tap_lnst_sq = SQ.n
            p1_ten_end, p1_sca_end = TEN.n, SCA.n
        p1ctx.close()
        vall = sbr("vall", [P, TPC * 9], BF16)

        aselt_guard = nc.sbuf_tensor("aselt", [P, CH * 16], BF16)
        aselt = aselt_guard.__enter__()
        bselt_guard = nc.sbuf_tensor("bselt", [16, 32 * P], BF16)
        bselt = bselt_guard.__enter__()
        p0bctx = ExitStack()
        a_all = p0bctx.enter_context(nc.sbuf_tensor("a_all", [P, CH * 48], BF16, side="right"))
        bsl = [p0bctx.enter_context(nc.sbuf_tensor(f"bsl{i}", [16, 32 * P], BF16, side="right"))
               for i in range(3)]
        p0b_vec0 = 2 + NT * 7
        p0b_mask_mark = p0b_vec0 + 12
        p0_vec = p0b_mask_mark + 6
        p0_b_vec_mark = [p0b_mask_mark + 3 + (a + 1) for a in range(3)]
        p0b_gq = p0_gq + 16

        @block.vector
        def _(v):
            assert VEC.n == p0b_vec0, VEC.n
            v.wait_ge(GQ.h, p0_gq)
            v.tensor_copy(lsif[:], lsi_sb[:]).then_inc(VEC.h, 1); VEC.n += 1
            v.wait_ge(VEC.h, VEC.n)
            v.tensor_scalar(cle[:], lsif[:], bsel_sb[0:1, 0:1], None, ALU.is_le)\
                .then_inc(VEC.h, 1); VEC.n += 1
            v.wait_ge(VEC.h, VEC.n)
            v.tensor_tensor(h3b[0:1, 2:3], cle[0:1, 2:3], cle[0:1, 3:4], ALU.mult)\
                .then_inc(VEC.h, 1); VEC.n += 1
            v.wait_ge(VEC.h, VEC.n)
            v.tensor_tensor(h3b[0:1, 1:2], cle[0:1, 2:3], h3b[0:1, 2:3], ALU.subtract)\
                .then_inc(VEC.h, 1); VEC.n += 1
            v.tensor_tensor(tmp1[0:1, 0:1], cle[0:1, 1:2], cle[0:1, 2:3], ALU.mult)\
                .then_inc(VEC.h, 1); VEC.n += 1
            v.wait_ge(VEC.h, VEC.n)
            v.tensor_tensor(h3b[0:1, 0:1], cle[0:1, 1:2], tmp1[0:1, 0:1], ALU.subtract)\
                .then_inc(VEC.h, 1); VEC.n += 1
            v.memset(ones_r[:], 1.0).then_inc(VEC.h, 1); VEC.n += 1
            bqv = bq_sb[:].rearrange("p (j r) -> p j r", r=4)
            v.tensor_tensor(bqss[:], bqv[:, :, 0], bqv[:, :, 1], ALU.add)\
                .then_inc(VEC.h, 1); VEC.n += 1
            v.wait_ge(VEC.h, VEC.n)
            v.tensor_tensor(bqss[:], bqss[:], bqv[:, :, 2], ALU.add)\
                .then_inc(VEC.h, 1); VEC.n += 1
            v.wait_ge(VEC.h, VEC.n)
            v.tensor_tensor(bqss[:], bqss[:], bqv[:, :, 3], ALU.add)\
                .then_inc(VEC.h, 1); VEC.n += 1
            v.wait_ge(VEC.h, VEC.n)
            v.tensor_scalar_mul(bqss[:], bqss[:], QSCALE).then_inc(VEC.h, 1); VEC.n += 1
            v.memset(vall[:], 1.0).then_inc(VEC.h, 1); VEC.n += 1
            assert VEC.n == p0b_mask_mark, VEC.n

        p2_ten0, p2_sca0 = TEN.n, SCA.n
        NQKV = 48

        def wq_mark(ft):
            return 16 * (ft // 2 + 1)

        if phases < 2:
            @block.sync
            def _(sy):
                if taps:
                    sy.wait_ge(SCA.h, p1_trev_mark[NT - 1])
                    sy.dma_start(out=tap_d["xnT"], in_=xnT[:]).then_inc(SQ.h, 16)
                    SQ.n += 16
                sy.wait_ge(SQ.h, SQ.n)
                for sem in (GQ, BS, XD[0], XD[1]):
                    sy.wait_ge(sem.h, sem.n)
            p0bctx.close()
            bselt_guard.__exit__(None, None, None)
            aselt_guard.__exit__(None, None, None)
            wsl1_guard.__exit__(None, None, None)
            wsl0_guard.__exit__(None, None, None)
            xnT_guard.__exit__(None, None, None)
            xaT_guard.__exit__(None, None, None)
            return nc
        def p2tm(ft):
            return p2_ten0 + ft + 1 + (1 if ft >= 3 else 0)

        @block.gpsimd
        def _(g):
            for ft in range(2, NQKV):
                if ft >= 2:
                    g.wait_ge(TEN.h, p2tm(ft - 2))
                g.dma_start(out=wsl[ft % 2][:].rearrange("p (c f) -> p c f", f=P),
                            in_=wall[ft].rearrange("(c p) f -> p c f", p=P))\
                    .then_inc(WQ[ft % 2].h, 16)
                WQ[ft % 2].n += 16
                if ft == 2:
                    g.wait_ge(SCA.h, p1_trev_mark[NT - 1])
                    if taps:
                        g.wait_ge(SQ.h, tap_lnst_sq)
                    g.dma_start(out=a_all[:].rearrange("p (c a r) -> p c a r", c=CH, a=3),
                                in_=ah_d.rearrange("c p a r -> p c a r")).then_inc(GQ.h, 16)
                    GQ.n += 16
                    for a in range(3):
                        g.dma_start(out=bsl[a][:], in_=bh_d[a]).then_inc(BS.h, 16)
                    BS.n += 3 * 16

        with nc.sbuf_tensor("qsks", [P, TPC * 16], BF16, side="right") as qsks:
            with nc.psum_tensor("qk_ps0", [P, 512], F32) as qk_ps0, \
                 nc.psum_tensor("qk_ps1", [P, 512], F32) as qk_ps1, \
                 nc.psum_tensor("qk_ps2", [P, 512], F32) as qk_ps2, \
                 nc.psum_tensor("qk_ps3", [P, 512], F32) as qk_ps3, \
                 nc.psum_tensor("xa_ps0", [16, 512], F32) as xa_ps0, \
                 nc.psum_tensor("xa_ps1", [16, 512], F32) as xa_ps1:
                qk_ps = [qk_ps0, qk_ps1, qk_ps2, qk_ps3]
                xa_ps = [xa_ps0, xa_ps1]

                @block.tensor
                def _(t):
                    t.wait_ge(SCA.h, p1_trev_mark[NT - 1])
                    for ft in range(NQKV):
                        t.wait_ge(WQ[ft % 2].h, wq_mark(ft))
                        if ft < 32:
                            j, r = divmod(ft, 4)
                            first, last = (r == 0), (r == 3)
                            bank = 2 * (j % 2)
                            if first and j >= 2:
                                t.wait_ge(SCA.h, p2_sca0 + 1 + 2 * (j - 1))
                        else:
                            first = last = True
                            bank = 2 * (ft % 2)
                            if ft == 32:
                                t.wait_ge(SCA.h, p2_sca0 + 15)
                            elif ft == 33:
                                t.wait_ge(SCA.h, p2_sca0 + 17)
                            else:
                                t.wait_ge(SCA.h, p2_sca0 + 17 + 2 * (ft - 33))
                        for c in range(CH):
                            for half in range(2):
                                mm = t.matmul(
                                    qk_ps[bank + half][:],
                                    wsl[ft % 2][:, c * P:(c + 1) * P],
                                    xnT[:, c * TPC + half * 512: c * TPC + (half + 1) * 512],
                                    start=(first and c == 0), stop=(last and c == CH - 1))
                        mm.then_inc(TEN.h, 1)
                        TEN.n += 1
                        if ft == 2:
                            t.wait_ge(VEC.h, p0b_mask_mark)
                            t.matmul(ps_h[:], ones_r[:], h3b[:], start=True, stop=True)\
                                .then_inc(TEN.h, 1)
                            TEN.n += 1
                    t.wait_ge(VEC.h, p0_vec)
                    for c in range(CH):
                        for half in range(2):
                            mm = t.matmul(xa_ps[half][:],
                                          aselt[:, c * 16:(c + 1) * 16],
                                          xnT[:, c * TPC + half * 512: c * TPC + (half + 1) * 512],
                                          start=(c == 0), stop=(c == CH - 1))
                    mm.then_inc(TEN.h, 1)
                    TEN.n += 1
                p2_ten_end = TEN.n
                p0_ten = p2_ten0 + 4
                assert p2_ten_end == p2_ten0 + NQKV + 2

                @block.scalar
                def _(s):
                    qv = qsks[:].rearrange("p (t cc) -> p t cc", cc=16)
                    vv = vall[:].rearrange("p (t cc) -> p t cc", cc=9)
                    s.wait_ge(TEN.h, p0_ten)
                    s.activation(hbc[:], ps_h[:], AF.Copy).then_inc(SCA.h, 1)
                    SCA.n += 1
                    for j in range(NJ):
                        s.wait_ge(TEN.h, p2tm(4 * j + 3))
                        if j == 0:
                            s.wait_ge(VEC.h, p0b_mask_mark)
                        bank = 2 * (j % 2)
                        for half in range(2):
                            dst = qv[:, half * 512:(half + 1) * 512, j]
                            s.activation(dst, qk_ps[bank + half][:], AF.Identity,
                                         bias=bqss[:, j:j + 1], scale=QSCALE)\
                                .then_inc(SCA.h, 1)
                            SCA.n += 1
                    for ft in range(32, NQKV):
                        s.wait_ge(TEN.h, p2tm(ft))
                        kv = ft - 32
                        j = kv % 8
                        bank = 2 * (ft % 2)
                        for half in range(2):
                            if kv < 8:
                                dst = qv[:, half * 512:(half + 1) * 512, 8 + j]
                                bias = bk_sb[:, j:j + 1]
                            else:
                                dst = vv[:, half * 512:(half + 1) * 512, j]
                                bias = bv_sb[:, j:j + 1]
                            s.activation(dst, qk_ps[bank + half][:], AF.Identity,
                                         bias=bias).then_inc(SCA.h, 1)
                            SCA.n += 1
                    s.wait_ge(TEN.h, p2_ten_end)
                    for half in range(2):
                        s.activation(xaT[:, half * 512:(half + 1) * 512],
                                     xa_ps[half][:], AF.Copy).then_inc(SCA.h, 1)
                        SCA.n += 1
                p2_sca_end = SCA.n
                p0_sca = p2_sca0 + 1
                assert p2_sca_end == p2_sca0 + 51

                @block.vector
                def _(v):
                    v.wait_ge(SCA.h, p0_sca)
                    v.wait_ge(GQ.h, p0b_gq)
                    av = a_all[:].rearrange("p (c a r) -> p c a r", c=CH, a=3)
                    sv = aselt[:].rearrange("p (c r) -> p c r", c=CH)
                    v.tensor_scalar(sv, av[:, :, 0, :], hbc[:, 0:1], None, ALU.mult)\
                        .then_inc(VEC.h, 1); VEC.n += 1
                    for a in (1, 2):
                        v.wait_ge(VEC.h, VEC.n)
                        v.scalar_tensor_tensor(sv, av[:, :, a, :], hbc[:, a:a + 1], sv,
                                               ALU.mult, ALU.add).then_inc(VEC.h, 1); VEC.n += 1
                    for a in range(3):
                        v.wait_ge(BS.h, 48)
                        if a == 0:
                            v.tensor_scalar(bselt[:], bsl[a][:], hbc[0:16, 0:1], None, ALU.mult)\
                                .then_inc(VEC.h, 1); VEC.n += 1
                        else:
                            v.wait_ge(VEC.h, VEC.n)
                            v.scalar_tensor_tensor(bselt[:], bsl[a][:], hbc[0:16, a:a + 1],
                                                   bselt[:], ALU.mult, ALU.add)\
                                .then_inc(VEC.h, 1); VEC.n += 1
                        assert p0_b_vec_mark[a] == VEC.n, (a, p0_b_vec_mark[a], VEC.n)
                assert VEC.n == p0_vec, (VEC.n, p0_vec)

            p3_ten0, p3_sca0 = TEN.n, SCA.n
            NB = 32
            with nc.sbuf_tensor("tsb0", [P, 1024], BF16) as tsb0, \
                 nc.sbuf_tensor("tsb1", [P, 1024], BF16) as tsb1, \
                 nc.psum_tensor("p3_ps0", [P, 512], BF16) as p3_ps0, \
                 nc.psum_tensor("p3_ps1", [P, 512], BF16) as p3_ps1:
                tsb = [tsb0, tsb1]
                p3_ps = [p3_ps0, p3_ps1]

                @block.tensor
                def _(t):
                    t.wait_ge(SCA.h, p2_sca_end)
                    for b in range(NB):
                        if b >= 2:
                            t.wait_ge(SCA.h, p3_sca0 + b - 1)
                        for g4 in range(4):
                            col = (b * 4 + g4) * P
                            tr = t.transpose(p3_ps[b % 2][:, g4 * P:(g4 + 1) * P],
                                             qsks[:, col:col + P], id_sb[:])
                        tr.then_inc(TEN.h, 1)
                        TEN.n += 1
                p3_ten_end = TEN.n

                @block.scalar
                def _(s):
                    if taps:
                        s.wait_ge(SQ.h, tap_lnst_sq)
                    for b in range(NB):
                        s.wait_ge(TEN.h, p3_ten0 + b + 1)
                        if b >= 4:
                            bbp = b // 2 - 2
                            s.wait_ge(BO[bbp % 2].h, 16 * (bbp // 2 + 1))
                        s.activation(tsb[(b // 2) % 2][:, (b % 2) * 512:(b % 2 + 1) * 512],
                                     p3_ps[b % 2][:], AF.Copy).then_inc(SCA.h, 1)
                        SCA.n += 1
                p3_sca_end = SCA.n

                @block.sync
                def _(sy):
                    scr4 = scr_d.rearrange("(bb gg p d) -> bb p gg d", bb=16, gg=8, p=P)
                    for bb in range(16):
                        sy.wait_ge(SCA.h, p3_sca0 + 2 * bb + 2)
                        sy.dma_start(out=scr4[bb],
                                    in_=tsb[bb % 2][:].rearrange("p (gg d) -> p gg d", gg=8))\
                            .then_inc(BO[bb % 2].h, 16)
                        BO[bb % 2].n += 16

                if taps:
                    @block.sync
                    def _(sy):
                        sy.wait_ge(TEN.h, p3_ten_end)
                        sy.dma_start(out=tap_d["qsks"], in_=qsks[:]).then_inc(SQ.h, 16)
                        SQ.n += 16
                tap_qsks_sq = SQ.n

        p0bctx.close()
        ps_h_guard.__exit__(None, None, None)

        ffno = ctx.enter_context(nc.sbuf_tensor("ffno", [P, TPC * 32], FP16, side="right"))
        p4_ten0, p4_sca0 = TEN.n, SCA.n
        with nc.psum_tensor("f_ps0", [P, 512], F32) as f_ps0, \
             nc.psum_tensor("f_ps1", [P, 512], F32) as f_ps1, \
             nc.psum_tensor("f_ps2", [P, 512], F32) as f_ps2, \
             nc.psum_tensor("f_ps3", [P, 512], F32) as f_ps3:
            f_ps = [f_ps0, f_ps1, f_ps2, f_ps3]

            @block.gpsimd
            def _(g):
                for hh in range(32):
                    ft = NQKV + hh
                    if hh == 0:
                        g.wait_ge(TEN.h, p2tm(46))
                    elif hh == 1:
                        g.wait_ge(TEN.h, p2tm(47))
                    else:
                        g.wait_ge(TEN.h, p4_ten0 + hh - 1)
                    g.dma_start(out=wsl[ft % 2][:].rearrange("p (c f) -> p c f", f=P),
                                in_=wall[ft].rearrange("(c p) f -> p c f", p=P))\
                        .then_inc(WQ[ft % 2].h, 16)
                    WQ[ft % 2].n += 16

            @block.tensor
            def _(t):
                t.wait_ge(SCA.h, p3_sca_end)
                t.wait_ge(VEC.h, p0_vec)
                for hh in range(32):
                    ft = NQKV + hh
                    t.wait_ge(WQ[ft % 2].h, wq_mark(ft))
                    bank = 2 * (hh % 2)
                    if hh >= 2:
                        t.wait_ge(SCA.h, p4_sca0 + 2 * (hh - 1))
                    for c in range(CH):
                        for half in range(2):
                            t.matmul(f_ps[bank + half][:],
                                     wsl[ft % 2][:, c * P:(c + 1) * P],
                                     xnT[:, c * TPC + half * 512: c * TPC + (half + 1) * 512],
                                     start=(c == 0), stop=False)
                    for half in range(2):
                        mm = t.matmul(f_ps[bank + half][:],
                                      bselt[:, hh * P:(hh + 1) * P],
                                      xaT[:, half * 512:(half + 1) * 512],
                                      start=False, stop=True)
                    mm.then_inc(TEN.h, 1)
                    TEN.n += 1
            p4_ten_end = TEN.n

            if taps:
                @block.sync
                def _(sy):
                    sy.wait_ge(TEN.h, p4_ten_end)
                    sy.dma_start(out=tap_d["xnT"], in_=xnT[:]).then_inc(SQ.h, 16)
                    sy.dma_start(out=tap_d["bselt"], in_=bselt[:]).then_inc(SQ.h, 16)
                    sy.dma_start(out=tap_d["xaT"], in_=xaT[:]).then_inc(SQ.h, 16)
                    SQ.n += 3 * 16
            tap_xnT_sq = SQ.n

            @block.scalar
            def _(s):
                if taps:
                    s.wait_ge(SQ.h, tap_qsks_sq)
                fv = ffno[:].rearrange("p (t h) -> p t h", h=32)
                for hh in range(32):
                    s.wait_ge(TEN.h, p4_ten0 + hh + 1)
                    bank = 2 * (hh % 2)
                    for half in range(2):
                        s.activation(fv[:, half * 512:(half + 1) * 512, hh],
                                     f_ps[bank + half][:], AF.Identity,
                                     bias=fb_sb[:, hh:hh + 1]).then_inc(SCA.h, 1)
                        SCA.n += 1
            p4_sca_end = SCA.n

        bselt_guard.__exit__(None, None, None)
        aselt_guard.__exit__(None, None, None)
        wsl1_guard.__exit__(None, None, None)
        wsl0_guard.__exit__(None, None, None)
        xnT_guard.__exit__(None, None, None)
        xaT_guard.__exit__(None, None, None)

        p5_ten0, p5_sca0, p5_vec0 = TEN.n, SCA.n, VEC.n
        ten_sgrp = lambda ck, g: p5_ten0 + ck * 10 + g + 1
        ten_schunk = lambda ck: p5_ten0 + ck * 10 + 8
        ten_osub = lambda ck, sidx: p5_ten0 + ck * 10 + 8 + sidx + 1
        sca_exp = lambda ck, g: p5_sca0 + ck * 8 + g + 1

        vmark_mult = {}
        vmark_q = {}
        vmark_tile = {}
        nv = p5_vec0
        for tt in range(NT):
            for sub in range(8):
                nv += 2
                vmark_mult[tt * 8 + sub] = nv
                if sub % 2 == 1:
                    nv += 2
                    vmark_q[(tt, sub // 2)] = nv
            vmark_tile[tt] = nv
        p5_vec_end = nv

        def qk_mark(cki):
            return 32 * (cki // 2 + 1)

        def xs_mark(tt):
            return 32 * (tt // 4 + 1)

        def od_mark(tt):
            return 32 * (tt // 4 + 1)

        with ExitStack() as p5ctx:
            def p5sb(n_, s_):
                return p5ctx.enter_context(nc.sbuf_tensor(n_, s_, BF16, side="right"))
            qst = [p5sb(f"qst{i}", [NJ, 32 * P]) for i in range(2)]
            kst = [p5sb(f"kst{i}", [NJ, 32 * P]) for i in range(2)]
            ebf = [p5sb(f"ebf{i}", [P, 32 * P]) for i in range(2)]
            xasm = [xasm_e] + [
                p5ctx.enter_context(nc.sbuf_tensor(f"xasm{i}", [P, P * 32], F32, side="right"))
                for i in range(1, 4)]
            tmpa_full = p5ctx.enter_context(
                nc.sbuf_tensor("tmpa_full", [P, 2048], F32, side="right"))

            def tmpa_slice(sub, n=512):
                base = ((sub // 2) % 2) * 1024 + (sub % 2) * 512
                return tmpa_full[:, base:base + n]
            rdb = p5ctx.enter_context(nc.sbuf_tensor("rdb", [P, 32], F32, side="right"))
            s_ps = [p5ctx.enter_context(nc.psum_tensor(f"s_ps{i}", [P, 512], F32)) for i in range(3)]
            a_ps = [p5ctx.enter_context(nc.psum_tensor(f"a_ps{i}", [P, 512], F32)) for i in range(3)]
            b_ps = [p5ctx.enter_context(nc.psum_tensor(f"b_ps{i}", [P, 16], F32)) for i in range(2)]
            scr5 = scr_d.rearrange("(t cc d) -> cc t d", t=TPC, cc=16)

            def qk_load(eng, cki):
                t0 = (cki // 4) * P + (cki % 4) * 32
                eng.dma_start(out=qst[cki % 2][:].rearrange("j (t d) -> j t d", t=32),
                              in_=scr5[0:8, t0:t0 + 32, :]).then_inc(QK[cki % 2].h, 16)
                eng.dma_start(out=kst[cki % 2][:].rearrange("j (t d) -> j t d", t=32),
                              in_=scr5[8:16, t0:t0 + 32, :]).then_inc(QK[cki % 2].h, 16)
                QK[cki % 2].n += 32

            @block.sync
            def _(sy):
                sy.wait_ge(SCA.h, p4_sca_end)
                if taps:
                    sy.wait_ge(SQ.h, tap_xnT_sq)

                def emit_out(tt, qq):
                    sy.wait_ge(VEC.h, vmark_q[(tt, qq)])
                    sy.dma_start(
                        out=out_d[tt * P + qq * 32: tt * P + (qq + 1) * 32, :]
                        .rearrange("t (d h) -> d t h", h=32),
                        in_=xasm[tt % 4][:, qq * 1024:(qq + 1) * 1024]
                        .rearrange("d (t h) -> d t h", h=32)).then_inc(OD[tt % 4].h, 16)
                    OD[tt % 4].n += 16

                for tt in range(NT):
                    emit_out(tt, 0)
                    emit_out(tt, 2)

            @block.gpsimd
            def _(g):
                def emit_out1(ttE):
                    for qq in (1, 3):
                        g.wait_ge(VEC.h, vmark_q[(ttE, qq)])
                        g.dma_start(
                            out=out_d[ttE * P + qq * 32: ttE * P + (qq + 1) * 32, :]
                            .rearrange("t (d h) -> d t h", h=32),
                            in_=xasm[ttE % 4][:, qq * 1024:(qq + 1) * 1024]
                            .rearrange("d (t h) -> d t h", h=32)).then_inc(OG[ttE % 4].h, 16)
                        OG[ttE % 4].n += 16

                if taps:
                    g.wait_ge(SQ.h, tap_xnT_sq)
                for tt in range(NT):
                    if tt == 1:
                        g.wait_ge(SCA.h, p4_sca_end)
                    if tt >= 4:
                        g.wait_ge(OD[tt % 4].h, od_mark(tt - 4))
                        g.wait_ge(OG[tt % 4].h, od_mark(tt - 4))
                    for hf in range(2):
                        g.dma_start(
                            out=xasm[tt % 4][:, hf * 2048:(hf + 1) * 2048]
                            .rearrange("d (t h) -> d t h", h=32),
                            in_=x_in[tt * P + hf * 64: tt * P + (hf + 1) * 64, :]
                            .rearrange("t (d h) -> d t h", h=32)).then_inc(XS[tt % 4].h, 16)
                        XS[tt % 4].n += 16
                    if tt >= 2:
                        emit_out1(tt - 2)
                emit_out1(NT - 2)
                emit_out1(NT - 1)

            @block.tensor
            def _(t):
                t.wait_ge(SCA.h, p4_sca_end)
                for tt in range(NT):
                    for ck in range(4):
                        cki = tt * 4 + ck
                        t.wait_ge(QK[cki % 2].h, qk_mark(cki))
                        for g in range(8):
                            gidx = cki * 8 + g
                            if gidx >= 3:
                                pc, pg = divmod(gidx - 3, 8)
                                t.wait_ge(SCA.h, sca_exp(pc, pg))
                            for k4 in range(4):
                                ts = g * 4 + k4
                                mm = t.matmul(
                                    s_ps[gidx % 3][:, k4 * P:(k4 + 1) * P],
                                    kst[cki % 2][:, ts * P:(ts + 1) * P],
                                    qst[cki % 2][:, ts * P:(ts + 1) * P],
                                    start=True, stop=True)
                            mm.then_inc(TEN.h, 1)
                            TEN.n += 1
                        for ts in range(32):
                            sidx = ts // 16
                            vs = cki * 2 + sidx
                            if ts % 4 == 0:
                                t.wait_ge(SCA.h, sca_exp(cki, ts // 4))
                            if ts % 16 == 0 and vs >= 3:
                                t.wait_ge(VEC.h, vmark_mult[vs - 3])
                            if ts % 16 == 0 and vs >= 2:
                                t.wait_ge(VEC.h, vmark_mult[vs - 2] - 1)
                            tok = tt * P + ck * 32 + ts
                            vrep = vall[:, tok * 9:tok * 9 + 8].unsqueeze(2)\
                                .broadcast_to([P, 8, 4])
                            t.matmul(a_ps[vs % 3][:, (ts % 16) * 32:(ts % 16) * 32 + 32],
                                     ebf[cki % 2][:, ts * P:(ts + 1) * P], vrep,
                                     start=True, stop=True)
                            mmb = t.matmul(b_ps[vs % 2][:, ts % 16:ts % 16 + 1],
                                           ebf[cki % 2][:, ts * P:(ts + 1) * P],
                                           vall[:, tok * 9 + 8:tok * 9 + 9],
                                           start=True, stop=True)
                            if ts % 16 == 15:
                                mmb.then_inc(TEN.h, 1)
                                TEN.n += 1
            p5_ten_end = TEN.n

            @block.scalar
            def _(s):
                s.wait_ge(BO[0].h, 8 * 16)
                s.wait_ge(BO[1].h, 8 * 16)
                if taps:
                    s.wait_ge(SQ.h, tap_xnT_sq)
                qk_load(s, 0)
                qk_load(s, 1)
                for tt in range(NT):
                    for ck in range(4):
                        cki = tt * 4 + ck
                        for g in range(8):
                            gidx = cki * 8 + g
                            s.wait_ge(TEN.h, ten_sgrp(cki, g))
                            if g == 0 and cki >= 2:
                                s.wait_ge(TEN.h, ten_osub(cki - 2, 1))
                            s.activation(ebf[cki % 2][:, g * 512:(g + 1) * 512],
                                         s_ps[gidx % 3][:], AF.Exp).then_inc(SCA.h, 1)
                            SCA.n += 1
                        if cki + 2 < 4 * NT:
                            qk_load(s, cki + 2)
            p5_sca_end = SCA.n

            @block.vector
            def _(v):
                v.wait_ge(SCA.h, p4_sca_end)
                for tt in range(NT):
                    for sub in range(8):
                        vs = tt * 8 + sub
                        v.wait_ge(TEN.h, ten_osub(vs // 2, vs % 2))
                        v.reciprocal(rdb[:, (sub % 2) * 16:(sub % 2) * 16 + 16],
                                     b_ps[vs % 2][:]).then_inc(VEC.h, 1)
                        v.wait_ge(VEC.h, VEC.n + 1)
                        rdv = rdb[:, (sub % 2) * 16:(sub % 2) * 16 + 16]\
                            .unsqueeze(2).broadcast_to([P, 16, 32])
                        v.tensor_tensor(
                            tmpa_slice(sub).rearrange("p (t h) -> p t h", h=32),
                            a_ps[vs % 3][:].rearrange("p (t h) -> p t h", h=32),
                            rdv, ALU.mult).then_inc(VEC.h, 1)
                        VEC.n += 2
                        assert VEC.n == vmark_mult[vs]
                        if sub % 2 == 1:
                            qq = sub // 2
                            lo, hi = qq * 1024, (qq + 1) * 1024
                            if qq == 0:
                                v.wait_ge(XS[tt % 4].h, xs_mark(tt))
                            v.wait_ge(VEC.h, VEC.n)
                            v.tensor_tensor(xasm[tt % 4][:, lo:hi],
                                            xasm[tt % 4][:, lo:hi],
                                            tmpa_slice(2 * qq, 1024), ALU.add)\
                                .then_inc(VEC.h, 1)
                            VEC.n += 1
                            v.wait_ge(VEC.h, VEC.n)
                            v.tensor_tensor(xasm[tt % 4][:, lo:hi],
                                            xasm[tt % 4][:, lo:hi],
                                            ffno[:, tt * 4096 + lo: tt * 4096 + hi],
                                            ALU.add).then_inc(VEC.h, 1)
                            VEC.n += 1
                            assert VEC.n == vmark_q[(tt, qq)]

            @block.sync
            def _(sy):
                if taps:
                    sy.wait_ge(SCA.h, p5_sca_end)
                    sy.dma_start(out=tap_d["vall"], in_=vall[:]).then_inc(SQ.h, 16)
                    sy.dma_start(out=tap_d["ffn"], in_=ffno[:]).then_inc(SQ.h, 16)
                    sy.dma_start(out=tap_d["hbc"], in_=hbc[:]).then_inc(SQ.h, 16)
                    SQ.n += 3 * 16
                sy.wait_ge(SQ.h, SQ.n)
                for sem in (GQ, BS, WQ[0], WQ[1], BO[0], BO[1],
                            XD[0], XD[1], XD[2],
                            QK[0], QK[1], XS[0], XS[1], XS[2],
                            OD[0], OD[1], OD[2], OG[0], OG[1], OG[2]):
                    sy.wait_ge(sem.h, sem.n)

    return nc


def _host_layout(inputs):
    Wq = np.asarray(inputs["L0_attn_Wq"])
    Wk = np.asarray(inputs["L0_attn_Wk"])
    Wv = np.asarray(inputs["L0_attn_Wv"])
    Wf = np.asarray(inputs["L0_ffn_weight"])
    wall = np.empty((NFT, 4096, P), np.float32)
    for j in range(8):
        for r in range(4):
            wall[j * 4 + r] = Wq[:, :, 4 * j + r]
    for j in range(8):
        wall[32 + j] = Wk[:, :, j]
        wall[40 + j] = Wv[:, :, j]
    wt = Wf.reshape(P, 32, 4096)
    for h in range(32):
        wall[48 + h] = np.ascontiguousarray(wt[:, h, :].T)
    base = {
        "wall": wall,
        "bq": np.ascontiguousarray(inputs["L0_attn_bq"], np.float32),
        "bk": np.ascontiguousarray(inputs["L0_attn_bk"], np.float32),
        "bv": np.ascontiguousarray(inputs["L0_attn_bv"], np.float32),
        "fb": np.ascontiguousarray(np.asarray(inputs["L0_ffn_bias"], np.float32).reshape(P, 32)),
        "ah": np.ascontiguousarray(np.asarray(inputs["L0_ffn_lora_A"], np.float32)
                                   .transpose(2, 0, 1).reshape(CH, P, 3, 16)),
        "bh": np.ascontiguousarray(np.asarray(inputs["L0_ffn_lora_B"], np.float32)
                                   .reshape(3, P, 32, 16).transpose(0, 3, 2, 1)
                                   .reshape(3, 16, 32 * P)),
        "lsi": np.ascontiguousarray(np.asarray(inputs["L0_ffn_lora_start_indices"], np.int32)
                                    .reshape(1, 4)),
        "ident": np.eye(P, dtype=np.float32),
    }
    return base


def kernel(**inputs):
    x = np.asarray(inputs["L0_attn_x"], np.float32)
    B, Sq, E = x.shape
    xf = np.ascontiguousarray(x.reshape(B * Sq, E))
    base = _host_layout(inputs)
    nc = build(taps=False)
    in_maps = []
    for c in range(8):
        m = dict(base)
        m["x_in"] = xf[c * TPC:(c + 1) * TPC]
        m["bsel"] = np.full((1, 1), float((c * TPC) // Sq), np.float32)
        in_maps.append(m)
    res = run_bass_kernel_spmd(nc, in_maps, core_ids=list(range(8)))
    out = np.concatenate([np.asarray(r["out"]) for r in res.results], axis=0)
    return out.reshape(B, Sq, E).astype(np.float32)


# version 26
# speedup vs baseline: 1.0598x; 1.0021x over previous
from contextlib import ExitStack

import numpy as np

import concourse.bass as bass
import concourse.mybir as mybir
from concourse.bass_utils import run_bass_kernel_spmd

F32 = mybir.dt.float32
BF16 = mybir.dt.bfloat16
FP16 = mybir.dt.float16
I32 = mybir.dt.int32
AF = mybir.ActivationFunctionType
ALU = mybir.AluOpType
AX = mybir.AxisListType.X

P = 128
TPC = 1024
NT = TPC // P
CH = 32
NJ = 8
NFT = 80
QSCALE = float(np.float32(1.0) / np.float32(np.sqrt(np.float32(128.0))))
EPS = 1e-5
DEBUG_TAPS = False


class S:
    def __init__(self, ctx, nc, name):
        self.h = ctx.enter_context(nc.semaphore(name))
        self.n = 0


def build(taps=DEBUG_TAPS, phases=5):
    nc = bass.Bass("TRN2", target_bir_lowering=False, debug=False)

    x_in = nc.dram_tensor("x_in", [TPC, 4096], F32, kind="ExternalInput").ap()
    wall = nc.dram_tensor("wall", [NFT, 4096, P], F32, kind="ExternalInput").ap()
    bq_d = nc.dram_tensor("bq", [P, 32], F32, kind="ExternalInput").ap()
    bk_d = nc.dram_tensor("bk", [P, 8], F32, kind="ExternalInput").ap()
    bv_d = nc.dram_tensor("bv", [P, 8], F32, kind="ExternalInput").ap()
    fb_d = nc.dram_tensor("fb", [P, 32], F32, kind="ExternalInput").ap()
    ah_d = nc.dram_tensor("ah", [CH, P, 3, 16], F32, kind="ExternalInput").ap()
    bh_d = nc.dram_tensor("bh", [3, 16, 32 * P], F32, kind="ExternalInput").ap()
    lsi_d = nc.dram_tensor("lsi", [1, 4], I32, kind="ExternalInput").ap()
    bsel_d = nc.dram_tensor("bsel", [1, 1], F32, kind="ExternalInput").ap()
    id_d = nc.dram_tensor("ident", [P, P], F32, kind="ExternalInput").ap()
    out_d = nc.dram_tensor("out", [TPC, 4096], F32, kind="ExternalOutput").ap()
    scr_d = nc.dram_tensor("qkscr", [TPC * 16 * P], BF16, kind="Internal").ap()
    tap_d = {}
    if taps:
        tap_d["xnT"] = nc.dram_tensor("tap_xnT", [P, CH * TPC], BF16, kind="ExternalOutput").ap()
        tap_d["qsks"] = nc.dram_tensor("tap_qsks", [P, TPC * 16], BF16, kind="ExternalOutput").ap()
        tap_d["vall"] = nc.dram_tensor("tap_vall", [P, TPC * 9], BF16, kind="ExternalOutput").ap()
        tap_d["xaT"] = nc.dram_tensor("tap_xaT", [16, TPC], BF16, kind="ExternalOutput").ap()
        tap_d["ffn"] = nc.dram_tensor("tap_ffn", [P, TPC * 32], FP16, kind="ExternalOutput").ap()
        tap_d["hbc"] = nc.dram_tensor("tap_hbc", [P, 3], F32, kind="ExternalOutput").ap()
        tap_d["bselt"] = nc.dram_tensor("tap_bselt", [16, 4096], BF16, kind="ExternalOutput").ap()
        tap_d["lnst"] = nc.dram_tensor("tap_lnst", [P, 80], F32, kind="ExternalOutput").ap()

    with ExitStack() as ctx:
        def sbl(n_, s_, d_):
            return ctx.enter_context(nc.sbuf_tensor(n_, s_, d_))

        def sbr(n_, s_, d_):
            return ctx.enter_context(nc.sbuf_tensor(n_, s_, d_, side="right"))

        id_sb = sbl("id_sb", [P, P], BF16)
        bqss = sbl("bqss", [P, NJ], F32)
        bk_sb = sbl("bk_sb", [P, 8], F32)
        bv_sb = sbl("bv_sb", [P, 8], F32)
        fb_sb = sbl("fb_sb", [P, 32], F32)
        bq_sb = sbl("bq_sb", [P, 32], F32)
        hbc = sbl("hbc", [P, 3], F32)
        ones_r = sbl("ones_r", [1, P], BF16)
        h3b = sbl("h3b", [1, 3], BF16)
        lsi_sb = sbl("lsi_sb", [1, 4], I32)
        lsif = sbl("lsif", [1, 4], F32)
        cle = sbl("cle", [1, 4], F32)
        bsel_sb = sbl("bsel_sb", [1, 1], F32)
        tmp1 = sbl("tmp1", [1, 4], F32)
        epsc = sbl("epsc", [P, 1], F32)
        ps_h_guard = nc.psum_tensor("ps_h", [P, 3], F32)
        ps_h = ps_h_guard.__enter__()
        xasm_e = sbr("xasm_e", [P, P * 32], F32)
        xaT_guard = nc.sbuf_tensor("xaT", [16, TPC], BF16)
        xaT = xaT_guard.__enter__()
        xnT_guard = nc.sbuf_tensor("xnT", [P, CH * TPC], BF16)
        xnT = xnT_guard.__enter__()
        wsl0_guard = nc.sbuf_tensor("wsl0", [P, CH * P], BF16)
        wsl1_guard = nc.sbuf_tensor("wsl1", [P, CH * P], BF16)
        wsl = [wsl0_guard.__enter__(), wsl1_guard.__enter__()]

        GQ = S(ctx, nc, "GQ")
        BS = S(ctx, nc, "BS")
        WQ = [S(ctx, nc, "WQ0"), S(ctx, nc, "WQ1")]
        BO = [S(ctx, nc, "BO0"), S(ctx, nc, "BO1")]
        XD = [S(ctx, nc, f"XD{i}") for i in range(3)]
        QK = [S(ctx, nc, "QK0"), S(ctx, nc, "QK1")]
        OD = [S(ctx, nc, f"OD{i}") for i in range(4)]
        OG = [S(ctx, nc, f"OG{i}") for i in range(4)]
        XS = [S(ctx, nc, f"XS{i}") for i in range(4)]
        SQ = S(ctx, nc, "SQ")
        TEN = S(ctx, nc, "TEN")
        SCA = S(ctx, nc, "SCA")
        VEC = S(ctx, nc, "VEC")
        PL = S(ctx, nc, "PL")

        block = ctx.enter_context(nc.Block())

        p1ctx = ExitStack()
        xt = [p1ctx.enter_context(nc.sbuf_tensor(f"xt{i}", [P, 4096], F32)) for i in range(3)]
        sqd = [p1ctx.enter_context(nc.sbuf_tensor(f"sqd{i}", [P, 4096], BF16))
               for i in range(2)]
        xnt_ = [p1ctx.enter_context(nc.sbuf_tensor(f"xn{i}", [P, 4096], BF16)) for i in range(2)]
        lnst = p1ctx.enter_context(nc.sbuf_tensor("lnst", [P, 80], F32))
        tr_ps = [p1ctx.enter_context(nc.psum_tensor(f"tr_ps{i}", [P, 512], BF16)) for i in range(4)]

        @block.gpsimd
        def _(g):
            g.dma_start(out=id_sb[:], in_=id_d).then_inc(GQ.h, 16)
            g.dma_start(out=bq_sb[:], in_=bq_d).then_inc(GQ.h, 16)
            g.dma_start(out=bk_sb[:], in_=bk_d).then_inc(GQ.h, 16)
            g.dma_start(out=bv_sb[:], in_=bv_d).then_inc(GQ.h, 16)
            g.dma_start(out=fb_sb[:], in_=fb_d).then_inc(GQ.h, 16)
            g.dma_start(out=lsi_sb[:], in_=lsi_d).then_inc(GQ.h, 16)
            g.dma_start(out=bsel_sb[:], in_=bsel_d).then_inc(GQ.h, 16)
            GQ.n += 7 * 16
            for ft in range(2):
                g.dma_start(out=wsl[ft][:].rearrange("p (c f) -> p c f", f=P),
                            in_=wall[ft].rearrange("(c p) f -> p c f", p=P))\
                    .then_inc(WQ[ft].h, 16)
                WQ[ft].n += 16
        p0_gq = GQ.n

        @block.vector
        def _(v):
            v.memset(epsc[:], EPS).then_inc(VEC.h, 1); VEC.n += 1
            v.memset(lnst[:], 0.0).then_inc(VEC.h, 1); VEC.n += 1
        assert VEC.n == 2, VEC.n

        p1_norm_mark = [None] * NT
        p1_trev_mark = [None] * NT
        p1_tr_mark = [[None] * 8 for _ in range(NT)]
        p1_var_mark = [None] * NT
        p1_nmr_mark = [None] * NT

        if True:
            def STT(tt, i):
                return lnst[:, tt * 10 + i: tt * 10 + i + 1]

            p1_sq_mark = [None] * NT
            p1_ln_mark = [None] * NT
            p1_exp_mark = [None] * NT
            p1_hn_mark = [None] * NT
            p1_evac_mark = [[None] * 8 for _ in range(NT)]
            vec, ten = VEC.n, TEN.n
            for tt in range(NT):
                vec += 3
                vec += 2
                p1_var_mark[tt] = vec
                vec += 1
                p1_hn_mark[tt] = vec
                vec += 1
                p1_norm_mark[tt] = vec
                for b4 in range(8):
                    ten += 1
                    p1_tr_mark[tt][b4] = ten
            sca = SCA.n
            for tt in range(NT):
                sca += 1
                p1_sq_mark[tt] = sca
                if tt >= 1:
                    for b4 in range(8):
                        sca += 1
                        p1_evac_mark[tt - 1][b4] = sca
                sca += 1
                p1_ln_mark[tt] = sca
                sca += 1
                p1_exp_mark[tt] = sca
            for b4 in range(8):
                sca += 1
                p1_evac_mark[NT - 1][b4] = sca
            for tt in range(NT):
                p1_trev_mark[tt] = p1_evac_mark[tt][7]

            def xd_mark(tt):
                return 16 * (tt // 3 + 1) + (16 if tt % 3 == 0 else 0)

            @block.sync
            def _(sy):
                for tt in range(NT):
                    if tt >= 3:
                        sy.wait_ge(VEC.h, p1_norm_mark[tt - 3])
                        sy.wait_ge(SCA.h, p1_sq_mark[tt - 3])
                    if tt == 0:
                        sy.dma_start(out=xt[0][:, 0:2048],
                                     in_=x_in[0:P, 0:2048]).then_inc(XD[0].h, 16)
                        XD[0].n += 16
                    else:
                        sy.dma_start(out=xt[tt % 3][:],
                                     in_=x_in[tt * P:(tt + 1) * P, :]).then_inc(XD[tt % 3].h, 16)
                        XD[tt % 3].n += 16

            @block.vector
            def _(v):
                v.wait_ge(VEC.h, VEC.n)
                for tt in range(NT):
                    v.wait_ge(XD[tt % 3].h, xd_mark(tt))
                    v.reduce_sum(STT(tt, 0), xt[tt % 3][:], axis=AX).then_inc(VEC.h, 1)
                    VEC.n += 1
                    v.wait_ge(VEC.h, VEC.n)
                    v.tensor_scalar_mul(STT(tt, 1), STT(tt, 0), -1.0 / 4096.0).then_inc(VEC.h, 1)
                    VEC.n += 1
                    v.wait_ge(VEC.h, VEC.n)
                    v.tensor_tensor(STT(tt, 2), STT(tt, 1), STT(tt, 1), ALU.mult).then_inc(VEC.h, 1)
                    VEC.n += 1
                    v.wait_ge(SCA.h, p1_sq_mark[tt])
                    v.tensor_scalar_mul(STT(tt, 3), STT(tt, 7), 1.0 / 4096.0).then_inc(VEC.h, 1)
                    VEC.n += 1
                    v.wait_ge(VEC.h, VEC.n)
                    v.tensor_tensor(STT(tt, 3), STT(tt, 3), STT(tt, 2), ALU.subtract).then_inc(VEC.h, 1)
                    VEC.n += 1
                    assert VEC.n == p1_var_mark[tt]
                    v.wait_ge(SCA.h, p1_ln_mark[tt])
                    v.tensor_scalar_mul(STT(tt, 8), STT(tt, 4), -0.5).then_inc(VEC.h, 1)
                    VEC.n += 1
                    assert VEC.n == p1_hn_mark[tt]
                    v.wait_ge(SCA.h, p1_exp_mark[tt])
                    if tt >= 2:
                        v.wait_ge(TEN.h, p1_tr_mark[tt - 2][7])
                    v.tensor_scalar(xnt_[tt % 2][:], xt[tt % 3][:], STT(tt, 1), STT(tt, 5),
                                    ALU.add, ALU.mult).then_inc(VEC.h, 1)
                    VEC.n += 1
                    assert VEC.n == p1_norm_mark[tt]

            def p1_emit_evacs(s, te):
                pair = 2 * (te % 2)
                for b4 in range(8):
                    s.wait_ge(TEN.h, p1_tr_mark[te][b4])
                    dst = xnT[:].rearrange("p (c t) -> p c t", t=TPC)[
                        :, b4 * 4:(b4 + 1) * 4, te * P:(te + 1) * P]
                    s.activation(dst, tr_ps[pair + b4 % 2][:].rearrange("p (c t) -> p c t", t=P),
                                 AF.Copy).then_inc(SCA.h, 1)
                    SCA.n += 1
                    assert SCA.n == p1_evac_mark[te][b4]

            @block.scalar
            def _(s):
                s.dma_start(out=xt[0][:, 2048:4096],
                            in_=x_in[0:P, 2048:4096]).then_inc(XD[0].h, 16)
                XD[0].n += 16
                s.wait_ge(VEC.h, 2)
                for tt in range(NT):
                    s.wait_ge(XD[tt % 3].h, xd_mark(tt))
                    s.activation(sqd[tt % 2][:], xt[tt % 3][:], AF.Square,
                                 accum_out=STT(tt, 7)).then_inc(SCA.h, 1)
                    SCA.n += 1
                    assert SCA.n == p1_sq_mark[tt]
                    if tt >= 1:
                        p1_emit_evacs(s, tt - 1)
                    s.wait_ge(VEC.h, p1_var_mark[tt])
                    s.activation(STT(tt, 4), STT(tt, 3), AF.Ln, bias=epsc[:, 0:1]).then_inc(SCA.h, 1)
                    SCA.n += 1
                    assert SCA.n == p1_ln_mark[tt]
                    s.wait_ge(VEC.h, p1_hn_mark[tt])
                    s.activation(STT(tt, 5), STT(tt, 8), AF.Exp).then_inc(SCA.h, 1)
                    SCA.n += 1
                    assert SCA.n == p1_exp_mark[tt]
                p1_emit_evacs(s, NT - 1)

            @block.tensor
            def _(t):
                t.wait_ge(GQ.h, p0_gq)
                for tt in range(NT):
                    pair = 2 * (tt % 2)
                    t.wait_ge(VEC.h, p1_norm_mark[tt])
                    if tt >= 2:
                        t.wait_ge(SCA.h, p1_evac_mark[tt - 2][7])
                    for b4 in range(8):
                        if b4 >= 2:
                            t.wait_ge(SCA.h, p1_evac_mark[tt][b4 - 2])
                        for c in range(4):
                            cc = b4 * 4 + c
                            tr = t.transpose(tr_ps[pair + b4 % 2][:, c * P:(c + 1) * P],
                                             xnt_[tt % 2][:, cc * P:(cc + 1) * P], id_sb[:])
                        tr.then_inc(TEN.h, 1)
                        TEN.n += 1
                        assert TEN.n == p1_tr_mark[tt][b4]
            if taps:
                @block.sync
                def _(sy):
                    sy.wait_ge(SCA.h, p1_trev_mark[NT - 1])
                    sy.dma_start(out=tap_d["lnst"], in_=lnst[:]).then_inc(SQ.h, 16)
                    SQ.n += 16
            tap_lnst_sq = SQ.n
            p1_ten_end, p1_sca_end = TEN.n, SCA.n
        p1ctx.close()
        vall = sbr("vall", [P, TPC * 9], BF16)

        aselt_guard = nc.sbuf_tensor("aselt", [P, CH * 16], BF16)
        aselt = aselt_guard.__enter__()
        bselt_guard = nc.sbuf_tensor("bselt", [16, 32 * P], BF16)
        bselt = bselt_guard.__enter__()
        p0bctx = ExitStack()
        a_all = p0bctx.enter_context(nc.sbuf_tensor("a_all", [P, CH * 48], BF16, side="right"))
        bsl = [p0bctx.enter_context(nc.sbuf_tensor(f"bsl{i}", [16, 32 * P], BF16, side="right"))
               for i in range(3)]
        p0b_vec0 = 2 + NT * 7
        p0b_mask_mark = p0b_vec0 + 12
        p0_vec = p0b_mask_mark + 6
        p0_b_vec_mark = [p0b_mask_mark + 3 + (a + 1) for a in range(3)]
        p0b_gq = p0_gq + 16

        @block.vector
        def _(v):
            assert VEC.n == p0b_vec0, VEC.n
            v.wait_ge(GQ.h, p0_gq)
            v.tensor_copy(lsif[:], lsi_sb[:]).then_inc(VEC.h, 1); VEC.n += 1
            v.wait_ge(VEC.h, VEC.n)
            v.tensor_scalar(cle[:], lsif[:], bsel_sb[0:1, 0:1], None, ALU.is_le)\
                .then_inc(VEC.h, 1); VEC.n += 1
            v.wait_ge(VEC.h, VEC.n)
            v.tensor_tensor(h3b[0:1, 2:3], cle[0:1, 2:3], cle[0:1, 3:4], ALU.mult)\
                .then_inc(VEC.h, 1); VEC.n += 1
            v.wait_ge(VEC.h, VEC.n)
            v.tensor_tensor(h3b[0:1, 1:2], cle[0:1, 2:3], h3b[0:1, 2:3], ALU.subtract)\
                .then_inc(VEC.h, 1); VEC.n += 1
            v.tensor_tensor(tmp1[0:1, 0:1], cle[0:1, 1:2], cle[0:1, 2:3], ALU.mult)\
                .then_inc(VEC.h, 1); VEC.n += 1
            v.wait_ge(VEC.h, VEC.n)
            v.tensor_tensor(h3b[0:1, 0:1], cle[0:1, 1:2], tmp1[0:1, 0:1], ALU.subtract)\
                .then_inc(VEC.h, 1); VEC.n += 1
            v.memset(ones_r[:], 1.0).then_inc(VEC.h, 1); VEC.n += 1
            bqv = bq_sb[:].rearrange("p (j r) -> p j r", r=4)
            v.tensor_tensor(bqss[:], bqv[:, :, 0], bqv[:, :, 1], ALU.add)\
                .then_inc(VEC.h, 1); VEC.n += 1
            v.wait_ge(VEC.h, VEC.n)
            v.tensor_tensor(bqss[:], bqss[:], bqv[:, :, 2], ALU.add)\
                .then_inc(VEC.h, 1); VEC.n += 1
            v.wait_ge(VEC.h, VEC.n)
            v.tensor_tensor(bqss[:], bqss[:], bqv[:, :, 3], ALU.add)\
                .then_inc(VEC.h, 1); VEC.n += 1
            v.wait_ge(VEC.h, VEC.n)
            v.tensor_scalar_mul(bqss[:], bqss[:], QSCALE).then_inc(VEC.h, 1); VEC.n += 1
            v.memset(vall[:], 1.0).then_inc(VEC.h, 1); VEC.n += 1
            assert VEC.n == p0b_mask_mark, VEC.n

        p2_ten0, p2_sca0 = TEN.n, SCA.n
        NQKV = 48

        def wq_mark(ft):
            return 16 * (ft // 2 + 1)

        if phases < 2:
            @block.sync
            def _(sy):
                if taps:
                    sy.wait_ge(SCA.h, p1_trev_mark[NT - 1])
                    sy.dma_start(out=tap_d["xnT"], in_=xnT[:]).then_inc(SQ.h, 16)
                    SQ.n += 16
                sy.wait_ge(SQ.h, SQ.n)
                for sem in (GQ, BS, XD[0], XD[1]):
                    sy.wait_ge(sem.h, sem.n)
            p0bctx.close()
            bselt_guard.__exit__(None, None, None)
            aselt_guard.__exit__(None, None, None)
            wsl1_guard.__exit__(None, None, None)
            wsl0_guard.__exit__(None, None, None)
            xnT_guard.__exit__(None, None, None)
            xaT_guard.__exit__(None, None, None)
            return nc
        def p2tm(ft):
            return p2_ten0 + ft + 1 + (1 if ft >= 3 else 0)

        @block.gpsimd
        def _(g):
            for ft in range(2, NQKV):
                if ft >= 2:
                    g.wait_ge(TEN.h, p2tm(ft - 2))
                g.dma_start(out=wsl[ft % 2][:].rearrange("p (c f) -> p c f", f=P),
                            in_=wall[ft].rearrange("(c p) f -> p c f", p=P))\
                    .then_inc(WQ[ft % 2].h, 16)
                WQ[ft % 2].n += 16
                if ft == 2:
                    g.wait_ge(SCA.h, p1_trev_mark[NT - 1])
                    if taps:
                        g.wait_ge(SQ.h, tap_lnst_sq)
                    g.dma_start(out=a_all[:].rearrange("p (c a r) -> p c a r", c=CH, a=3),
                                in_=ah_d.rearrange("c p a r -> p c a r")).then_inc(GQ.h, 16)
                    GQ.n += 16
                    for a in range(3):
                        g.dma_start(out=bsl[a][:], in_=bh_d[a]).then_inc(BS.h, 16)
                    BS.n += 3 * 16

        with nc.sbuf_tensor("qsks", [P, TPC * 16], BF16, side="right") as qsks:
            with nc.psum_tensor("qk_ps0", [P, 512], F32) as qk_ps0, \
                 nc.psum_tensor("qk_ps1", [P, 512], F32) as qk_ps1, \
                 nc.psum_tensor("qk_ps2", [P, 512], F32) as qk_ps2, \
                 nc.psum_tensor("qk_ps3", [P, 512], F32) as qk_ps3, \
                 nc.psum_tensor("xa_ps0", [16, 512], F32) as xa_ps0, \
                 nc.psum_tensor("xa_ps1", [16, 512], F32) as xa_ps1:
                qk_ps = [qk_ps0, qk_ps1, qk_ps2, qk_ps3]
                xa_ps = [xa_ps0, xa_ps1]

                @block.tensor
                def _(t):
                    t.wait_ge(SCA.h, p1_trev_mark[NT - 1])
                    for ft in range(NQKV):
                        t.wait_ge(WQ[ft % 2].h, wq_mark(ft))
                        if ft < 32:
                            j, r = divmod(ft, 4)
                            first, last = (r == 0), (r == 3)
                            bank = 2 * (j % 2)
                            if first and j >= 2:
                                t.wait_ge(SCA.h, p2_sca0 + 1 + 2 * (j - 1))
                        else:
                            first = last = True
                            bank = 2 * (ft % 2)
                            if ft == 32:
                                t.wait_ge(SCA.h, p2_sca0 + 15)
                            elif ft == 33:
                                t.wait_ge(SCA.h, p2_sca0 + 17)
                            else:
                                t.wait_ge(SCA.h, p2_sca0 + 17 + 2 * (ft - 33))
                        for c in range(CH):
                            for half in range(2):
                                mm = t.matmul(
                                    qk_ps[bank + half][:],
                                    wsl[ft % 2][:, c * P:(c + 1) * P],
                                    xnT[:, c * TPC + half * 512: c * TPC + (half + 1) * 512],
                                    start=(first and c == 0), stop=(last and c == CH - 1))
                        mm.then_inc(TEN.h, 1)
                        TEN.n += 1
                        if ft == 2:
                            t.wait_ge(VEC.h, p0b_mask_mark)
                            t.matmul(ps_h[:], ones_r[:], h3b[:], start=True, stop=True)\
                                .then_inc(TEN.h, 1)
                            TEN.n += 1
                    t.wait_ge(VEC.h, p0_vec)
                    for c in range(CH):
                        for half in range(2):
                            mm = t.matmul(xa_ps[half][:],
                                          aselt[:, c * 16:(c + 1) * 16],
                                          xnT[:, c * TPC + half * 512: c * TPC + (half + 1) * 512],
                                          start=(c == 0), stop=(c == CH - 1))
                    mm.then_inc(TEN.h, 1)
                    TEN.n += 1
                p2_ten_end = TEN.n
                p0_ten = p2_ten0 + 4
                assert p2_ten_end == p2_ten0 + NQKV + 2

                @block.scalar
                def _(s):
                    qv = qsks[:].rearrange("p (t cc) -> p t cc", cc=16)
                    vv = vall[:].rearrange("p (t cc) -> p t cc", cc=9)
                    s.wait_ge(TEN.h, p0_ten)
                    s.activation(hbc[:], ps_h[:], AF.Copy).then_inc(SCA.h, 1)
                    SCA.n += 1
                    for j in range(NJ):
                        s.wait_ge(TEN.h, p2tm(4 * j + 3))
                        if j == 0:
                            s.wait_ge(VEC.h, p0b_mask_mark)
                        bank = 2 * (j % 2)
                        for half in range(2):
                            dst = qv[:, half * 512:(half + 1) * 512, j]
                            s.activation(dst, qk_ps[bank + half][:], AF.Identity,
                                         bias=bqss[:, j:j + 1], scale=QSCALE)\
                                .then_inc(SCA.h, 1)
                            SCA.n += 1
                    for ft in range(32, NQKV):
                        s.wait_ge(TEN.h, p2tm(ft))
                        kv = ft - 32
                        j = kv % 8
                        bank = 2 * (ft % 2)
                        for half in range(2):
                            if kv < 8:
                                dst = qv[:, half * 512:(half + 1) * 512, 8 + j]
                                bias = bk_sb[:, j:j + 1]
                            else:
                                dst = vv[:, half * 512:(half + 1) * 512, j]
                                bias = bv_sb[:, j:j + 1]
                            s.activation(dst, qk_ps[bank + half][:], AF.Identity,
                                         bias=bias).then_inc(SCA.h, 1)
                            SCA.n += 1
                    s.wait_ge(TEN.h, p2_ten_end)
                    for half in range(2):
                        s.activation(xaT[:, half * 512:(half + 1) * 512],
                                     xa_ps[half][:], AF.Copy).then_inc(SCA.h, 1)
                        SCA.n += 1
                p2_sca_end = SCA.n
                p0_sca = p2_sca0 + 1
                assert p2_sca_end == p2_sca0 + 51

                @block.vector
                def _(v):
                    v.wait_ge(SCA.h, p0_sca)
                    v.wait_ge(GQ.h, p0b_gq)
                    av = a_all[:].rearrange("p (c a r) -> p c a r", c=CH, a=3)
                    sv = aselt[:].rearrange("p (c r) -> p c r", c=CH)
                    v.tensor_scalar(sv, av[:, :, 0, :], hbc[:, 0:1], None, ALU.mult)\
                        .then_inc(VEC.h, 1); VEC.n += 1
                    for a in (1, 2):
                        v.wait_ge(VEC.h, VEC.n)
                        v.scalar_tensor_tensor(sv, av[:, :, a, :], hbc[:, a:a + 1], sv,
                                               ALU.mult, ALU.add).then_inc(VEC.h, 1); VEC.n += 1
                    for a in range(3):
                        v.wait_ge(BS.h, 48)
                        if a == 0:
                            v.tensor_scalar(bselt[:], bsl[a][:], hbc[0:16, 0:1], None, ALU.mult)\
                                .then_inc(VEC.h, 1); VEC.n += 1
                        else:
                            v.wait_ge(VEC.h, VEC.n)
                            v.scalar_tensor_tensor(bselt[:], bsl[a][:], hbc[0:16, a:a + 1],
                                                   bselt[:], ALU.mult, ALU.add)\
                                .then_inc(VEC.h, 1); VEC.n += 1
                        assert p0_b_vec_mark[a] == VEC.n, (a, p0_b_vec_mark[a], VEC.n)
                assert VEC.n == p0_vec, (VEC.n, p0_vec)

            p3_ten0, p3_sca0 = TEN.n, SCA.n
            NB = 32
            with nc.sbuf_tensor("tsb0", [P, 1024], BF16) as tsb0, \
                 nc.sbuf_tensor("tsb1", [P, 1024], BF16) as tsb1, \
                 nc.psum_tensor("p3_ps0", [P, 512], BF16) as p3_ps0, \
                 nc.psum_tensor("p3_ps1", [P, 512], BF16) as p3_ps1:
                tsb = [tsb0, tsb1]
                p3_ps = [p3_ps0, p3_ps1]

                @block.tensor
                def _(t):
                    t.wait_ge(SCA.h, p2_sca_end)
                    for b in range(NB):
                        if b >= 2:
                            t.wait_ge(SCA.h, p3_sca0 + b - 1)
                        for g4 in range(4):
                            col = (b * 4 + g4) * P
                            tr = t.transpose(p3_ps[b % 2][:, g4 * P:(g4 + 1) * P],
                                             qsks[:, col:col + P], id_sb[:])
                        tr.then_inc(TEN.h, 1)
                        TEN.n += 1
                p3_ten_end = TEN.n

                @block.scalar
                def _(s):
                    if taps:
                        s.wait_ge(SQ.h, tap_lnst_sq)
                    for b in range(NB):
                        s.wait_ge(TEN.h, p3_ten0 + b + 1)
                        if b >= 4:
                            bbp = b // 2 - 2
                            s.wait_ge(BO[bbp % 2].h, 16 * (bbp // 2 + 1))
                        s.activation(tsb[(b // 2) % 2][:, (b % 2) * 512:(b % 2 + 1) * 512],
                                     p3_ps[b % 2][:], AF.Copy).then_inc(SCA.h, 1)
                        SCA.n += 1
                p3_sca_end = SCA.n

                @block.sync
                def _(sy):
                    scr4 = scr_d.rearrange("(bb gg p d) -> bb p gg d", bb=16, gg=8, p=P)
                    for bb in range(16):
                        sy.wait_ge(SCA.h, p3_sca0 + 2 * bb + 2)
                        sy.dma_start(out=scr4[bb],
                                    in_=tsb[bb % 2][:].rearrange("p (gg d) -> p gg d", gg=8))\
                            .then_inc(BO[bb % 2].h, 16)
                        BO[bb % 2].n += 16

                if taps:
                    @block.sync
                    def _(sy):
                        sy.wait_ge(TEN.h, p3_ten_end)
                        sy.dma_start(out=tap_d["qsks"], in_=qsks[:]).then_inc(SQ.h, 16)
                        SQ.n += 16
                tap_qsks_sq = SQ.n

        p0bctx.close()
        ps_h_guard.__exit__(None, None, None)

        ffno = ctx.enter_context(nc.sbuf_tensor("ffno", [P, TPC * 32], FP16, side="right"))
        p4_ten0, p4_sca0 = TEN.n, SCA.n
        with nc.psum_tensor("f_ps0", [P, 512], F32) as f_ps0, \
             nc.psum_tensor("f_ps1", [P, 512], F32) as f_ps1, \
             nc.psum_tensor("f_ps2", [P, 512], F32) as f_ps2, \
             nc.psum_tensor("f_ps3", [P, 512], F32) as f_ps3:
            f_ps = [f_ps0, f_ps1, f_ps2, f_ps3]

            @block.gpsimd
            def _(g):
                for hh in range(32):
                    ft = NQKV + hh
                    if hh == 0:
                        g.wait_ge(TEN.h, p2tm(46))
                    elif hh == 1:
                        g.wait_ge(TEN.h, p2tm(47))
                    else:
                        g.wait_ge(TEN.h, p4_ten0 + hh - 1)
                    g.dma_start(out=wsl[ft % 2][:].rearrange("p (c f) -> p c f", f=P),
                                in_=wall[ft].rearrange("(c p) f -> p c f", p=P))\
                        .then_inc(WQ[ft % 2].h, 16)
                    WQ[ft % 2].n += 16

            @block.tensor
            def _(t):
                t.wait_ge(SCA.h, p3_sca_end)
                t.wait_ge(VEC.h, p0_vec)
                for hh in range(32):
                    ft = NQKV + hh
                    t.wait_ge(WQ[ft % 2].h, wq_mark(ft))
                    bank = 2 * (hh % 2)
                    if hh >= 2:
                        t.wait_ge(SCA.h, p4_sca0 + 2 * (hh - 1))
                    for c in range(CH):
                        for half in range(2):
                            t.matmul(f_ps[bank + half][:],
                                     wsl[ft % 2][:, c * P:(c + 1) * P],
                                     xnT[:, c * TPC + half * 512: c * TPC + (half + 1) * 512],
                                     start=(c == 0), stop=False)
                    for half in range(2):
                        mm = t.matmul(f_ps[bank + half][:],
                                      bselt[:, hh * P:(hh + 1) * P],
                                      xaT[:, half * 512:(half + 1) * 512],
                                      start=False, stop=True)
                    mm.then_inc(TEN.h, 1)
                    TEN.n += 1
            p4_ten_end = TEN.n

            if taps:
                @block.sync
                def _(sy):
                    sy.wait_ge(TEN.h, p4_ten_end)
                    sy.dma_start(out=tap_d["xnT"], in_=xnT[:]).then_inc(SQ.h, 16)
                    sy.dma_start(out=tap_d["bselt"], in_=bselt[:]).then_inc(SQ.h, 16)
                    sy.dma_start(out=tap_d["xaT"], in_=xaT[:]).then_inc(SQ.h, 16)
                    SQ.n += 3 * 16
            tap_xnT_sq = SQ.n

            @block.scalar
            def _(s):
                if taps:
                    s.wait_ge(SQ.h, tap_qsks_sq)
                fv = ffno[:].rearrange("p (t h) -> p t h", h=32)
                for hh in range(32):
                    s.wait_ge(TEN.h, p4_ten0 + hh + 1)
                    bank = 2 * (hh % 2)
                    for half in range(2):
                        s.activation(fv[:, half * 512:(half + 1) * 512, hh],
                                     f_ps[bank + half][:], AF.Identity,
                                     bias=fb_sb[:, hh:hh + 1]).then_inc(SCA.h, 1)
                        SCA.n += 1
            p4_sca_end = SCA.n

        bselt_guard.__exit__(None, None, None)
        aselt_guard.__exit__(None, None, None)
        wsl1_guard.__exit__(None, None, None)
        wsl0_guard.__exit__(None, None, None)
        xnT_guard.__exit__(None, None, None)
        xaT_guard.__exit__(None, None, None)

        p5_ten0, p5_sca0, p5_vec0 = TEN.n, SCA.n, VEC.n
        ten_sgrp = lambda ck, g: p5_ten0 + ck * 10 + g + 1
        ten_schunk = lambda ck: p5_ten0 + ck * 10 + 8
        ten_osub = lambda ck, sidx: p5_ten0 + ck * 10 + 8 + sidx + 1
        sca_exp = lambda ck, g: p5_sca0 + ck * 8 + g + 1

        vmark_mult = {}
        vmark_q = {}
        vmark_tile = {}
        nv = p5_vec0
        for tt in range(NT):
            for sub in range(8):
                nv += 2
                vmark_mult[tt * 8 + sub] = nv
                if sub % 2 == 1:
                    nv += 2
                    vmark_q[(tt, sub // 2)] = nv
            vmark_tile[tt] = nv
        p5_vec_end = nv

        def qk_mark(cki):
            return 32 * (cki // 2 + 1)

        def xs_mark(tt):
            return 32 * (tt // 4 + 1)

        def od_mark(tt):
            return 32 * (tt // 4 + 1)

        with ExitStack() as p5ctx:
            def p5sb(n_, s_):
                return p5ctx.enter_context(nc.sbuf_tensor(n_, s_, BF16, side="right"))
            qst = [p5sb(f"qst{i}", [NJ, 32 * P]) for i in range(2)]
            kst = [p5sb(f"kst{i}", [NJ, 32 * P]) for i in range(2)]
            ebf = [p5sb(f"ebf{i}", [P, 32 * P]) for i in range(2)]
            xasm = [xasm_e] + [
                p5ctx.enter_context(nc.sbuf_tensor(f"xasm{i}", [P, P * 32], F32, side="right"))
                for i in range(1, 4)]
            tmpa_full = p5ctx.enter_context(
                nc.sbuf_tensor("tmpa_full", [P, 2048], F32, side="right"))

            def tmpa_slice(sub, n=512):
                base = ((sub // 2) % 2) * 1024 + (sub % 2) * 512
                return tmpa_full[:, base:base + n]
            rdb = p5ctx.enter_context(nc.sbuf_tensor("rdb", [P, 32], F32, side="right"))
            s_ps = [p5ctx.enter_context(nc.psum_tensor(f"s_ps{i}", [P, 512], F32)) for i in range(3)]
            a_ps = [p5ctx.enter_context(nc.psum_tensor(f"a_ps{i}", [P, 512], F32)) for i in range(3)]
            b_ps = [p5ctx.enter_context(nc.psum_tensor(f"b_ps{i}", [P, 16], F32)) for i in range(2)]
            scr5 = scr_d.rearrange("(t cc d) -> cc t d", t=TPC, cc=16)

            def qk_load(eng, cki):
                t0 = (cki // 4) * P + (cki % 4) * 32
                eng.dma_start(out=qst[cki % 2][:].rearrange("j (t d) -> j t d", t=32),
                              in_=scr5[0:8, t0:t0 + 32, :]).then_inc(QK[cki % 2].h, 16)
                eng.dma_start(out=kst[cki % 2][:].rearrange("j (t d) -> j t d", t=32),
                              in_=scr5[8:16, t0:t0 + 32, :]).then_inc(QK[cki % 2].h, 16)
                QK[cki % 2].n += 32

            @block.sync
            def _(sy):
                sy.wait_ge(BO[0].h, 8 * 16)
                sy.wait_ge(BO[1].h, 8 * 16)
                sy.wait_ge(TEN.h, p4_ten_end)
                if taps:
                    sy.wait_ge(SQ.h, tap_xnT_sq)
                qk_load(sy, 0)
                qk_load(sy, 1)
                sy.wait_ge(SCA.h, p4_sca_end)

                def emit_out(tt, qq):
                    sy.wait_ge(VEC.h, vmark_q[(tt, qq)])
                    sy.dma_start(
                        out=out_d[tt * P + qq * 32: tt * P + (qq + 1) * 32, :]
                        .rearrange("t (d h) -> d t h", h=32),
                        in_=xasm[tt % 4][:, qq * 1024:(qq + 1) * 1024]
                        .rearrange("d (t h) -> d t h", h=32)).then_inc(OD[tt % 4].h, 16)
                    OD[tt % 4].n += 16

                for tt in range(NT):
                    emit_out(tt, 0)
                    emit_out(tt, 2)

            @block.gpsimd
            def _(g):
                def emit_out1(ttE):
                    for qq in (1, 3):
                        g.wait_ge(VEC.h, vmark_q[(ttE, qq)])
                        g.dma_start(
                            out=out_d[ttE * P + qq * 32: ttE * P + (qq + 1) * 32, :]
                            .rearrange("t (d h) -> d t h", h=32),
                            in_=xasm[ttE % 4][:, qq * 1024:(qq + 1) * 1024]
                            .rearrange("d (t h) -> d t h", h=32)).then_inc(OG[ttE % 4].h, 16)
                        OG[ttE % 4].n += 16

                if taps:
                    g.wait_ge(SQ.h, tap_xnT_sq)
                for tt in range(NT):
                    if tt == 1:
                        g.wait_ge(SCA.h, p4_sca_end)
                    if tt >= 4:
                        g.wait_ge(OD[tt % 4].h, od_mark(tt - 4))
                        g.wait_ge(OG[tt % 4].h, od_mark(tt - 4))
                    for hf in range(2):
                        g.dma_start(
                            out=xasm[tt % 4][:, hf * 2048:(hf + 1) * 2048]
                            .rearrange("d (t h) -> d t h", h=32),
                            in_=x_in[tt * P + hf * 64: tt * P + (hf + 1) * 64, :]
                            .rearrange("t (d h) -> d t h", h=32)).then_inc(XS[tt % 4].h, 16)
                        XS[tt % 4].n += 16
                    if tt >= 2:
                        emit_out1(tt - 2)
                emit_out1(NT - 2)
                emit_out1(NT - 1)

            @block.tensor
            def _(t):
                t.wait_ge(SCA.h, p4_sca_end)
                for tt in range(NT):
                    for ck in range(4):
                        cki = tt * 4 + ck
                        t.wait_ge(QK[cki % 2].h, qk_mark(cki))
                        for g in range(8):
                            gidx = cki * 8 + g
                            if gidx >= 3:
                                pc, pg = divmod(gidx - 3, 8)
                                t.wait_ge(SCA.h, sca_exp(pc, pg))
                            for k4 in range(4):
                                ts = g * 4 + k4
                                mm = t.matmul(
                                    s_ps[gidx % 3][:, k4 * P:(k4 + 1) * P],
                                    kst[cki % 2][:, ts * P:(ts + 1) * P],
                                    qst[cki % 2][:, ts * P:(ts + 1) * P],
                                    start=True, stop=True)
                            mm.then_inc(TEN.h, 1)
                            TEN.n += 1
                        for ts in range(32):
                            sidx = ts // 16
                            vs = cki * 2 + sidx
                            if ts % 4 == 0:
                                t.wait_ge(SCA.h, sca_exp(cki, ts // 4))
                            if ts % 16 == 0 and vs >= 3:
                                t.wait_ge(VEC.h, vmark_mult[vs - 3])
                            if ts % 16 == 0 and vs >= 2:
                                t.wait_ge(VEC.h, vmark_mult[vs - 2] - 1)
                            tok = tt * P + ck * 32 + ts
                            vrep = vall[:, tok * 9:tok * 9 + 8].unsqueeze(2)\
                                .broadcast_to([P, 8, 4])
                            t.matmul(a_ps[vs % 3][:, (ts % 16) * 32:(ts % 16) * 32 + 32],
                                     ebf[cki % 2][:, ts * P:(ts + 1) * P], vrep,
                                     start=True, stop=True)
                            mmb = t.matmul(b_ps[vs % 2][:, ts % 16:ts % 16 + 1],
                                           ebf[cki % 2][:, ts * P:(ts + 1) * P],
                                           vall[:, tok * 9 + 8:tok * 9 + 9],
                                           start=True, stop=True)
                            if ts % 16 == 15:
                                mmb.then_inc(TEN.h, 1)
                                TEN.n += 1
            p5_ten_end = TEN.n

            @block.scalar
            def _(s):
                for tt in range(NT):
                    for ck in range(4):
                        cki = tt * 4 + ck
                        for g in range(8):
                            gidx = cki * 8 + g
                            s.wait_ge(TEN.h, ten_sgrp(cki, g))
                            if g == 0 and cki >= 2:
                                s.wait_ge(TEN.h, ten_osub(cki - 2, 1))
                            s.activation(ebf[cki % 2][:, g * 512:(g + 1) * 512],
                                         s_ps[gidx % 3][:], AF.Exp).then_inc(SCA.h, 1)
                            SCA.n += 1
                        if cki + 2 < 4 * NT:
                            qk_load(s, cki + 2)
            p5_sca_end = SCA.n

            @block.vector
            def _(v):
                v.wait_ge(SCA.h, p4_sca_end)
                for tt in range(NT):
                    for sub in range(8):
                        vs = tt * 8 + sub
                        v.wait_ge(TEN.h, ten_osub(vs // 2, vs % 2))
                        v.reciprocal(rdb[:, (sub % 2) * 16:(sub % 2) * 16 + 16],
                                     b_ps[vs % 2][:]).then_inc(VEC.h, 1)
                        v.wait_ge(VEC.h, VEC.n + 1)
                        rdv = rdb[:, (sub % 2) * 16:(sub % 2) * 16 + 16]\
                            .unsqueeze(2).broadcast_to([P, 16, 32])
                        v.tensor_tensor(
                            tmpa_slice(sub).rearrange("p (t h) -> p t h", h=32),
                            a_ps[vs % 3][:].rearrange("p (t h) -> p t h", h=32),
                            rdv, ALU.mult).then_inc(VEC.h, 1)
                        VEC.n += 2
                        assert VEC.n == vmark_mult[vs]
                        if sub % 2 == 1:
                            qq = sub // 2
                            lo, hi = qq * 1024, (qq + 1) * 1024
                            if qq == 0:
                                v.wait_ge(XS[tt % 4].h, xs_mark(tt))
                            v.wait_ge(VEC.h, VEC.n)
                            v.tensor_tensor(xasm[tt % 4][:, lo:hi],
                                            xasm[tt % 4][:, lo:hi],
                                            tmpa_slice(2 * qq, 1024), ALU.add)\
                                .then_inc(VEC.h, 1)
                            VEC.n += 1
                            v.wait_ge(VEC.h, VEC.n)
                            v.tensor_tensor(xasm[tt % 4][:, lo:hi],
                                            xasm[tt % 4][:, lo:hi],
                                            ffno[:, tt * 4096 + lo: tt * 4096 + hi],
                                            ALU.add).then_inc(VEC.h, 1)
                            VEC.n += 1
                            assert VEC.n == vmark_q[(tt, qq)]

            @block.sync
            def _(sy):
                if taps:
                    sy.wait_ge(SCA.h, p5_sca_end)
                    sy.dma_start(out=tap_d["vall"], in_=vall[:]).then_inc(SQ.h, 16)
                    sy.dma_start(out=tap_d["ffn"], in_=ffno[:]).then_inc(SQ.h, 16)
                    sy.dma_start(out=tap_d["hbc"], in_=hbc[:]).then_inc(SQ.h, 16)
                    SQ.n += 3 * 16
                sy.wait_ge(SQ.h, SQ.n)
                for sem in (GQ, BS, WQ[0], WQ[1], BO[0], BO[1],
                            XD[0], XD[1], XD[2],
                            QK[0], QK[1], XS[0], XS[1], XS[2],
                            OD[0], OD[1], OD[2], OG[0], OG[1], OG[2]):
                    sy.wait_ge(sem.h, sem.n)

    return nc


def _host_layout(inputs):
    Wq = np.asarray(inputs["L0_attn_Wq"])
    Wk = np.asarray(inputs["L0_attn_Wk"])
    Wv = np.asarray(inputs["L0_attn_Wv"])
    Wf = np.asarray(inputs["L0_ffn_weight"])
    wall = np.empty((NFT, 4096, P), np.float32)
    for j in range(8):
        for r in range(4):
            wall[j * 4 + r] = Wq[:, :, 4 * j + r]
    for j in range(8):
        wall[32 + j] = Wk[:, :, j]
        wall[40 + j] = Wv[:, :, j]
    wt = Wf.reshape(P, 32, 4096)
    for h in range(32):
        wall[48 + h] = np.ascontiguousarray(wt[:, h, :].T)
    base = {
        "wall": wall,
        "bq": np.ascontiguousarray(inputs["L0_attn_bq"], np.float32),
        "bk": np.ascontiguousarray(inputs["L0_attn_bk"], np.float32),
        "bv": np.ascontiguousarray(inputs["L0_attn_bv"], np.float32),
        "fb": np.ascontiguousarray(np.asarray(inputs["L0_ffn_bias"], np.float32).reshape(P, 32)),
        "ah": np.ascontiguousarray(np.asarray(inputs["L0_ffn_lora_A"], np.float32)
                                   .transpose(2, 0, 1).reshape(CH, P, 3, 16)),
        "bh": np.ascontiguousarray(np.asarray(inputs["L0_ffn_lora_B"], np.float32)
                                   .reshape(3, P, 32, 16).transpose(0, 3, 2, 1)
                                   .reshape(3, 16, 32 * P)),
        "lsi": np.ascontiguousarray(np.asarray(inputs["L0_ffn_lora_start_indices"], np.int32)
                                    .reshape(1, 4)),
        "ident": np.eye(P, dtype=np.float32),
    }
    return base


def kernel(**inputs):
    x = np.asarray(inputs["L0_attn_x"], np.float32)
    B, Sq, E = x.shape
    xf = np.ascontiguousarray(x.reshape(B * Sq, E))
    base = _host_layout(inputs)
    nc = build(taps=False)
    in_maps = []
    for c in range(8):
        m = dict(base)
        m["x_in"] = xf[c * TPC:(c + 1) * TPC]
        m["bsel"] = np.full((1, 1), float((c * TPC) // Sq), np.float32)
        in_maps.append(m)
    res = run_bass_kernel_spmd(nc, in_maps, core_ids=list(range(8)))
    out = np.concatenate([np.asarray(r["out"]) for r in res.results], axis=0)
    return out.reshape(B, Sq, E).astype(np.float32)
